# version 14
# baseline (speedup 1.0000x reference)
import os
import numpy as np
import ml_dtypes
from contextlib import ExitStack

import concourse.bass as bass
import concourse.mybir as mybir
from concourse.bass_utils import run_bass_kernel_spmd

F32 = mybir.dt.float32
BF16 = mybir.dt.bfloat16
AF = mybir.ActivationFunctionType
ALU = mybir.AluOpType
AX = mybir.AxisListType
BF = ml_dtypes.bfloat16

D = 1024
H = 16
DH = 64
NQ = 2048
NS = 4096
NE = 8
DF = 512
EPS = 1e-5

DEBUG = int(os.environ.get("KMK_DEBUG", "0"))
LASTPH = os.environ.get("KMK_LAST", "p5")


class SelfSem:
    def __init__(self, eng, sem):
        self.eng, self.sem, self.n = eng, sem, 0

    def step(self, ins):
        ins.then_inc(self.sem, 1)
        self.n += 1
        return ins

    def sync(self):
        self.eng.wait_ge(self.sem, self.n)


def build_nc():
    nc = bass.Bass("TRN2", target_bir_lowering=False,
                   detect_race_conditions=bool(int(os.environ.get("KMK_RACE", "0"))))

    x_seq = nc.dram_tensor("x_seq", [NS, D], F32, kind="ExternalInput")
    w_qkv = nc.dram_tensor("w_qkv", [128, 8, 3 * D], F32, kind="ExternalInput")
    w_proj = nc.dram_tensor("w_proj", [128, 8, D], F32, kind="ExternalInput")
    w1r = nc.dram_tensor("w1r", [NE, 128, 8, DF], F32, kind="ExternalInput")
    w2r = nc.dram_tensor("w2r", [8, 128, NE, 4, 128], F32, kind="ExternalInput")
    b1c = nc.dram_tensor("b1c", [128, NE * 4], F32, kind="ExternalInput")
    b2 = nc.dram_tensor("b2", [NE, D], F32, kind="ExternalInput")
    rw = nc.dram_tensor("rw", [128, 8, NE], F32, kind="ExternalInput")
    rbias = nc.dram_tensor("rbias", [128, NE], F32, kind="ExternalInput")
    qg2 = nc.dram_tensor("qg2", [128, 1], F32, kind="ExternalInput")
    qb2 = nc.dram_tensor("qb2", [128, 1], F32, kind="ExternalInput")
    kg2 = nc.dram_tensor("kg2", [128, 1], F32, kind="ExternalInput")
    kb2 = nc.dram_tensor("kb2", [128, 1], F32, kind="ExternalInput")
    idf = nc.dram_tensor("idf", [128, 128], F32, kind="ExternalInput")
    idb = nc.dram_tensor("idb", [128, 128], BF16, kind="ExternalInput")
    hones = nc.dram_tensor("hones", [128, 128], BF16, kind="ExternalInput")
    aones = nc.dram_tensor("aones", [128, 128], BF16, kind="ExternalInput")
    onesrow = nc.dram_tensor("onesrow", [1, 128], BF16, kind="ExternalInput")
    onescolf = nc.dram_tensor("onescolf", [1, 128], F32, kind="ExternalInput")
    emats = nc.dram_tensor("emats", [8, NE * 128], BF16, kind="ExternalInput")
    zsel = nc.dram_tensor("zsel", [16, 8 * 128], BF16, kind="ExternalInput")
    epscol = nc.dram_tensor("epscol", [128, 1], F32, kind="ExternalInput")

    out = nc.dram_tensor("out", [NQ, D], F32, kind="ExternalOutput")
    attn_scr = nc.dram_tensor("attn_scr", [D, NQ], BF16, kind="Internal")
    x1_scr = nc.dram_tensor("x1_scr", [D, NQ], F32, kind="Internal")
    w1s = nc.dram_tensor("w1s", [NE, 128, 8 * DF], BF16, kind="Internal")
    w2s = nc.dram_tensor("w2s", [8, 128, 32 * 128], BF16, kind="Internal")

    dbg = {}
    def dbg_t(name, shape, dt):
        if DEBUG:
            dbg[name] = nc.dram_tensor("dbg_" + name, shape, dt, kind="ExternalOutput")
    dbg_t("qTb", [128, 8 * NQ], BF16)
    dbg_t("kTb", [128, 8 * NS], BF16)
    dbg_t("vt", [128, 32 * H * 65], BF16)
    dbg_t("cb", [128, H], F32)
    dbg_t("attn", [D, NQ], BF16)
    dbg_t("x1", [D, NQ], F32)
    dbg_t("mask", [NQ, NE], F32)

    es0 = ExitStack()
    sb0 = lambda n, s, d=F32: es0.enter_context(nc.sbuf_tensor(n, s, d))
    wcl = es0.enter_context(nc.semaphore("wcl"))
    wcs = es0.enter_context(nc.semaphore("wcs"))
    wcb = sb0("wcb", [128, 2048], BF16)

    c_idf = sb0("c_idf", [128, 128])
    c_idb = sb0("c_idb", [128, 128], BF16)
    c_hones = sb0("c_hones", [128, 128], BF16)
    c_aones = sb0("c_aones", [128, 128], BF16)
    c_onesrow = sb0("c_onesrow", [1, 128], BF16)
    c_onescolf = sb0("c_onescolf", [1, 128])
    c_emats = sb0("c_emats", [8, NE * 128], BF16)
    c_qg2 = sb0("c_qg2", [128, 1]); c_qb2 = sb0("c_qb2", [128, 1])
    c_kg2 = sb0("c_kg2", [128, 1]); c_kb2 = sb0("c_kb2", [128, 1])
    c_eps = sb0("c_eps", [128, 1])
    c_rbias = sb0("c_rbias", [128, NE])
    c_cb = sb0("c_cb", [128, H])
    c_zsel = sb0("c_zsel", [16, 8 * 128], BF16)
    zrball = sb0("zrball", [16, NQ], BF16)

    with nc.semaphore("p0s") as p0s, nc.Block() as blk:
        @blk.sync
        def _(y):
            n = 0
            for dst, src in [
                (c_idf, idf), (c_idb, idb), (c_hones, hones), (c_aones, aones),
                (c_onesrow, onesrow), (c_onescolf, onescolf), (c_emats, emats),
                (c_qg2, qg2), (c_qb2, qb2), (c_kg2, kg2), (c_kb2, kb2),
                (c_eps, epscol), (c_rbias, rbias), (c_zsel, zsel),
            ]:
                y.dma_start(dst[:, :], src[:, :]).then_inc(p0s, 16)
                n += 16
            y.wait_ge(p0s, n)
    nc.all_engine_barrier()

    es1 = ExitStack()
    sb1 = lambda n, s, d=F32: es1.enter_context(nc.sbuf_tensor(n, s, d))
    qTb = sb1("qTb", [128, 8 * NQ], BF16)
    kTb = sb1("kTb", [128, 8 * NS], BF16)

    with ExitStack() as esp:
        sbp = lambda n, s, d=F32: esp.enter_context(nc.sbuf_tensor(n, s, d))
        psp = lambda n, s, d=F32: esp.enter_context(nc.psum_tensor(n, s, d))
        wqk = sbp("wqk", [128, 8 * 2048], BF16)
        xb = sbp("xb", [128, 2 * D], BF16)
        xT = sbp("xT", [128, 2 * 8 * 512], BF16)
        stageb = sbp("stageb", [128, 4 * 512], BF16)
        sqb = sbp("sqb", [128, 4 * 512], BF16)
        m64 = sbp("m64", [128, 4 * 512])
        msq = sbp("msq", [128, 2 * 512])
        varr = sbp("varr", [128, 2 * 512])
        lvr = sbp("lvr", [128, 2 * 512])
        rstd = sbp("rstd", [128, 4 * 512])
        uu = sbp("uu", [128, 2 * 512])
        ww = sbp("ww", [128, 2 * 512])
        mark1 = sbp("mark1", [1, 1])
        pT = psp("pT", [128, 1024], BF16)
        pT2 = psp("pT2", [128, 1024], BF16)
        pmm = [psp("pmm0", [128, 512]), psp("pmm1", [128, 512]),
               psp("pmm2", [128, 512]), psp("pmm3", [128, 512])]
        pst = [psp("pst0", [128, 512]), psp("pst1", [128, 512])]

        NCH, NG = 32, 8
        units = []
        for g in range(NG):
            for t in range(8):
                units.append((1, t, g))
            if g < 4:
                for t in range(8):
                    units.append((0, t, g))
        NU = len(units)
        ucum = [0] * (NG + 1)
        for g in range(NG):
            ucum[g + 1] = ucum[g] + (16 if g < 4 else 8)

        def unit_dst(kind, t, g):
            base = (qTb if kind == 0 else kTb)
            return base[:, t * (NQ if kind == 0 else NS) + 512 * g:
                        t * (NQ if kind == 0 else NS) + 512 * (g + 1)]

        with (
            nc.semaphore("dxg") as dxg,
            nc.semaphore("tT") as tT,
            nc.semaphore("eT") as eT,
            nc.semaphore("mmq") as mmq,
            nc.semaphore("evq") as evq,
            nc.semaphore("smq") as smq,
            nc.semaphore("dch") as dch,
            nc.semaphore("stl") as stl,
            nc.semaphore("apq") as apq,
            nc.semaphore("aplast") as aplast,
            nc.semaphore("evs") as evs,
            nc.semaphore("m64s") as m64s,
            nc.semaphore("vself") as vself,
            nc.semaphore("sself") as sself,
            nc.semaphore("dbgs") as dbgs,
            nc.Block() as blk,
        ):
            @blk.gpsimd
            def _(g):
                g.dma_start(wqk[:, :].rearrange("p (c n) -> p c n", c=8),
                            w_qkv[:, :, 0:2048]).then_inc(dxg, 16)
                for ch in range(NCH):
                    if ch >= 2:
                        g.wait_ge(tT, 8 * (ch - 1))
                    g.dma_start(xb[:, (ch % 2) * D:(ch % 2 + 1) * D],
                                x_seq[128 * ch:128 * (ch + 1), :]).then_inc(dxg, 16)

            @blk.tensor
            def _(t):
                emitted = [0]

                def qkv_unit(u):
                    kind, tl, g = units[u]
                    if u >= 4:
                        t.wait_ge(aplast, u - 3)
                    ins = None
                    for c in range(8):
                        ins = t.matmul(
                            pmm[u % 4][:, :],
                            wqk[:, c * 2048 + kind * 1024 + 128 * tl:
                                c * 2048 + kind * 1024 + 128 * tl + 128],
                            xT[:, ((g % 2) * 8 + c) * 512:((g % 2) * 8 + c) * 512 + 512],
                            start=(c == 0), stop=(c == 7))
                    ins.then_inc(mmq, 1)

                def stat_unit(u):
                    t.wait_ge(evq, u + 1)
                    if u >= 1:
                        t.wait_ge(dch, u)
                    par = u % 4
                    t.matmul(pst[0][:, :], c_hones[:, :], stageb[:, 512 * par:512 * (par + 1)],
                             start=True, stop=True)
                    t.matmul(pst[1][:, :], c_hones[:, :], sqb[:, 512 * par:512 * (par + 1)],
                             start=True, stop=True).then_inc(smq, 1)

                u = 0
                for g in range(NG):
                    for ch in range(4 * g, 4 * g + 4):
                        t.wait_ge(dxg, max(48, 16 * (ch + 2)))
                        if ch >= 2:
                            t.wait_ge(eT, 8 * (ch - 1))
                        pt = pT if ch % 2 == 0 else pT2
                        ins = None
                        for j in range(8):
                            ins = t.transpose(pt[:, 128 * j:128 * (j + 1)],
                                              xb[:, (ch % 2) * D + 128 * j:(ch % 2) * D + 128 * (j + 1)],
                                              c_idb[:, :])
                        ins.then_inc(tT, 8)
                    t.wait_ge(eT, 8 * (4 * g + 4))
                    for _i in range(ucum[g + 1] - ucum[g]):
                        qkv_unit(u)
                        if u > ucum[g]:
                            stat_unit(u - 1)
                        u += 1
                    stat_unit(u - 1)

            @blk.scalar
            def _(s):
                ss = SelfSem(s, sself)
                for g in range(NG):
                    for ch in range(4 * g, 4 * g + 4):
                        s.wait_ge(tT, 8 * (ch + 1))
                        if g >= 2:
                            s.wait_ge(mmq, ucum[g - 1])
                        pt = pT if ch % 2 == 0 else pT2
                        ins = None
                        for j in range(8):
                            ins = s.copy(
                                xT[:, ((g % 2) * 8 + j) * 512 + 128 * (ch % 4):
                                   ((g % 2) * 8 + j) * 512 + 128 * (ch % 4) + 128],
                                pt[:, 128 * j:128 * (j + 1)])
                        ins.then_inc(eT, 8)
                    def sevac(u):
                        p4e = u % 4
                        s.wait_ge(mmq, u + 1)
                        if u >= 4:
                            s.wait_ge(smq, u - 3)
                        s.copy(stageb[:, 512 * p4e:512 * (p4e + 1)],
                               pmm[p4e][:, :]).then_inc(evs, 1)

                    def schain(u):
                        p4_ = u % 4
                        p2_ = u % 2
                        s.wait_ge(dch, u + 1)
                        ins = s.activation(lvr[:, 512 * p2_:512 * (p2_ + 1)],
                                           varr[:, 512 * p2_:512 * (p2_ + 1)],
                                           AF.Ln, bias=c_eps[:, 0:1], scale=1.0)
                        ss.step(ins)
                        ss.sync()
                        s.activation(rstd[:, 512 * p4_:512 * (p4_ + 1)],
                                     lvr[:, 512 * p2_:512 * (p2_ + 1)],
                                     AF.Exp, bias=0.0, scale=-0.5).then_inc(stl, 1)

                    def sm64(u):
                        p4_ = u % 4
                        s.wait_ge(smq, u + 1)
                        if u >= 4:
                            s.wait_ge(dch, u - 3)
                        s.activation(m64[:, 512 * p4_:512 * (p4_ + 1)], pst[0][:, :],
                                     AF.Copy, bias=0.0, scale=1.0 / DH).then_inc(m64s, 1)

                    st_, en_ = ucum[g], ucum[g + 1]
                    for u in range(st_, en_):
                        sevac(u)
                        sm64(u)
                        if u >= st_ + 2:
                            schain(u - 2)
                    schain(en_ - 2)
                    schain(en_ - 1)

            @blk.vector
            def _(v):
                vs = SelfSem(v, vself)
                v.memset(mark1[:, :], 0.0)

                def ph1(u):
                    p4_ = u % 4
                    s4 = slice(512 * p4_, 512 * (p4_ + 1))
                    v.wait_ge(evs, u + 1)
                    v.tensor_tensor(sqb[:, s4], stageb[:, s4], stageb[:, s4],
                                    ALU.mult).then_inc(evq, 1)

                def ph2(u):
                    p4_ = u % 4
                    p2_ = u % 2
                    s4 = slice(512 * p4_, 512 * (p4_ + 1))
                    s2 = slice(512 * p2_, 512 * (p2_ + 1))
                    v.wait_ge(m64s, u + 1)
                    vs.step(v.tensor_tensor(msq[:, s2], m64[:, s4], m64[:, s4], ALU.mult))
                    vs.sync()
                    v.scalar_tensor_tensor(varr[:, s2], pst[1][:, :], 1.0 / DH, msq[:, s2],
                                           ALU.mult, ALU.subtract).then_inc(dch, 1)
                    vs.step(v.tensor_tensor(uu[:, s2], pmm[p4_][:, :], m64[:, s4], ALU.subtract))
                    vs.sync()
                    v.tensor_copy(mark1[:, :], mark1[:, :]).then_inc(aplast, 1)

                def ph3(u):
                    kind, tl, g = units[u]
                    p4_ = u % 4
                    p2_ = u % 2
                    s2 = slice(512 * p2_, 512 * (p2_ + 1))
                    s4 = slice(512 * p4_, 512 * (p4_ + 1))
                    v.wait_ge(stl, u + 1)
                    vs.step(v.tensor_tensor(ww[:, s2], uu[:, s2], rstd[:, s4], ALU.mult))
                    vs.sync()
                    gcol = c_qg2 if kind == 0 else c_kg2
                    bcol = c_qb2 if kind == 0 else c_kb2
                    v.tensor_scalar(unit_dst(kind, tl, g), ww[:, s2],
                                    gcol[:, 0:1], bcol[:, 0:1],
                                    ALU.mult, ALU.add).then_inc(apq, 1)

                for g in range(NG):
                    st_, en_ = ucum[g], ucum[g + 1]
                    for u in range(st_, en_):
                        ph1(u)
                        if u >= st_ + 1:
                            ph2(u - 1)
                        if u >= st_ + 2:
                            ph3(u - 2)
                    ph2(en_ - 1)
                    ph3(en_ - 2)
                    ph3(en_ - 1)

            @blk.sync
            def _(y):
                y.wait_ge(apq, NU)
                if DEBUG:
                    y.dma_start(dbg["qTb"][:, :], qTb[:, :]).then_inc(dbgs, 16)
                    y.dma_start(dbg["kTb"][:, :], kTb[:, :]).then_inc(dbgs, 16)
                    y.wait_ge(dbgs, 32)

    nc.all_engine_barrier()

    es1b = ExitStack()
    vt = es1b.enter_context(nc.sbuf_tensor("vt", [128, 32 * H * 65], BF16))

    with ExitStack() as esp:
        sbp = lambda n, s, d=F32: esp.enter_context(nc.sbuf_tensor(n, s, d))
        psp = lambda n, s, d=F32: esp.enter_context(nc.psum_tensor(n, s, d))
        wv = sbp("wv", [128, 8 * 1024], BF16)
        xb = sbp("xbv", [128, 2 * D], BF16)
        xTc = sbp("xTc", [128, 2 * 8 * 128], BF16)
        pT = psp("pTv", [128, 1024], BF16)
        pT2 = psp("pTv2", [128, 1024], BF16)
        pv = [psp("pv0", [128, 512]), psp("pv1", [128, 512]),
              psp("pv2", [128, 512]), psp("pv3", [128, 512])]

        with (
            nc.semaphore("dxg2") as dxg,
            nc.semaphore("tT2") as tT,
            nc.semaphore("eT2") as eT,
            nc.semaphore("mmv") as mmv,
            nc.semaphore("evv") as evv,
            nc.semaphore("ones") as onesem,
            nc.semaphore("dbgs2") as dbgs2,
            nc.Block() as blk,
        ):
            @blk.vector
            def _(v):
                v.memset(vt[:, :].rearrange("p (kc h d) -> p kc h d", kc=32, h=H)[:, :, :, 64:65],
                         1.0).then_inc(onesem, 1)

            @blk.gpsimd
            def _(g):
                g.dma_start(wv[:, :].rearrange("p (c n) -> p c n", c=8),
                            w_qkv[:, :, 2048:3072]).then_inc(dxg, 16)
                for ch in range(32):
                    if ch >= 2:
                        g.wait_ge(tT, 8 * (ch - 1))
                    g.dma_start(xb[:, (ch % 2) * D:(ch % 2 + 1) * D],
                                x_seq[128 * ch:128 * (ch + 1), :]).then_inc(dxg, 16)

            @blk.tensor
            def _(t):
                for ch in range(32):
                    par = ch % 2
                    t.wait_ge(dxg, max(48, 16 * (ch + 2)))
                    if ch >= 2:
                        t.wait_ge(eT, 8 * (ch - 1))
                    pt = pT if par == 0 else pT2
                    ins = None
                    for j in range(8):
                        ins = t.transpose(pt[:, 128 * j:128 * (j + 1)],
                                          xb[:, par * D + 128 * j:par * D + 128 * (j + 1)],
                                          c_idb[:, :])
                    ins.then_inc(tT, 8)
                    t.wait_ge(eT, 8 * (ch + 1))
                    if ch >= 2:
                        t.wait_ge(evv, 2 * (ch - 1))
                    for half in range(2):
                        ins = None
                        for c in range(8):
                            ins = t.matmul(
                                pv[2 * par + half][:, :],
                                xTc[:, (par * 8 + c) * 128:(par * 8 + c) * 128 + 128],
                                wv[:, c * 1024 + 512 * half:c * 1024 + 512 * half + 512],
                                start=(c == 0), stop=(c == 7))
                        ins.then_inc(mmv, 1)

            @blk.scalar
            def _(s):
                for ch in range(32):
                    par = ch % 2
                    s.wait_ge(tT, 8 * (ch + 1))
                    if ch >= 2:
                        s.wait_ge(mmv, 2 * (ch - 1))
                    pt = pT if par == 0 else pT2
                    ins = None
                    for j in range(8):
                        ins = s.copy(xTc[:, (par * 8 + j) * 128:(par * 8 + j) * 128 + 128],
                                     pt[:, 128 * j:128 * (j + 1)])
                    ins.then_inc(eT, 8)

            @blk.vector
            def _(v):
                v.wait_ge(onesem, 1)
                for ch in range(32):
                    par = ch % 2
                    v.wait_ge(mmv, 2 * (ch + 1))
                    dst = vt[:, :].rearrange("p (kc h d) -> p kc h d", kc=32, h=H)[:, ch, :, 0:64]
                    ins = v.tensor_copy(
                        dst[:, 0:8, :],
                        pv[2 * par][:, :].rearrange("p (h d) -> p h d", h=8))
                    ins = v.tensor_copy(
                        dst[:, 8:16, :],
                        pv[2 * par + 1][:, :].rearrange("p (h d) -> p h d", h=8))
                    ins.then_inc(evv, 2)

            @blk.sync
            def _(y):
                y.wait_ge(evv, 64)
                if DEBUG:
                    y.dma_start(dbg["vt"][:, :], vt[:, :]).then_inc(dbgs2, 16)
                    y.wait_ge(dbgs2, 16)

    nc.all_engine_barrier()

    if LASTPH == "p1":
        es1b.close(); es1.close(); es0.close()
        return nc

    with ExitStack() as esp:
        sbp = lambda n, s, d=F32: esp.enter_context(nc.sbuf_tensor(n, s, d))
        psp = lambda n, s, d=F32: esp.enter_context(nc.psum_tensor(n, s, d))
        sqt = sbp("sqt", [128, NS], BF16)
        nmx = sbp("nmx", [128, 8])
        qnq = sbp("qnq", [128, 8])
        qnk = sbp("qnk", [128, 8])
        prod = sbp("prod", [128, 8])
        hsel = sbp("hsel", [128, 256])
        mark2 = sbp("mark2", [1, 1])
        pns = [psp("pns0", [128, 512]), psp("pns1", [128, 512])]
        pcb = psp("pcb", [128, 2])

        with (
            nc.semaphore("vsq2") as vsq,
            nc.semaphore("tsm2") as tsm,
            nc.semaphore("vrd2") as vrdc,
            nc.semaphore("sch2") as sch,
            nc.semaphore("tbc2") as tbc,
            nc.semaphore("vcb2") as vcb,
            nc.semaphore("vself2") as vself2,
            nc.semaphore("sself2") as sself2,
            nc.Block() as blk,
        ):
            jobs = []
            for t in range(8):
                jobs.append((0, t, 4))
            for t in range(8):
                jobs.append((1, t, 8))

            @blk.vector
            def _(v):
                vs = SelfSem(v, vself2)
                v.memset(mark2[:, :], 0.0)
                v.memset(hsel[:, :], 0.0)
                vs.step(v.memset(hsel[0:64, 0:128], 1.0 / 64))
                vs.step(v.memset(hsel[64:128, 128:256], 1.0 / 64))
                vs.sync()
                v.tensor_copy(mark2[:, :], mark2[:, :]).then_inc(vcb, 1)
                nred = 0
                for j, (kind, t, nch) in enumerate(jobs):
                    srcb = qTb if kind == 0 else kTb
                    width = NQ if kind == 0 else NS
                    if j >= 1:
                        v.wait_ge(tsm, nred)
                    vs.step(v.tensor_tensor(sqt[:, 0:width],
                                            srcb[:, t * width:(t + 1) * width],
                                            srcb[:, t * width:(t + 1) * width],
                                            ALU.mult))
                    vs.sync()
                    v.tensor_copy(mark2[:, :], mark2[:, :]).then_inc(vsq, 1)
                    for ch in range(nch):
                        v.wait_ge(tsm, nred + ch + 1)
                        v.reduce_max(nmx[:, ch:ch + 1], pns[ch % 2][:, :],
                                     axis=AX.X).then_inc(vrdc, 1)
                    nred += nch
                    v.wait_ge(vrdc, nred)
                    dstcol = (qnq if kind == 0 else qnk)[:, t:t + 1]
                    vs.step(v.reduce_max(dstcol, nmx[:, 0:nch], axis=AX.X))
                    vs.sync()
                    if kind == 1:
                        vs.step(v.tensor_tensor(prod[:, t:t + 1], qnq[:, t:t + 1],
                                                qnk[:, t:t + 1], ALU.mult))
                        vs.sync()
                        v.tensor_copy(mark2[:, :], mark2[:, :]).then_inc(sch, 1)
                        v.wait_ge(tbc, t + 1)
                        vs.step(v.tensor_scalar(c_cb[:, 2 * t:2 * t + 1], pcb[:, 0:1],
                                                -1.0, None, ALU.mult))
                        vs.step(v.tensor_scalar(c_cb[:, 2 * t + 1:2 * t + 2], pcb[:, 1:2],
                                                -1.0, None, ALU.mult))
                        vs.sync()
                        v.tensor_copy(mark2[:, :], mark2[:, :]).then_inc(vcb, 1)

            @blk.tensor
            def _(t_):
                t_.wait_ge(vcb, 1)
                nred = 0
                for j, (kind, t, nch) in enumerate(jobs):
                    t_.wait_ge(vsq, j + 1)
                    for ch in range(nch):
                        if nred + ch >= 2:
                            t_.wait_ge(vrdc, nred + ch - 1)
                        t_.matmul(pns[ch % 2][:, :], c_hones[:, :],
                                  sqt[:, 512 * ch:512 * (ch + 1)],
                                  start=True, stop=True).then_inc(tsm, 1)
                    nred += nch
                    if kind == 1:
                        t_.wait_ge(sch, t + 1)
                        if t >= 1:
                            t_.wait_ge(vcb, 1 + t)
                        t_.matmul(pcb[:, 0:1], hsel[:, 0:128], prod[:, t:t + 1],
                                  start=True, stop=True)
                        t_.matmul(pcb[:, 1:2], hsel[:, 128:256], prod[:, t:t + 1],
                                  start=True, stop=True).then_inc(tbc, 1)

    nc.all_engine_barrier()
    with ExitStack() as esp:
        sbp = lambda n, s, d=F32: esp.enter_context(nc.sbuf_tensor(n, s, d))
        tmpa = sbp("tmpa", [128, H])
        with (
            nc.semaphore("fx1") as fx1,
            nc.semaphore("fx2") as fx2,
            nc.Block() as blk,
        ):
            @blk.vector
            def _(v):
                v.tensor_scalar(tmpa[:, :], c_cb[:, :], -1.0, None,
                                ALU.mult).then_inc(fx1, 1)
                v.wait_ge(fx2, 2)
                v.tensor_scalar(c_cb[:, :], tmpa[:, :], -0.125, None,
                                ALU.mult).then_inc(fx1, 1)

            @blk.scalar
            def _(s):
                s.wait_ge(fx1, 1)
                s.activation(tmpa[:, :], tmpa[:, :], AF.Ln, bias=0.0,
                             scale=1.0).then_inc(fx2, 1)
                s.wait_ge(fx2, 1)
                s.activation(tmpa[:, :], tmpa[:, :], AF.Exp, bias=0.0,
                             scale=0.5).then_inc(fx2, 1)

            @blk.sync
            def _(y):
                if DEBUG:
                    y.wait_ge(fx1, 2)
                    with nc.semaphore("dcb") as dcb:
                        y.dma_start(dbg["cb"][:, :], c_cb[:, :]).then_inc(dcb, 16)
                        y.wait_ge(dcb, 16)

    nc.all_engine_barrier()

    if LASTPH == "p2":
        es1b.close(); es1.close(); es0.close()
        return nc

    with ExitStack() as esp:
        sbp = lambda n, s, d=F32: esp.enter_context(nc.sbuf_tensor(n, s, d))
        psp = lambda n, s, d=F32: esp.enter_context(nc.psum_tensor(n, s, d))
        probs0 = sbp("probs0", [128, 2048], BF16)
        probs1 = sbp("probs1", [128, 2048], BF16)
        att0 = sbp("att0", [128, 1024], BF16)
        att1 = sbp("att1", [128, 1024], BF16)
        zpk = sbp("zpk", [128, 16], BF16)
        zpkf = sbp("zpkf", [128, 16])
        zpki = sbp("zpki", [128, 16])
        psc = [psp("psc0", [128, 1024]), psp("psc1", [128, 1024])]
        pE0 = psp("pE0", [128, 1024])
        pE1 = psp("pE1", [128, 1024])

        with (
            nc.semaphore("scm") as scm,
            nc.semaphore("exq") as exq,
            nc.semaphore("avm") as avm,
            nc.semaphore("rev") as rev,
            nc.semaphore("zpd") as zpd,
            nc.semaphore("zrc") as zrc,
            nc.semaphore("zud") as zud,
            nc.semaphore("std") as std,
            nc.semaphore("vself3") as vself3,
            nc.Block() as blk,
        ):
            NIT = 16
            NC_ = 32

            def it_hp(it):
                return it // 2, it % 2


            @blk.tensor
            def _(t_):
                def emit_attv(k):
                    it, c = k // NC_, k % NC_
                    hp, qc = it_hp(it)
                    for i in range(2):
                        t_.wait_ge(exq, 2 * k + i + 1)
                        if c == 0:
                            t_.wait_ge(rev, 2 * it)
                        pe = pE0 if i == 0 else pE1
                        pr = probs0 if i == 0 else probs1
                        ins = None
                        for n in range(2):
                            ins = t_.matmul(
                                pe[0:65, 512 * n:512 * (n + 1)],
                                vt[:, (c * H + 2 * hp + i) * 65:
                                   (c * H + 2 * hp + i) * 65 + 65],
                                pr[:, (c % 2) * 1024 + 512 * n:(c % 2) * 1024 + 512 * n + 512],
                                start=(c == 0), stop=(c == NC_ - 1))
                        ins.then_inc(avm, 1)

                for k in range(NIT * NC_):
                    it, c = k // NC_, k % NC_
                    hp, qc = it_hp(it)
                    for i in range(2):
                        if 2 * k + i >= 2:
                            t_.wait_ge(exq, 2 * k + i - 1)
                        ins = None
                        for n in range(2):
                            ins = t_.matmul(
                                psc[i][:, 512 * n:512 * (n + 1)],
                                kTb[64 * i:64 * i + 64,
                                    hp * NS + 128 * c:hp * NS + 128 * c + 128],
                                qTb[64 * i:64 * i + 64,
                                    hp * NQ + 1024 * qc + 512 * n:
                                    hp * NQ + 1024 * qc + 512 * n + 512],
                                start=True, stop=True,
                                tile_position=(64 * i, 0))
                        ins.then_inc(scm, 1)
                    if k >= 1:
                        emit_attv(k - 1)
                emit_attv(NIT * NC_ - 1)

            @blk.scalar
            def _(s):
                for it in range(NIT):
                    hp, qc = it_hp(it)
                    for c in range(NC_):
                        k = it * NC_ + c
                        for i in range(2):
                            h = 2 * hp + i
                            pr = probs0 if i == 0 else probs1
                            s.wait_ge(scm, 2 * k + i + 1)
                            if k >= 2:
                                s.wait_ge(avm, 2 * (k - 2) + i + 1)
                            s.activation(pr[:, (c % 2) * 1024:(c % 2) * 1024 + 1024],
                                         psc[i][:, :], AF.Exp,
                                         bias=c_cb[:, h:h + 1],
                                         scale=0.125).then_inc(exq, 1)

            @blk.vector
            def _(v):
                vs = SelfSem(v, vself3)
                for it in range(NIT):
                    hp, qc = it_hp(it)
                    v.wait_ge(avm, 64 * (it + 1))
                    if it >= 1:
                        v.wait_ge(std, 32 * it)
                        v.wait_ge(zpd, 32 * it)
                    v.tensor_copy(att0[0:65, :], pE0[0:65, :])
                    v.tensor_copy(att1[0:65, :], pE1[0:65, :]).then_inc(rev, 2)
                    v.wait_ge(zpd, 32 * it + 32)
                    vs.step(v.tensor_copy(zpkf[:, :], zpk[:, :]))
                    vs.sync()
                    v.reciprocal(zpki[:, :], zpkf[:, :]).then_inc(zrc, 1)

            @blk.gpsimd
            def _(g):
                for it in range(NIT):
                    g.wait_ge(rev, 2 * it + 2)
                    g.dma_start(zpk[:, 0:8], att0[64:65, :]).then_inc(zpd, 16)
                    g.dma_start(zpk[:, 8:16], att1[64:65, :]).then_inc(zpd, 16)
                    hp, qc = it_hp(it)
                    g.wait_ge(zrc, it + 1)
                    g.dma_start(zrball[2 * hp:2 * hp + 1, 1024 * qc:1024 * (qc + 1)],
                                zpki[:, 0:8]).then_inc(zud, 16)
                    g.dma_start(zrball[2 * hp + 1:2 * hp + 2, 1024 * qc:1024 * (qc + 1)],
                                zpki[:, 8:16]).then_inc(zud, 16)
                    for k in (2 * it, 2 * it + 1):
                        if k >= 1:
                            g.wait_ge(wcs, 16 * k)
                        if k < 16:
                            src_ = w1r[k // 2, :, :, :].rearrange("p c n -> p (c n)")
                        else:
                            src_ = w2r[(k - 16) // 2, :, :, :, :].rearrange("p e c n -> p (e c n)")
                        g.dma_start(wcb[:, :],
                                    src_[:, 2048 * (k & 1):2048 * (k & 1) + 2048]).then_inc(wcl, 16)

            @blk.sync
            def _(y):
                for it in range(NIT):
                    hp, qc = it_hp(it)
                    y.wait_ge(rev, 2 * it + 2)
                    y.dma_start(attn_scr[128 * hp:128 * hp + 64, 1024 * qc:1024 * (qc + 1)],
                                att0[0:64, :]).then_inc(std, 16)
                    y.dma_start(attn_scr[128 * hp + 64:128 * hp + 128, 1024 * qc:1024 * (qc + 1)],
                                att1[0:64, :]).then_inc(std, 16)
                    for k in (2 * it, 2 * it + 1):
                        y.wait_ge(wcl, 16 * (k + 1))
                        if k < 16:
                            dst_ = w1s[k // 2, :, 2048 * (k & 1):2048 * (k & 1) + 2048]
                        else:
                            dst_ = w2s[(k - 16) // 2, :, 2048 * (k & 1):2048 * (k & 1) + 2048]
                        y.dma_start(dst_, wcb[:, :]).then_inc(wcs, 16)
                y.wait_ge(std, 32 * NIT)
                y.wait_ge(zud, 32 * NIT)
                y.wait_ge(wcs, 512)
                if DEBUG:
                    with nc.semaphore("datn") as datn:
                        y.dma_start(dbg["attn"][:, :], attn_scr[:, :]).then_inc(datn, 16)
                        y.wait_ge(datn, 16)

    nc.all_engine_barrier()

    if LASTPH == "p3":
        es1b.close(); es1.close(); es0.close()
        return nc

    es1b.close()
    es1.close()

    es2 = ExitStack()
    x1b = es2.enter_context(nc.sbuf_tensor("x1b", [128, 8 * NQ], BF16))

    with ExitStack() as esp:
        sbp = lambda n, s, d=F32: esp.enter_context(nc.sbuf_tensor(n, s, d))
        psp = lambda n, s, d=F32: esp.enter_context(nc.psum_tensor(n, s, d))
        wpb = sbp("wpb", [128, 8 * 1024], BF16)
        attnb = sbp("attnb", [128, 2 * 8 * 512], BF16)
        xb2 = sbp("xb2", [128, 2 * D])
        xoT = sbp("xoT", [128, 8 * 512])
        x1raw = sbp("x1raw", [128, 8 * 512])
        x1rb = sbp("x1rb", [128, 512], BF16)
        sq4 = sbp("sq4", [128, 512], BF16)
        m1k = sbp("m1k", [128, 512])
        msq4 = sbp("msq4", [128, 512])
        var4 = sbp("var4", [128, 512])
        lv4 = sbp("lv4", [128, 512])
        rstd4 = sbp("rstd4", [128, 512])
        uu4 = sbp("uu4", [128, 512])
        x1f = sbp("x1f", [128, 8 * 512])
        mark4 = sbp("mark4", [1, 1])
        attnd = sbp("attnd", [128, 2 * 8 * 512], BF16)
        pTx = [psp("pTx0", [128, 512]), psp("pTx1", [128, 512])]
        pp = [psp("pp0", [128, 512]), psp("pp1", [128, 512])]
        pls = [psp("pls0", [128, 512]), psp("pls1", [128, 512])]
        pzb = [psp("pzb0", [128, 512]), psp("pzb1", [128, 512])]

        with (
            nc.semaphore("d4a") as d4a,
            nc.semaphore("d4g") as d4g,
            nc.semaphore("t4t") as t4t,
            nc.semaphore("e4") as e4,
            nc.semaphore("p4m") as p4m,
            nc.semaphore("r4") as r4,
            nc.semaphore("c4") as c4,
            nc.semaphore("s4m") as s4m,
            nc.semaphore("ch4") as ch4,
            nc.semaphore("sl4") as sl4,
            nc.semaphore("ap4") as ap4,
            nc.semaphore("st4") as st4,
            nc.semaphore("vs4") as vs4s,
            nc.semaphore("ss4") as ss4s,
            nc.semaphore("dv4") as dv4,
            nc.semaphore("zb4") as zb4,
            nc.Block() as blk,
        ):
            attn_v = attn_scr[:, :].rearrange("(f p) n -> p f n", p=128)
            x1_v = x1_scr[:, :].rearrange("(f p) n -> p f n", p=128)

            @blk.sync
            def _(y):
                for qc in range(4):
                    if qc >= 2:
                        y.wait_ge(dv4, 8 * (qc - 1))
                    y.dma_start(attnb[:, (qc % 2) * 4096:(qc % 2) * 4096 + 4096]
                                .rearrange("p (f n) -> p f n", f=8),
                                attn_v[:, :, 512 * qc:512 * (qc + 1)]).then_inc(d4a, 16)
                    y.wait_ge(ap4, 8 * (qc + 1))
                    y.dma_start(x1_v[:, :, 512 * qc:512 * (qc + 1)],
                                x1f[:, :].rearrange("p (f n) -> p f n", f=8)).then_inc(st4, 16)
                y.wait_ge(st4, 64)

            @blk.gpsimd
            def _(g):
                g.dma_start(wpb[:, :].rearrange("p (c n) -> p c n", c=8),
                            w_proj[:, :, :]).then_inc(d4g, 16)
                for ch in range(16):
                    if ch >= 2:
                        g.wait_ge(t4t, 8 * (ch - 1))
                    g.dma_start(xb2[:, (ch % 2) * D:(ch % 2 + 1) * D],
                                x_seq[128 * ch:128 * (ch + 1), :]).then_inc(d4g, 16)

            @blk.tensor
            def _(t_):
                for qc in range(4):
                    for ch in range(4 * qc, 4 * qc + 4):
                        t_.wait_ge(d4g, max(48, 16 * (ch + 2)))
                        for half in range(2):
                            bi = 2 * ch + half
                            if bi >= 2:
                                t_.wait_ge(e4, 4 * (bi - 1))
                            pt = pTx[bi % 2]
                            ins = None
                            for j in range(4 * half, 4 * half + 4):
                                ins = t_.transpose(pt[:, 128 * (j % 4):128 * (j % 4) + 128],
                                                   xb2[:, (ch % 2) * D + 128 * j:(ch % 2) * D + 128 * j + 128],
                                                   c_idf[:, :])
                            ins.then_inc(t4t, 4)
                    t_.wait_ge(d4a, 16 * (qc + 1))
                    for hp in range(8):
                        zi = qc * 8 + hp
                        if zi >= 2:
                            t_.wait_ge(dv4, zi - 1)
                        t_.matmul(pzb[zi % 2][:, :], c_zsel[:, 128 * hp:128 * hp + 128],
                                  zrball[:, 512 * qc:512 * (qc + 1)],
                                  start=True, stop=True).then_inc(zb4, 1)
                    t_.wait_ge(e4, 4 * (8 * qc + 8))
                    t_.wait_ge(dv4, 8 * (qc + 1))
                    for dout in range(8):
                        gi = qc * 8 + dout
                        if gi >= 2:
                            t_.wait_ge(r4, gi - 1)
                        ins = None
                        for c in range(8):
                            ins = t_.matmul(pp[gi % 2][:, :],
                                            wpb[:, c * 1024 + 128 * dout:c * 1024 + 128 * dout + 128],
                                            attnd[:, (qc % 2) * 4096 + 512 * c:(qc % 2) * 4096 + 512 * c + 512],
                                            start=(c == 0), stop=(c == 7))
                        ins.then_inc(p4m, 1)
                    if qc >= 1:
                        t_.wait_ge(ch4, qc)
                    for dout in range(8):
                        t_.wait_ge(c4, qc * 8 + dout + 1)
                        t_.matmul(pls[0][:, :], c_aones[:, :], x1rb[:, :],
                                  start=(dout == 0), stop=(dout == 7))
                        t_.matmul(pls[1][:, :], c_aones[:, :], sq4[:, :],
                                  start=(dout == 0), stop=(dout == 7)).then_inc(s4m, 1)

            @blk.vector
            def _(v):
                vs = SelfSem(v, vs4s)
                v.memset(mark4[:, :], 0.0)
                for qc in range(4):
                    for ch in range(4 * qc, 4 * qc + 4):
                        for half in range(2):
                            bi = 2 * ch + half
                            v.wait_ge(t4t, 4 * (bi + 1))
                            pt = pTx[bi % 2]
                            ins = None
                            for j in range(4 * half, 4 * half + 4):
                                ins = v.tensor_copy(
                                    xoT[:, j * 512 + 128 * (ch % 4):j * 512 + 128 * (ch % 4) + 128],
                                    pt[:, 128 * (j % 4):128 * (j % 4) + 128])
                            ins.then_inc(e4, 4)
                    if qc >= 2:
                        v.wait_ge(p4m, 8 * (qc - 1))
                    for hp in range(8):
                        zi = qc * 8 + hp
                        v.wait_ge(zb4, zi + 1)
                        v.tensor_tensor(attnd[:, (qc % 2) * 4096 + 512 * hp:(qc % 2) * 4096 + 512 * hp + 512],
                                        attnb[:, (qc % 2) * 4096 + 512 * hp:(qc % 2) * 4096 + 512 * hp + 512],
                                        pzb[zi % 2][:, :], ALU.mult).then_inc(dv4, 1)
                    for dout in range(8):
                        gi = qc * 8 + dout
                        v.wait_ge(p4m, gi + 1)
                        sl = slice(512 * dout, 512 * (dout + 1))
                        vs.step(v.tensor_tensor(x1raw[:, sl], pp[gi % 2][:, :],
                                                xoT[:, sl], ALU.add))
                        vs.sync()
                        v.tensor_copy(mark4[:, :], mark4[:, :]).then_inc(r4, 1)
                    for dout in range(8):
                        gi = qc * 8 + dout
                        sl = slice(512 * dout, 512 * (dout + 1))
                        if gi >= 1:
                            v.wait_ge(s4m, gi)
                        vs.step(v.tensor_copy(x1rb[:, :], x1raw[:, sl]))
                        vs.sync()
                        vs.step(v.tensor_tensor(sq4[:, :], x1rb[:, :], x1rb[:, :], ALU.mult))
                        vs.sync()
                        v.tensor_copy(mark4[:, :], mark4[:, :]).then_inc(c4, 1)
                    v.wait_ge(s4m, 8 * (qc + 1))
                    vs.step(v.tensor_scalar(m1k[:, :], pls[0][:, :], 1.0 / D, None, ALU.mult))
                    vs.sync()
                    vs.step(v.tensor_tensor(msq4[:, :], m1k[:, :], m1k[:, :], ALU.mult))
                    vs.sync()
                    vs.step(v.scalar_tensor_tensor(var4[:, :], pls[1][:, :], 1.0 / D,
                                                   msq4[:, :], ALU.mult, ALU.subtract))
                    vs.sync()
                    v.tensor_copy(mark4[:, :], mark4[:, :]).then_inc(ch4, 1)
                    v.wait_ge(sl4, qc + 1)
                    if qc >= 1:
                        v.wait_ge(st4, 16 * qc)
                    for dout in range(8):
                        sl = slice(512 * dout, 512 * (dout + 1))
                        vs.step(v.tensor_tensor(uu4[:, :], x1raw[:, sl], m1k[:, :], ALU.subtract))
                        vs.sync()
                        vs.step(v.tensor_tensor(x1f[:, sl], uu4[:, :], rstd4[:, :], ALU.mult))
                        vs.sync()
                        vs.step(v.tensor_copy(x1b[:, dout * NQ + 512 * qc:dout * NQ + 512 * qc + 512],
                                              x1f[:, sl]))
                        vs.sync()
                        v.tensor_copy(mark4[:, :], mark4[:, :]).then_inc(ap4, 1)

            @blk.scalar
            def _(s):
                ss = SelfSem(s, ss4s)
                for qc in range(4):
                    s.wait_ge(ch4, qc + 1)
                    ss.step(s.activation(lv4[:, :], var4[:, :], AF.Ln,
                                         bias=c_eps[:, 0:1], scale=1.0))
                    ss.sync()
                    s.activation(rstd4[:, :], lv4[:, :], AF.Exp, bias=0.0,
                                 scale=-0.5).then_inc(sl4, 1)

    nc.all_engine_barrier()

    if LASTPH == "p4":
        if DEBUG:
            with nc.semaphore("dx1") as dx1, nc.Block() as blk:
                @blk.sync
                def _(y):
                    y.dma_start(dbg["x1"][:, :], x1_scr[:, :]).then_inc(dx1, 16)
                    y.wait_ge(dx1, 16)
        es2.close(); es0.close()
        return nc

    with ExitStack() as esp:
        sbp = lambda n, s, d=F32: esp.enter_context(nc.sbuf_tensor(n, s, d))
        psp = lambda n, s, d=F32: esp.enter_context(nc.psum_tensor(n, s, d))
        c_rw = sbp("c_rw", [128, 64])
        c_b1 = sbp("c_b1", [128, 32])
        b2b = sbp("b2b", [8, D], BF16)
        x1fq = sbp("x1fq", [128, 8 * 512])
        lgT = sbp("lgT", [8, 512])
        ltok = sbp("ltok", [128, 32])
        mwork = sbp("mwork", [128, 8 * 10 * 4])
        maskt = sbp("maskt", [128, 32])
        maskTb = sbp("maskTb", [8, 512], BF16)
        w1b = sbp("w1b", [128, 2 * 8 * 512], BF16)
        w2b = sbp("w2b", [128, 2 * 4096], BF16)
        h1m = sbp("h1m", [128, 32 * 512], BF16)
        x2raw = sbp("x2raw", [128, 8 * 512])
        x2rb = sbp("x2rb", [128, 512], BF16)
        sq5 = sbp("sq5", [128, 512], BF16)
        m1k5 = sbp("m1k5", [128, 512])
        msq5 = sbp("msq5", [128, 512])
        var5 = sbp("var5", [128, 512])
        lv5 = sbp("lv5", [128, 512])
        rstd5 = sbp("rstd5", [128, 512])
        uu5 = sbp("uu5", [128, 512])
        outf = sbp("outf", [128, 8 * 512])
        outtok = sbp("outtok", [128, 4 * 1024])
        mark5 = sbp("mark5", [1, 1])
        pstat = [psp("pstat0", [128, 512]), psp("pstat1", [128, 512])]
        pmb = [psp("pmb0", [128, 512]), psp("pmb1", [128, 512])]
        pf1 = [psp("pf10", [128, 512]), psp("pf11", [128, 512])]
        pout = [psp("pout0", [128, 512]), psp("pout1", [128, 512])]

        sems = ExitStack()
        _sm = lambda n: sems.enter_context(nc.semaphore(n))
        d5c = _sm("d5c"); d5x = _sm("d5x"); d5w = _sm("d5w")
        rt5 = _sm("rt5"); rv5 = _sm("rv5"); rs5 = _sm("rs5")
        f1m = _sm("f1m"); g5 = _sm("g5"); h5 = _sm("h5")
        o5m = _sm("o5m"); r5 = _sm("r5"); c5 = _sm("c5")
        s5m = _sm("s5m"); ch5 = _sm("ch5"); sl5 = _sm("sl5")
        ap5 = _sm("ap5"); tr5 = _sm("tr5"); ev5 = _sm("ev5"); mb5m = _sm("mb5m")
        st5 = _sm("st5"); vs5s = _sm("vs5"); ss5s = _sm("ss5")
        with nc.Block() as blk:
            x1_v = x1_scr[:, :].rearrange("(f p) n -> p f n", p=128)

            @blk.sync
            def _(y):
                y.dma_start(c_rw[:, :].rearrange("p (c e) -> p c e", c=8),
                            rw[:, :, :]).then_inc(d5c, 16)
                y.dma_start(c_b1[:, :], b1c[:, :]).then_inc(d5c, 16)
                for qc in range(4):
                    if qc >= 1:
                        y.wait_ge(r5, 8 * qc)
                    y.dma_start(x1fq[:, :].rearrange("p (f n) -> p f n", f=8),
                                x1_v[:, :, 512 * qc:512 * (qc + 1)]).then_inc(d5x, 16)
                    for tc in range(4):
                        y.wait_ge(ev5, 32 * qc + 8 * (tc + 1))
                        y.dma_start(out[512 * qc + 128 * tc:512 * qc + 128 * tc + 128, :],
                                    outtok[:, 1024 * tc:1024 * (tc + 1)]).then_inc(st5, 16)
                y.wait_ge(st5, 16 * 16)

            @blk.gpsimd
            def _(g):
                g.dma_start(b2b[:, :], b2[:, :]).then_inc(d5w, 16)
                for qc in range(4):
                    for e in range(NE):
                        gi = qc * 8 + e
                        if gi >= 2:
                            g.wait_ge(f1m, 4 * (gi - 1))
                        g.dma_start(w1b[:, (gi % 2) * 4096:(gi % 2) * 4096 + 4096],
                                    w1s[e, :, :]).then_inc(d5w, 16)
                    for dt in range(8):
                        gj = qc * 8 + dt
                        if gj >= 2:
                            g.wait_ge(o5m, gj - 1)
                        g.dma_start(w2b[:, (gj % 2) * 4096:(gj % 2) * 4096 + 4096],
                                    w2s[dt, :, :]).then_inc(d5w, 16)

            @blk.tensor
            def _(t_):
                t_.wait_ge(d5c, 32)
                for qc in range(4):
                    t_.wait_ge(d5x, 16 * (qc + 1))
                    if qc >= 1:
                        t_.wait_ge(s5m, 8 * qc)
                        t_.wait_ge(ch5, qc)
                    ins = None
                    for c in range(8):
                        ins = t_.matmul(pstat[0][0:8, :], c_rw[:, 8 * c:8 * c + 8],
                                        x1fq[:, 512 * c:512 * (c + 1)],
                                        start=(c == 0), stop=(c == 7))
                    ins.then_inc(rt5, 1)
                    t_.wait_ge(rv5, 5 * qc + 1)
                    for tc in range(4):
                        t_.transpose(pstat[1][:, 32 * tc:32 * tc + 8],
                                     lgT[0:8, 128 * tc:128 * tc + 128],
                                     c_idf[0:8, 0:8]).then_inc(rt5, 1)
                    if qc >= 1:
                        t_.wait_ge(ev5, 32 * qc)
                    for e in range(NE):
                        gi = qc * 8 + e
                        t_.wait_ge(d5w, 16 * (2 + 16 * qc + e))
                        for j in range(4):
                            ji = gi * 4 + j
                            if ji >= 2:
                                t_.wait_ge(g5, ji - 1)
                            if e == 4 and j == 0:
                                t_.wait_ge(rv5, 5 * qc + 4)
                                t_.wait_ge(h5, 32 * qc)
                                for tc in range(4):
                                    t_.transpose(pmb[0][0:8, 128 * tc:128 * tc + 128],
                                                 maskt[:, 8 * tc:8 * tc + 8],
                                                 c_idf[:, :]).then_inc(rt5, 1)
                            ins = None
                            for c in range(8):
                                ins = t_.matmul(
                                    pf1[ji % 2][:, :],
                                    w1b[:, (gi % 2) * 4096 + 512 * c + 128 * j:
                                        (gi % 2) * 4096 + 512 * c + 128 * j + 128],
                                    x1b[:, c * NQ + 512 * qc:c * NQ + 512 * qc + 512],
                                    start=(c == 0), stop=(c == 7))
                            ins.then_inc(f1m, 1)
                    t_.wait_ge(rv5, 5 * qc + 5)
                    for e in range(NE):
                        gi = qc * 8 + e
                        t_.wait_ge(h5, 4 * max(0, gi - 1))
                        t_.matmul(pmb[gi % 2][:, :],
                                  c_emats[:, 128 * e:128 * e + 128],
                                  maskTb[:, :], start=True, stop=True).then_inc(mb5m, 1)
                    for dt in range(8):
                        gj = qc * 8 + dt
                        t_.wait_ge(d5w, 16 * (10 + 16 * qc + dt))
                        if gj >= 2:
                            t_.wait_ge(r5, gj - 1)
                        for e in range(NE):
                            t_.wait_ge(h5, qc * 32 + 4 * (e + 1))
                            for c in range(4):
                                t_.matmul(
                                    pout[gj % 2][:, :],
                                    w2b[:, (gj % 2) * 4096 + (e * 4 + c) * 128:
                                        (gj % 2) * 4096 + (e * 4 + c) * 128 + 128],
                                    h1m[:, (e * 4 + c) * 512:(e * 4 + c) * 512 + 512],
                                    start=(e == 0 and c == 0), stop=False)
                        t_.matmul(pout[gj % 2][:, :], b2b[:, 128 * dt:128 * dt + 128],
                                  maskTb[:, :], start=False, stop=True).then_inc(o5m, 1)
                    if qc >= 1:
                        t_.wait_ge(ch5, qc)
                    for dt in range(8):
                        t_.wait_ge(c5, qc * 8 + dt + 1)
                        t_.matmul(pstat[0][:, :], c_aones[:, :], x2rb[:, :],
                                  start=(dt == 0), stop=(dt == 7))
                        t_.matmul(pstat[1][:, :], c_aones[:, :], sq5[:, :],
                                  start=(dt == 0), stop=(dt == 7)).then_inc(s5m, 1)
                    t_.wait_ge(ap5, 8 * (qc + 1))
                    for tc in range(4):
                        ti = qc * 4 + tc
                        if ti >= 2:
                            t_.wait_ge(ev5, 8 * (ti - 1))
                        ins = None
                        for dtj in range(8):
                            ins = t_.transpose(pf1[tc % 2][:, 128 * (dtj % 4):128 * (dtj % 4) + 128]
                                               , outf[:, dtj * 512 + 128 * tc:dtj * 512 + 128 * tc + 128],
                                               c_idf[:, :])
                            if dtj == 3:
                                ins.then_inc(tr5, 4)
                                t_.wait_ge(ev5, 8 * ti + 4)
                        ins.then_inc(tr5, 4)

            @blk.vector
            def _(v):
                vs = SelfSem(v, vs5s)
                v.memset(mark5[:, :], 0.0)
                for qc in range(4):
                    v.wait_ge(rt5, 9 * qc + 1)
                    vs.step(v.tensor_copy(lgT[0:8, :], pstat[0][0:8, :]))
                    vs.sync()
                    v.tensor_copy(mark5[:, :], mark5[:, :]).then_inc(rv5, 1)
                    v.wait_ge(rt5, 9 * qc + 5)
                    for tc in range(4):
                        vs.step(v.tensor_tensor(ltok[:, 8 * tc:8 * tc + 8],
                                                pstat[1][:, 32 * tc:32 * tc + 8],
                                                c_rbias[:, :], ALU.add))
                    vs.sync()
                    v.tensor_copy(mark5[:, :], mark5[:, :]).then_inc(rv5, 1)
                    for tc in range(4):
                        lsl = ltok[:, 8 * tc:8 * tc + 8]
                        w0 = mwork[:, 80 * tc:80 * tc + 80]
                        m1 = w0[:, 0:1]; ge1 = w0[:, 8:16]; lm = w0[:, 16:24]
                        m2 = w0[:, 1:2]; ind = w0[:, 24:32]; nm1 = w0[:, 4:5]
                        vs.step(v.reduce_max(m1, lsl, axis=AX.X)); vs.sync()
                        vs.step(v.tensor_scalar(ge1, lsl, m1, None, ALU.is_ge)); vs.sync()
                        vs.step(v.scalar_tensor_tensor(lm, ge1, -1e30, lsl, ALU.mult, ALU.add)); vs.sync()
                        vs.step(v.reduce_max(m2, lm, axis=AX.X)); vs.sync()
                        vs.step(v.tensor_scalar(ind, lsl, m2, None, ALU.is_ge)); vs.sync()
                        vs.step(v.tensor_scalar(nm1, m1, -1.0, None, ALU.mult)); vs.sync()
                    v.tensor_copy(mark5[:, :], mark5[:, :]).then_inc(rv5, 1)
                    v.wait_ge(rs5, 4 * (qc + 1))
                    for tc in range(4):
                        w0 = mwork[:, 80 * tc:80 * tc + 80]
                        ind = w0[:, 24:32]; ex = w0[:, 32:40]
                        wv = w0[:, 40:48]; ssm = w0[:, 2:3]; rs = w0[:, 3:4]
                        vs.step(v.tensor_tensor(wv, ex, ind, ALU.mult)); vs.sync()
                        vs.step(v.reduce_sum(ssm, wv, axis=AX.X)); vs.sync()
                        vs.step(v.reciprocal(rs, ssm)); vs.sync()
                        vs.step(v.tensor_scalar(maskt[:, 8 * tc:8 * tc + 8], wv, rs, None,
                                                ALU.mult)); vs.sync()
                    v.tensor_copy(mark5[:, :], mark5[:, :]).then_inc(rv5, 1)
                    v.wait_ge(rt5, 9 * qc + 9)
                    for tc in range(4):
                        vs.step(v.tensor_copy(maskTb[0:8, 128 * tc:128 * tc + 128],
                                              pmb[0][0:8, 128 * tc:128 * tc + 128]))
                    vs.sync()
                    v.tensor_copy(mark5[:, :], mark5[:, :]).then_inc(rv5, 1)
                    for e in range(NE):
                        gi = qc * 8 + e
                        v.wait_ge(mb5m, qc * 8 + e + 1)
                        for j in range(4):
                            ji = gi * 4 + j
                            v.wait_ge(g5, ji + 1)
                            v.tensor_tensor(h1m[:, (e * 4 + j) * 512:(e * 4 + j) * 512 + 512],
                                            h1m[:, (e * 4 + j) * 512:(e * 4 + j) * 512 + 512],
                                            pmb[gi % 2][:, :], ALU.mult).then_inc(h5, 1)
                    for dt in range(8):
                        gj = qc * 8 + dt
                        v.wait_ge(o5m, gj + 1)
                        sl = slice(512 * dt, 512 * (dt + 1))
                        vs.step(v.tensor_tensor(x2raw[:, sl], pout[gj % 2][:, :],
                                                x1fq[:, sl], ALU.add))
                        vs.sync()
                        v.tensor_copy(mark5[:, :], mark5[:, :]).then_inc(r5, 1)
                    for dt in range(8):
                        gj = qc * 8 + dt
                        sl = slice(512 * dt, 512 * (dt + 1))
                        if gj >= 1:
                            v.wait_ge(s5m, gj)
                        vs.step(v.tensor_copy(x2rb[:, :], x2raw[:, sl]))
                        vs.sync()
                        vs.step(v.tensor_tensor(sq5[:, :], x2rb[:, :], x2rb[:, :], ALU.mult))
                        vs.sync()
                        v.tensor_copy(mark5[:, :], mark5[:, :]).then_inc(c5, 1)
                    v.wait_ge(s5m, 8 * (qc + 1))
                    vs.step(v.tensor_scalar(m1k5[:, :], pstat[0][:, :], 1.0 / D, None, ALU.mult))
                    vs.sync()
                    vs.step(v.tensor_tensor(msq5[:, :], m1k5[:, :], m1k5[:, :], ALU.mult))
                    vs.sync()
                    vs.step(v.scalar_tensor_tensor(var5[:, :], pstat[1][:, :], 1.0 / D,
                                                   msq5[:, :], ALU.mult, ALU.subtract))
                    vs.sync()
                    v.tensor_copy(mark5[:, :], mark5[:, :]).then_inc(ch5, 1)
                    v.wait_ge(sl5, qc + 1)
                    if qc >= 1:
                        v.wait_ge(tr5, 32 * qc)
                    for dt in range(8):
                        sl = slice(512 * dt, 512 * (dt + 1))
                        vs.step(v.tensor_tensor(uu5[:, :], x2raw[:, sl], m1k5[:, :], ALU.subtract))
                        vs.sync()
                        vs.step(v.tensor_tensor(outf[:, sl], uu5[:, :], rstd5[:, :], ALU.mult))
                        vs.sync()
                        v.tensor_copy(mark5[:, :], mark5[:, :]).then_inc(ap5, 1)
                    for tc in range(4):
                        ti = qc * 4 + tc
                        v.wait_ge(tr5, 32 * qc + 8 * tc + 4)
                        if ti >= 1:
                            v.wait_ge(st5, 16 * ti - 16)
                        ins = None
                        for dtj in range(4):
                            ins = v.tensor_copy(outtok[:, 1024 * tc + 128 * dtj:1024 * tc + 128 * dtj + 128],
                                                pf1[tc % 2][:, 128 * dtj:128 * dtj + 128])
                        ins.then_inc(ev5, 4)
                        v.wait_ge(tr5, 32 * qc + 8 * tc + 8)
                        ins = None
                        for dtj in range(4, 8):
                            ins = v.tensor_copy(outtok[:, 1024 * tc + 128 * dtj:1024 * tc + 128 * dtj + 128],
                                                pf1[tc % 2][:, 128 * (dtj % 4):128 * (dtj % 4) + 128])
                        ins.then_inc(ev5, 4)

            @blk.scalar
            def _(s):
                ss = SelfSem(s, ss5s)
                for qc in range(4):
                    s.wait_ge(rv5, 5 * qc + 3)
                    for tc in range(4):
                        w0 = mwork[:, 80 * tc:80 * tc + 80]
                        nm1 = w0[:, 4:5]; ex = w0[:, 32:40]
                        s.activation(ex, ltok[:, 8 * tc:8 * tc + 8], AF.Exp,
                                     bias=nm1, scale=1.0).then_inc(rs5, 1)
                    for e in range(NE):
                        gi = qc * 8 + e
                        for j in range(4):
                            ji = gi * 4 + j
                            s.wait_ge(f1m, ji + 1)
                            s.activation(h1m[:, (e * 4 + j) * 512:(e * 4 + j) * 512 + 512],
                                         pf1[ji % 2][:, :], AF.Gelu,
                                         bias=c_b1[:, 4 * e + j:4 * e + j + 1],
                                         scale=1.0).then_inc(g5, 1)
                    s.wait_ge(ch5, qc + 1)
                    ss.step(s.activation(lv5[:, :], var5[:, :], AF.Ln,
                                         bias=c_eps[:, 0:1], scale=1.0))
                    ss.sync()
                    s.activation(rstd5[:, :], lv5[:, :], AF.Exp, bias=0.0,
                                 scale=-0.5).then_inc(sl5, 1)

        sems.close()

    nc.all_engine_barrier()
    es2.close()
    es0.close()
    return nc



def _prep_shared(inputs):
    w_qkv = np.ascontiguousarray(
        inputs["L0_attn_w_qkv"].reshape(8, 128, 3 * D).transpose(1, 0, 2))
    w_proj = np.ascontiguousarray(
        inputs["L0_attn_w_proj"].reshape(8, 128, D).transpose(1, 0, 2))
    w1r = np.ascontiguousarray(
        inputs["L0_ffn_W1"].reshape(NE, 8, 128, DF).transpose(0, 2, 1, 3))
    w2r = np.ascontiguousarray(
        inputs["L0_ffn_W2"].reshape(NE, 4, 128, 8, 128).transpose(3, 2, 0, 1, 4))
    b1c = np.ascontiguousarray(
        inputs["L0_ffn_b1"].reshape(NE, 4, 128).transpose(2, 0, 1).reshape(128, NE * 4))
    rw = np.ascontiguousarray(
        inputs["L0_ffn_router_w"].reshape(8, 128, NE).transpose(1, 0, 2))
    rbias = np.ascontiguousarray(
        np.broadcast_to(inputs["L0_ffn_router_b"][None, :], (128, NE)))
    qg2 = np.ascontiguousarray(np.tile(inputs["L0_attn_q_gamma"], 2))[:, None]
    qb2 = np.ascontiguousarray(np.tile(inputs["L0_attn_q_beta"], 2))[:, None]
    kg2 = np.ascontiguousarray(np.tile(inputs["L0_attn_k_gamma"], 2))[:, None]
    kb2 = np.ascontiguousarray(np.tile(inputs["L0_attn_k_beta"], 2))[:, None]
    emats = np.zeros((8, NE * 128), np.float32)
    for e in range(NE):
        emats[e, e * 128:(e + 1) * 128] = 1.0
    zsel = np.zeros((16, 8 * 128), np.float32)
    for hp in range(8):
        zsel[2 * hp, 128 * hp:128 * hp + 64] = 1.0
        zsel[2 * hp + 1, 128 * hp + 64:128 * hp + 128] = 1.0
    shared = {
        "w_qkv": w_qkv, "w_proj": w_proj, "w1r": w1r, "w2r": w2r,
        "b1c": b1c, "b2": np.ascontiguousarray(inputs["L0_ffn_b2"]),
        "rw": rw, "rbias": rbias,
        "qg2": qg2, "qb2": qb2, "kg2": kg2, "kb2": kb2,
        "idf": np.eye(128, dtype=np.float32),
        "idb": np.eye(128, dtype=np.float32).astype(BF),
        "hones": np.kron(np.eye(2, dtype=np.float32), np.ones((64, 64), np.float32)).astype(BF),
        "aones": np.ones((128, 128), np.float32).astype(BF),
        "onesrow": np.ones((1, 128), np.float32).astype(BF),
        "onescolf": np.ones((1, 128), np.float32),
        "emats": emats.astype(BF),
        "zsel": zsel.astype(BF),
        "epscol": np.full((128, 1), EPS, np.float32),
    }
    return shared


def make_in_maps(inputs):
    shared = _prep_shared(inputs)
    x = inputs["L0_attn_x"]
    in_maps = []
    for core in range(8):
        b, hf = core // 2, core % 2
        own = x[b, hf * NQ:(hf + 1) * NQ]
        other = x[b, (1 - hf) * NQ:(2 - hf) * NQ]
        x_roll = np.ascontiguousarray(np.concatenate([own, other], axis=0))
        m = dict(shared)
        m["x_seq"] = x_roll
        in_maps.append(m)
    return in_maps


def kernel(**inputs):
    in_maps = make_in_maps(inputs)
    nc = build_nc()
    res = run_bass_kernel_spmd(nc, in_maps, core_ids=list(range(8)))
    outs = [np.asarray(r["out"]) for r in res.results]
    full = np.empty((4, NS, D), np.float32)
    for core in range(8):
        b, hf = core // 2, core % 2
        full[b, hf * NQ:(hf + 1) * NQ] = outs[core]
    return full


# version 15
# speedup vs baseline: 1.0755x; 1.0755x over previous
import os
import numpy as np
import ml_dtypes
from contextlib import ExitStack

import concourse.bass as bass
import concourse.mybir as mybir
from concourse.bass_utils import run_bass_kernel_spmd

F32 = mybir.dt.float32
BF16 = mybir.dt.bfloat16
AF = mybir.ActivationFunctionType
ALU = mybir.AluOpType
AX = mybir.AxisListType
BF = ml_dtypes.bfloat16

D = 1024
H = 16
DH = 64
NQ = 2048
NS = 4096
NE = 8
DF = 512
EPS = 1e-5

DEBUG = int(os.environ.get("KMK_DEBUG", "0"))
LASTPH = os.environ.get("KMK_LAST", "p5")


class SelfSem:
    def __init__(self, eng, sem):
        self.eng, self.sem, self.n = eng, sem, 0

    def step(self, ins):
        ins.then_inc(self.sem, 1)
        self.n += 1
        return ins

    def sync(self):
        self.eng.wait_ge(self.sem, self.n)


def build_nc():
    nc = bass.Bass("TRN2", target_bir_lowering=False,
                   detect_race_conditions=bool(int(os.environ.get("KMK_RACE", "0"))))

    x_seq = nc.dram_tensor("x_seq", [NS, D], F32, kind="ExternalInput")
    w_qkv = nc.dram_tensor("w_qkv", [128, 8, 3 * D], F32, kind="ExternalInput")
    w_proj = nc.dram_tensor("w_proj", [128, 8, D], F32, kind="ExternalInput")
    w1r = nc.dram_tensor("w1r", [NE, 128, 8, DF], F32, kind="ExternalInput")
    w2r = nc.dram_tensor("w2r", [8, 128, NE, 4, 128], F32, kind="ExternalInput")
    b1c = nc.dram_tensor("b1c", [128, NE * 4], F32, kind="ExternalInput")
    b2 = nc.dram_tensor("b2", [NE, D], F32, kind="ExternalInput")
    rw = nc.dram_tensor("rw", [128, 8, NE], F32, kind="ExternalInput")
    rbias = nc.dram_tensor("rbias", [128, NE], F32, kind="ExternalInput")
    qg2 = nc.dram_tensor("qg2", [128, 1], F32, kind="ExternalInput")
    qb2 = nc.dram_tensor("qb2", [128, 1], F32, kind="ExternalInput")
    kg2 = nc.dram_tensor("kg2", [128, 1], F32, kind="ExternalInput")
    kb2 = nc.dram_tensor("kb2", [128, 1], F32, kind="ExternalInput")
    idf = nc.dram_tensor("idf", [128, 128], F32, kind="ExternalInput")
    idb = nc.dram_tensor("idb", [128, 128], BF16, kind="ExternalInput")
    hones = nc.dram_tensor("hones", [128, 128], BF16, kind="ExternalInput")
    aones = nc.dram_tensor("aones", [128, 128], BF16, kind="ExternalInput")
    onesrow = nc.dram_tensor("onesrow", [1, 128], BF16, kind="ExternalInput")
    onescolf = nc.dram_tensor("onescolf", [1, 128], F32, kind="ExternalInput")
    emats = nc.dram_tensor("emats", [8, NE * 128], BF16, kind="ExternalInput")
    zsel = nc.dram_tensor("zsel", [16, 8 * 128], BF16, kind="ExternalInput")
    epscol = nc.dram_tensor("epscol", [128, 1], F32, kind="ExternalInput")

    out = nc.dram_tensor("out", [NQ, D], F32, kind="ExternalOutput")
    attn_scr = nc.dram_tensor("attn_scr", [D, NQ], BF16, kind="Internal")
    x1_scr = nc.dram_tensor("x1_scr", [D, NQ], F32, kind="Internal")
    w1s = nc.dram_tensor("w1s", [NE, 128, 8 * DF], BF16, kind="Internal")
    w2s = nc.dram_tensor("w2s", [8, 128, 32 * 128], BF16, kind="Internal")

    dbg = {}
    def dbg_t(name, shape, dt):
        if DEBUG:
            dbg[name] = nc.dram_tensor("dbg_" + name, shape, dt, kind="ExternalOutput")
    dbg_t("qTb", [128, 8 * NQ], BF16)
    dbg_t("kTb", [128, 8 * NS], BF16)
    dbg_t("vt", [128, 32 * H * 65], BF16)
    dbg_t("cb", [128, H], F32)
    dbg_t("attn", [D, NQ], BF16)
    dbg_t("x1", [D, NQ], F32)
    dbg_t("mask", [NQ, NE], F32)

    es0 = ExitStack()
    sb0 = lambda n, s, d=F32: es0.enter_context(nc.sbuf_tensor(n, s, d))
    wcl = es0.enter_context(nc.semaphore("wcl"))
    wcs = es0.enter_context(nc.semaphore("wcs"))
    wcb = sb0("wcb", [128, 2048], BF16)

    c_idf = sb0("c_idf", [128, 128])
    c_idb = sb0("c_idb", [128, 128], BF16)
    c_hones = sb0("c_hones", [128, 128], BF16)
    c_aones = sb0("c_aones", [128, 128], BF16)
    c_onesrow = sb0("c_onesrow", [1, 128], BF16)
    c_onescolf = sb0("c_onescolf", [1, 128])
    c_emats = sb0("c_emats", [8, NE * 128], BF16)
    c_qg2 = sb0("c_qg2", [128, 1]); c_qb2 = sb0("c_qb2", [128, 1])
    c_kg2 = sb0("c_kg2", [128, 1]); c_kb2 = sb0("c_kb2", [128, 1])
    c_eps = sb0("c_eps", [128, 1])
    c_rbias = sb0("c_rbias", [128, NE])
    c_cb = sb0("c_cb", [128, H])
    c_zsel = sb0("c_zsel", [16, 8 * 128], BF16)
    zrball = sb0("zrball", [16, NQ], BF16)

    with nc.semaphore("p0s") as p0s, nc.Block() as blk:
        @blk.sync
        def _(y):
            n = 0
            for dst, src in [
                (c_idf, idf), (c_idb, idb), (c_hones, hones), (c_aones, aones),
                (c_onesrow, onesrow), (c_onescolf, onescolf), (c_emats, emats),
                (c_qg2, qg2), (c_qb2, qb2), (c_kg2, kg2), (c_kb2, kb2),
                (c_eps, epscol), (c_rbias, rbias), (c_zsel, zsel),
            ]:
                y.dma_start(dst[:, :], src[:, :]).then_inc(p0s, 16)
                n += 16
            y.wait_ge(p0s, n)
    nc.all_engine_barrier()

    es1 = ExitStack()
    sb1 = lambda n, s, d=F32: es1.enter_context(nc.sbuf_tensor(n, s, d))
    qTb = sb1("qTb", [128, 8 * NQ], BF16)
    kTb = sb1("kTb", [128, 8 * NS], BF16)

    with ExitStack() as esp:
        sbp = lambda n, s, d=F32: esp.enter_context(nc.sbuf_tensor(n, s, d))
        psp = lambda n, s, d=F32: esp.enter_context(nc.psum_tensor(n, s, d))
        wqk = sbp("wqk", [128, 8 * 2048], BF16)
        xb = sbp("xb", [128, 2 * D], BF16)
        xT = sbp("xT", [128, 2 * 8 * 512], BF16)
        stageb = sbp("stageb", [128, 4 * 512], BF16)
        sqb = sbp("sqb", [128, 4 * 512], BF16)
        m64 = sbp("m64", [128, 4 * 512])
        msq = sbp("msq", [128, 2 * 512])
        varr = sbp("varr", [128, 2 * 512])
        lvr = sbp("lvr", [128, 2 * 512])
        rstd = sbp("rstd", [128, 4 * 512])
        uu = sbp("uu", [128, 2 * 512])
        ww = sbp("ww", [128, 2 * 512])
        mark1 = sbp("mark1", [1, 1])
        pT = psp("pT", [128, 1024], BF16)
        pT2 = psp("pT2", [128, 1024], BF16)
        pmm = [psp("pmm0", [128, 512]), psp("pmm1", [128, 512]),
               psp("pmm2", [128, 512]), psp("pmm3", [128, 512])]
        pst = [psp("pst0", [128, 512]), psp("pst1", [128, 512])]

        NCH, NG = 32, 8
        units = []
        for g in range(NG):
            for t in range(8):
                units.append((1, t, g))
            if g < 4:
                for t in range(8):
                    units.append((0, t, g))
        NU = len(units)
        ucum = [0] * (NG + 1)
        for g in range(NG):
            ucum[g + 1] = ucum[g] + (16 if g < 4 else 8)

        def unit_dst(kind, t, g):
            base = (qTb if kind == 0 else kTb)
            return base[:, t * (NQ if kind == 0 else NS) + 512 * g:
                        t * (NQ if kind == 0 else NS) + 512 * (g + 1)]

        with (
            nc.semaphore("dxg") as dxg,
            nc.semaphore("tT") as tT,
            nc.semaphore("eT") as eT,
            nc.semaphore("mmq") as mmq,
            nc.semaphore("evq") as evq,
            nc.semaphore("smq") as smq,
            nc.semaphore("dch") as dch,
            nc.semaphore("stl") as stl,
            nc.semaphore("apq") as apq,
            nc.semaphore("aplast") as aplast,
            nc.semaphore("evs") as evs,
            nc.semaphore("vself") as vself,
            nc.semaphore("sself") as sself,
            nc.semaphore("dbgs") as dbgs,
            nc.Block() as blk,
        ):
            @blk.gpsimd
            def _(g):
                g.dma_start(wqk[:, :].rearrange("p (c n) -> p c n", c=8),
                            w_qkv[:, :, 0:2048]).then_inc(dxg, 16)
                for ch in range(NCH):
                    if ch >= 2:
                        g.wait_ge(tT, 8 * (ch - 1))
                    g.dma_start(xb[:, (ch % 2) * D:(ch % 2 + 1) * D],
                                x_seq[128 * ch:128 * (ch + 1), :]).then_inc(dxg, 16)

            @blk.tensor
            def _(t):
                emitted = [0]

                def qkv_unit(u):
                    kind, tl, g = units[u]
                    if u >= 4:
                        t.wait_ge(aplast, u - 3)
                    ins = None
                    for c in range(8):
                        ins = t.matmul(
                            pmm[u % 4][:, :],
                            wqk[:, c * 2048 + kind * 1024 + 128 * tl:
                                c * 2048 + kind * 1024 + 128 * tl + 128],
                            xT[:, ((g % 2) * 8 + c) * 512:((g % 2) * 8 + c) * 512 + 512],
                            start=(c == 0), stop=(c == 7))
                    ins.then_inc(mmq, 1)

                def stat_unit(u):
                    t.wait_ge(evq, u + 1)
                    if u >= 1:
                        t.wait_ge(dch, u)
                    par = u % 4
                    t.matmul(pst[0][:, :], c_hones[:, :], stageb[:, 512 * par:512 * (par + 1)],
                             start=True, stop=True)
                    t.matmul(pst[1][:, :], c_hones[:, :], sqb[:, 512 * par:512 * (par + 1)],
                             start=True, stop=True).then_inc(smq, 1)

                u = 0
                for g in range(NG):
                    for ch in range(4 * g, 4 * g + 4):
                        t.wait_ge(dxg, max(48, 16 * (ch + 2)))
                        if ch >= 2:
                            t.wait_ge(eT, 8 * (ch - 1))
                        pt = pT if ch % 2 == 0 else pT2
                        ins = None
                        for j in range(8):
                            ins = t.transpose(pt[:, 128 * j:128 * (j + 1)],
                                              xb[:, (ch % 2) * D + 128 * j:(ch % 2) * D + 128 * (j + 1)],
                                              c_idb[:, :])
                        ins.then_inc(tT, 8)
                    t.wait_ge(eT, 8 * (4 * g + 4))
                    for _i in range(ucum[g + 1] - ucum[g]):
                        qkv_unit(u)
                        if u > ucum[g]:
                            stat_unit(u - 1)
                        u += 1
                    stat_unit(u - 1)

            @blk.scalar
            def _(s):
                ss = SelfSem(s, sself)
                for g in range(NG):
                    for ch in range(4 * g, 4 * g + 4):
                        s.wait_ge(tT, 8 * (ch + 1))
                        if g >= 2:
                            s.wait_ge(mmq, ucum[g - 1])
                        pt = pT if ch % 2 == 0 else pT2
                        ins = None
                        for j in range(8):
                            ins = s.copy(
                                xT[:, ((g % 2) * 8 + j) * 512 + 128 * (ch % 4):
                                   ((g % 2) * 8 + j) * 512 + 128 * (ch % 4) + 128],
                                pt[:, 128 * j:128 * (j + 1)])
                        ins.then_inc(eT, 8)
                    def sevac(u):
                        p4e = u % 4
                        s.wait_ge(mmq, u + 1)
                        if u >= 4:
                            s.wait_ge(smq, u - 3)
                        s.copy(stageb[:, 512 * p4e:512 * (p4e + 1)],
                               pmm[p4e][:, :]).then_inc(evs, 1)

                    def schain(u):
                        p4_ = u % 4
                        p2_ = u % 2
                        s.wait_ge(dch, u + 1)
                        ins = s.activation(lvr[:, 512 * p2_:512 * (p2_ + 1)],
                                           varr[:, 512 * p2_:512 * (p2_ + 1)],
                                           AF.Ln, bias=c_eps[:, 0:1], scale=1.0)
                        ss.step(ins)
                        ss.sync()
                        s.activation(rstd[:, 512 * p4_:512 * (p4_ + 1)],
                                     lvr[:, 512 * p2_:512 * (p2_ + 1)],
                                     AF.Exp, bias=0.0, scale=-0.5).then_inc(stl, 1)

                    st_, en_ = ucum[g], ucum[g + 1]
                    for u in range(st_, en_):
                        sevac(u)
                        if u >= st_ + 2:
                            schain(u - 2)
                    schain(en_ - 2)
                    schain(en_ - 1)

            @blk.vector
            def _(v):
                vs = SelfSem(v, vself)
                v.memset(mark1[:, :], 0.0)

                def ph1(u):
                    p4_ = u % 4
                    s4 = slice(512 * p4_, 512 * (p4_ + 1))
                    v.wait_ge(evs, u + 1)
                    v.tensor_tensor(sqb[:, s4], stageb[:, s4], stageb[:, s4],
                                    ALU.mult).then_inc(evq, 1)

                def ph2(u):
                    p4_ = u % 4
                    p2_ = u % 2
                    s4 = slice(512 * p4_, 512 * (p4_ + 1))
                    s2 = slice(512 * p2_, 512 * (p2_ + 1))
                    v.wait_ge(smq, u + 1)
                    vs.step(v.tensor_scalar(m64[:, s4], pst[0][:, :], 1.0 / DH, None, ALU.mult))
                    vs.sync()
                    vs.step(v.tensor_tensor(msq[:, s2], m64[:, s4], m64[:, s4], ALU.mult))
                    vs.sync()
                    v.scalar_tensor_tensor(varr[:, s2], pst[1][:, :], 1.0 / DH, msq[:, s2],
                                           ALU.mult, ALU.subtract).then_inc(dch, 1)
                    vs.step(v.tensor_tensor(uu[:, s2], pmm[p4_][:, :], m64[:, s4], ALU.subtract))
                    vs.sync()
                    v.tensor_copy(mark1[:, :], mark1[:, :]).then_inc(aplast, 1)

                def ph3(u):
                    kind, tl, g = units[u]
                    p4_ = u % 4
                    p2_ = u % 2
                    s4 = slice(512 * p4_, 512 * (p4_ + 1))
                    s2 = slice(512 * p2_, 512 * (p2_ + 1))
                    v.wait_ge(stl, u + 1)
                    vs.step(v.tensor_tensor(ww[:, s2], uu[:, s2], rstd[:, s4], ALU.mult))
                    vs.sync()
                    gcol = c_qg2 if kind == 0 else c_kg2
                    bcol = c_qb2 if kind == 0 else c_kb2
                    v.tensor_scalar(unit_dst(kind, tl, g), ww[:, s2],
                                    gcol[:, 0:1], bcol[:, 0:1],
                                    ALU.mult, ALU.add).then_inc(apq, 1)

                for g in range(NG):
                    st_, en_ = ucum[g], ucum[g + 1]
                    for u in range(st_, en_):
                        ph1(u)
                        if u >= st_ + 1:
                            ph2(u - 1)
                        if u >= st_ + 2:
                            ph3(u - 2)
                    ph2(en_ - 1)
                    ph3(en_ - 2)
                    ph3(en_ - 1)

            @blk.sync
            def _(y):
                y.wait_ge(apq, NU)
                if DEBUG:
                    y.dma_start(dbg["qTb"][:, :], qTb[:, :]).then_inc(dbgs, 16)
                    y.dma_start(dbg["kTb"][:, :], kTb[:, :]).then_inc(dbgs, 16)
                    y.wait_ge(dbgs, 32)

    nc.all_engine_barrier()

    es1b = ExitStack()
    vt = es1b.enter_context(nc.sbuf_tensor("vt", [128, 32 * H * 65], BF16))

    with ExitStack() as esp:
        sbp = lambda n, s, d=F32: esp.enter_context(nc.sbuf_tensor(n, s, d))
        psp = lambda n, s, d=F32: esp.enter_context(nc.psum_tensor(n, s, d))
        wv = sbp("wv", [128, 8 * 1024], BF16)
        xb = sbp("xbv", [128, 2 * D], BF16)
        xTc = sbp("xTc", [128, 2 * 8 * 128], BF16)
        pT = psp("pTv", [128, 1024], BF16)
        pT2 = psp("pTv2", [128, 1024], BF16)
        pv = [psp("pv0", [128, 512]), psp("pv1", [128, 512]),
              psp("pv2", [128, 512]), psp("pv3", [128, 512])]

        with (
            nc.semaphore("dxg2") as dxg,
            nc.semaphore("tT2") as tT,
            nc.semaphore("eT2") as eT,
            nc.semaphore("mmv") as mmv,
            nc.semaphore("evv") as evv,
            nc.semaphore("ones") as onesem,
            nc.semaphore("dbgs2") as dbgs2,
            nc.Block() as blk,
        ):
            @blk.vector
            def _(v):
                v.memset(vt[:, :].rearrange("p (kc h d) -> p kc h d", kc=32, h=H)[:, :, :, 64:65],
                         1.0).then_inc(onesem, 1)

            @blk.gpsimd
            def _(g):
                g.dma_start(wv[:, :].rearrange("p (c n) -> p c n", c=8),
                            w_qkv[:, :, 2048:3072]).then_inc(dxg, 16)
                for ch in range(32):
                    if ch >= 2:
                        g.wait_ge(tT, 8 * (ch - 1))
                    g.dma_start(xb[:, (ch % 2) * D:(ch % 2 + 1) * D],
                                x_seq[128 * ch:128 * (ch + 1), :]).then_inc(dxg, 16)

            @blk.tensor
            def _(t):
                for ch in range(32):
                    par = ch % 2
                    t.wait_ge(dxg, max(48, 16 * (ch + 2)))
                    if ch >= 2:
                        t.wait_ge(eT, 8 * (ch - 1))
                    pt = pT if par == 0 else pT2
                    ins = None
                    for j in range(8):
                        ins = t.transpose(pt[:, 128 * j:128 * (j + 1)],
                                          xb[:, par * D + 128 * j:par * D + 128 * (j + 1)],
                                          c_idb[:, :])
                    ins.then_inc(tT, 8)
                    t.wait_ge(eT, 8 * (ch + 1))
                    if ch >= 2:
                        t.wait_ge(evv, 2 * (ch - 1))
                    for half in range(2):
                        ins = None
                        for c in range(8):
                            ins = t.matmul(
                                pv[2 * par + half][:, :],
                                xTc[:, (par * 8 + c) * 128:(par * 8 + c) * 128 + 128],
                                wv[:, c * 1024 + 512 * half:c * 1024 + 512 * half + 512],
                                start=(c == 0), stop=(c == 7))
                        ins.then_inc(mmv, 1)

            @blk.scalar
            def _(s):
                for ch in range(32):
                    par = ch % 2
                    s.wait_ge(tT, 8 * (ch + 1))
                    if ch >= 2:
                        s.wait_ge(mmv, 2 * (ch - 1))
                    pt = pT if par == 0 else pT2
                    ins = None
                    for j in range(8):
                        ins = s.copy(xTc[:, (par * 8 + j) * 128:(par * 8 + j) * 128 + 128],
                                     pt[:, 128 * j:128 * (j + 1)])
                    ins.then_inc(eT, 8)

            @blk.vector
            def _(v):
                v.wait_ge(onesem, 1)
                for ch in range(32):
                    par = ch % 2
                    v.wait_ge(mmv, 2 * (ch + 1))
                    dst = vt[:, :].rearrange("p (kc h d) -> p kc h d", kc=32, h=H)[:, ch, :, 0:64]
                    ins = v.tensor_copy(
                        dst[:, 0:8, :],
                        pv[2 * par][:, :].rearrange("p (h d) -> p h d", h=8))
                    ins = v.tensor_copy(
                        dst[:, 8:16, :],
                        pv[2 * par + 1][:, :].rearrange("p (h d) -> p h d", h=8))
                    ins.then_inc(evv, 2)

            @blk.sync
            def _(y):
                y.wait_ge(evv, 64)
                if DEBUG:
                    y.dma_start(dbg["vt"][:, :], vt[:, :]).then_inc(dbgs2, 16)
                    y.wait_ge(dbgs2, 16)

    nc.all_engine_barrier()

    if LASTPH == "p1":
        es1b.close(); es1.close(); es0.close()
        return nc

    with ExitStack() as esp:
        sbp = lambda n, s, d=F32: esp.enter_context(nc.sbuf_tensor(n, s, d))
        psp = lambda n, s, d=F32: esp.enter_context(nc.psum_tensor(n, s, d))
        sqt = sbp("sqt", [128, NS], BF16)
        nmx = sbp("nmx", [128, 8])
        qnq = sbp("qnq", [128, 8])
        qnk = sbp("qnk", [128, 8])
        prod = sbp("prod", [128, 8])
        hsel = sbp("hsel", [128, 256])
        mark2 = sbp("mark2", [1, 1])
        pns = [psp("pns0", [128, 512]), psp("pns1", [128, 512])]
        pcb = psp("pcb", [128, 2])

        with (
            nc.semaphore("vsq2") as vsq,
            nc.semaphore("tsm2") as tsm,
            nc.semaphore("vrd2") as vrdc,
            nc.semaphore("sch2") as sch,
            nc.semaphore("tbc2") as tbc,
            nc.semaphore("vcb2") as vcb,
            nc.semaphore("vself2") as vself2,
            nc.semaphore("sself2") as sself2,
            nc.Block() as blk,
        ):
            jobs = []
            for t in range(8):
                jobs.append((0, t, 4))
            for t in range(8):
                jobs.append((1, t, 8))

            @blk.vector
            def _(v):
                vs = SelfSem(v, vself2)
                v.memset(mark2[:, :], 0.0)
                v.memset(hsel[:, :], 0.0)
                vs.step(v.memset(hsel[0:64, 0:128], 1.0 / 64))
                vs.step(v.memset(hsel[64:128, 128:256], 1.0 / 64))
                vs.sync()
                v.tensor_copy(mark2[:, :], mark2[:, :]).then_inc(vcb, 1)
                nred = 0
                for j, (kind, t, nch) in enumerate(jobs):
                    srcb = qTb if kind == 0 else kTb
                    width = NQ if kind == 0 else NS
                    if j >= 1:
                        v.wait_ge(tsm, nred)
                    vs.step(v.tensor_tensor(sqt[:, 0:width],
                                            srcb[:, t * width:(t + 1) * width],
                                            srcb[:, t * width:(t + 1) * width],
                                            ALU.mult))
                    vs.sync()
                    v.tensor_copy(mark2[:, :], mark2[:, :]).then_inc(vsq, 1)
                    for ch in range(nch):
                        v.wait_ge(tsm, nred + ch + 1)
                        v.reduce_max(nmx[:, ch:ch + 1], pns[ch % 2][:, :],
                                     axis=AX.X).then_inc(vrdc, 1)
                    nred += nch
                    v.wait_ge(vrdc, nred)
                    dstcol = (qnq if kind == 0 else qnk)[:, t:t + 1]
                    vs.step(v.reduce_max(dstcol, nmx[:, 0:nch], axis=AX.X))
                    vs.sync()
                    if kind == 1:
                        vs.step(v.tensor_tensor(prod[:, t:t + 1], qnq[:, t:t + 1],
                                                qnk[:, t:t + 1], ALU.mult))
                        vs.sync()
                        v.tensor_copy(mark2[:, :], mark2[:, :]).then_inc(sch, 1)
                        v.wait_ge(tbc, t + 1)
                        vs.step(v.tensor_scalar(c_cb[:, 2 * t:2 * t + 1], pcb[:, 0:1],
                                                -1.0, None, ALU.mult))
                        vs.step(v.tensor_scalar(c_cb[:, 2 * t + 1:2 * t + 2], pcb[:, 1:2],
                                                -1.0, None, ALU.mult))
                        vs.sync()
                        v.tensor_copy(mark2[:, :], mark2[:, :]).then_inc(vcb, 1)

            @blk.tensor
            def _(t_):
                t_.wait_ge(vcb, 1)
                nred = 0
                for j, (kind, t, nch) in enumerate(jobs):
                    t_.wait_ge(vsq, j + 1)
                    for ch in range(nch):
                        if nred + ch >= 2:
                            t_.wait_ge(vrdc, nred + ch - 1)
                        t_.matmul(pns[ch % 2][:, :], c_hones[:, :],
                                  sqt[:, 512 * ch:512 * (ch + 1)],
                                  start=True, stop=True).then_inc(tsm, 1)
                    nred += nch
                    if kind == 1:
                        t_.wait_ge(sch, t + 1)
                        if t >= 1:
                            t_.wait_ge(vcb, 1 + t)
                        t_.matmul(pcb[:, 0:1], hsel[:, 0:128], prod[:, t:t + 1],
                                  start=True, stop=True)
                        t_.matmul(pcb[:, 1:2], hsel[:, 128:256], prod[:, t:t + 1],
                                  start=True, stop=True).then_inc(tbc, 1)

    nc.all_engine_barrier()
    with ExitStack() as esp:
        sbp = lambda n, s, d=F32: esp.enter_context(nc.sbuf_tensor(n, s, d))
        tmpa = sbp("tmpa", [128, H])
        with (
            nc.semaphore("fx1") as fx1,
            nc.semaphore("fx2") as fx2,
            nc.Block() as blk,
        ):
            @blk.vector
            def _(v):
                v.tensor_scalar(tmpa[:, :], c_cb[:, :], -1.0, None,
                                ALU.mult).then_inc(fx1, 1)
                v.wait_ge(fx2, 2)
                v.tensor_scalar(c_cb[:, :], tmpa[:, :], -0.125, None,
                                ALU.mult).then_inc(fx1, 1)

            @blk.scalar
            def _(s):
                s.wait_ge(fx1, 1)
                s.activation(tmpa[:, :], tmpa[:, :], AF.Ln, bias=0.0,
                             scale=1.0).then_inc(fx2, 1)
                s.wait_ge(fx2, 1)
                s.activation(tmpa[:, :], tmpa[:, :], AF.Exp, bias=0.0,
                             scale=0.5).then_inc(fx2, 1)

            @blk.sync
            def _(y):
                if DEBUG:
                    y.wait_ge(fx1, 2)
                    with nc.semaphore("dcb") as dcb:
                        y.dma_start(dbg["cb"][:, :], c_cb[:, :]).then_inc(dcb, 16)
                        y.wait_ge(dcb, 16)

    nc.all_engine_barrier()

    if LASTPH == "p2":
        es1b.close(); es1.close(); es0.close()
        return nc

    with ExitStack() as esp:
        sbp = lambda n, s, d=F32: esp.enter_context(nc.sbuf_tensor(n, s, d))
        psp = lambda n, s, d=F32: esp.enter_context(nc.psum_tensor(n, s, d))
        probs0 = sbp("probs0", [128, 2048], BF16)
        probs1 = sbp("probs1", [128, 2048], BF16)
        att0 = sbp("att0", [128, 1024], BF16)
        att1 = sbp("att1", [128, 1024], BF16)
        zpk = sbp("zpk", [128, 16], BF16)
        zpkf = sbp("zpkf", [128, 16])
        zpki = sbp("zpki", [128, 16])
        psc = [psp("psc0", [128, 1024]), psp("psc1", [128, 1024])]
        pE0 = psp("pE0", [128, 1024])
        pE1 = psp("pE1", [128, 1024])

        with (
            nc.semaphore("scm") as scm,
            nc.semaphore("exq") as exq,
            nc.semaphore("avm") as avm,
            nc.semaphore("rev") as rev,
            nc.semaphore("zpd") as zpd,
            nc.semaphore("zrc") as zrc,
            nc.semaphore("zud") as zud,
            nc.semaphore("std") as std,
            nc.semaphore("vself3") as vself3,
            nc.Block() as blk,
        ):
            NIT = 16
            NC_ = 32

            def it_hp(it):
                return it // 2, it % 2


            @blk.tensor
            def _(t_):
                def emit_attv(k):
                    it, c = k // NC_, k % NC_
                    hp, qc = it_hp(it)
                    for i in range(2):
                        t_.wait_ge(exq, 2 * k + i + 1)
                        if c == 0:
                            t_.wait_ge(rev, 2 * it)
                        pe = pE0 if i == 0 else pE1
                        pr = probs0 if i == 0 else probs1
                        ins = None
                        for n in range(2):
                            ins = t_.matmul(
                                pe[0:65, 512 * n:512 * (n + 1)],
                                vt[:, (c * H + 2 * hp + i) * 65:
                                   (c * H + 2 * hp + i) * 65 + 65],
                                pr[:, (c % 2) * 1024 + 512 * n:(c % 2) * 1024 + 512 * n + 512],
                                start=(c == 0), stop=(c == NC_ - 1))
                        ins.then_inc(avm, 1)

                for k in range(NIT * NC_):
                    it, c = k // NC_, k % NC_
                    hp, qc = it_hp(it)
                    for i in range(2):
                        if 2 * k + i >= 2:
                            t_.wait_ge(exq, 2 * k + i - 1)
                        ins = None
                        for n in range(2):
                            ins = t_.matmul(
                                psc[i][:, 512 * n:512 * (n + 1)],
                                kTb[64 * i:64 * i + 64,
                                    hp * NS + 128 * c:hp * NS + 128 * c + 128],
                                qTb[64 * i:64 * i + 64,
                                    hp * NQ + 1024 * qc + 512 * n:
                                    hp * NQ + 1024 * qc + 512 * n + 512],
                                start=True, stop=True,
                                tile_position=(64 * i, 0))
                        ins.then_inc(scm, 1)
                    if k >= 1:
                        emit_attv(k - 1)
                emit_attv(NIT * NC_ - 1)

            @blk.scalar
            def _(s):
                for it in range(NIT):
                    hp, qc = it_hp(it)
                    for c in range(NC_):
                        k = it * NC_ + c
                        for i in range(2):
                            h = 2 * hp + i
                            pr = probs0 if i == 0 else probs1
                            s.wait_ge(scm, 2 * k + i + 1)
                            if k >= 2:
                                s.wait_ge(avm, 2 * (k - 2) + i + 1)
                            s.activation(pr[:, (c % 2) * 1024:(c % 2) * 1024 + 1024],
                                         psc[i][:, :], AF.Exp,
                                         bias=c_cb[:, h:h + 1],
                                         scale=0.125).then_inc(exq, 1)

            @blk.vector
            def _(v):
                vs = SelfSem(v, vself3)
                for it in range(NIT):
                    hp, qc = it_hp(it)
                    v.wait_ge(avm, 64 * (it + 1))
                    if it >= 1:
                        v.wait_ge(std, 32 * it)
                        v.wait_ge(zpd, 32 * it)
                    v.tensor_copy(att0[0:65, :], pE0[0:65, :])
                    v.tensor_copy(att1[0:65, :], pE1[0:65, :]).then_inc(rev, 2)
                    v.wait_ge(zpd, 32 * it + 32)
                    vs.step(v.tensor_copy(zpkf[:, :], zpk[:, :]))
                    vs.sync()
                    v.reciprocal(zpki[:, :], zpkf[:, :]).then_inc(zrc, 1)

            @blk.gpsimd
            def _(g):
                for it in range(NIT):
                    g.wait_ge(rev, 2 * it + 2)
                    g.dma_start(zpk[:, 0:8], att0[64:65, :]).then_inc(zpd, 16)
                    g.dma_start(zpk[:, 8:16], att1[64:65, :]).then_inc(zpd, 16)
                    hp, qc = it_hp(it)
                    g.wait_ge(zrc, it + 1)
                    g.dma_start(zrball[2 * hp:2 * hp + 1, 1024 * qc:1024 * (qc + 1)],
                                zpki[:, 0:8]).then_inc(zud, 16)
                    g.dma_start(zrball[2 * hp + 1:2 * hp + 2, 1024 * qc:1024 * (qc + 1)],
                                zpki[:, 8:16]).then_inc(zud, 16)
                    for k in (2 * it, 2 * it + 1):
                        if k >= 1:
                            g.wait_ge(wcs, 16 * k)
                        if k < 16:
                            src_ = w1r[k // 2, :, :, :].rearrange("p c n -> p (c n)")
                        else:
                            src_ = w2r[(k - 16) // 2, :, :, :, :].rearrange("p e c n -> p (e c n)")
                        g.dma_start(wcb[:, :],
                                    src_[:, 2048 * (k & 1):2048 * (k & 1) + 2048]).then_inc(wcl, 16)

            @blk.sync
            def _(y):
                for it in range(NIT):
                    hp, qc = it_hp(it)
                    y.wait_ge(rev, 2 * it + 2)
                    y.dma_start(attn_scr[128 * hp:128 * hp + 64, 1024 * qc:1024 * (qc + 1)],
                                att0[0:64, :]).then_inc(std, 16)
                    y.dma_start(attn_scr[128 * hp + 64:128 * hp + 128, 1024 * qc:1024 * (qc + 1)],
                                att1[0:64, :]).then_inc(std, 16)
                    for k in (2 * it, 2 * it + 1):
                        y.wait_ge(wcl, 16 * (k + 1))
                        if k < 16:
                            dst_ = w1s[k // 2, :, 2048 * (k & 1):2048 * (k & 1) + 2048]
                        else:
                            dst_ = w2s[(k - 16) // 2, :, 2048 * (k & 1):2048 * (k & 1) + 2048]
                        y.dma_start(dst_, wcb[:, :]).then_inc(wcs, 16)
                y.wait_ge(std, 32 * NIT)
                y.wait_ge(zud, 32 * NIT)
                y.wait_ge(wcs, 512)
                if DEBUG:
                    with nc.semaphore("datn") as datn:
                        y.dma_start(dbg["attn"][:, :], attn_scr[:, :]).then_inc(datn, 16)
                        y.wait_ge(datn, 16)

    nc.all_engine_barrier()

    if LASTPH == "p3":
        es1b.close(); es1.close(); es0.close()
        return nc

    es1b.close()
    es1.close()

    es2 = ExitStack()
    x1b = es2.enter_context(nc.sbuf_tensor("x1b", [128, 8 * NQ], BF16))

    with ExitStack() as esp:
        sbp = lambda n, s, d=F32: esp.enter_context(nc.sbuf_tensor(n, s, d))
        psp = lambda n, s, d=F32: esp.enter_context(nc.psum_tensor(n, s, d))
        wpb = sbp("wpb", [128, 8 * 1024], BF16)
        attnb = sbp("attnb", [128, 2 * 8 * 512], BF16)
        xb2 = sbp("xb2", [128, 2 * D])
        xoT = sbp("xoT", [128, 8 * 512])
        x1raw = sbp("x1raw", [128, 8 * 512])
        x1rb = sbp("x1rb", [128, 2 * 512], BF16)
        sq4 = sbp("sq4", [128, 2 * 512], BF16)
        m1k = sbp("m1k", [128, 512])
        msq4 = sbp("msq4", [128, 512])
        var4 = sbp("var4", [128, 512])
        lv4 = sbp("lv4", [128, 512])
        rstd4 = sbp("rstd4", [128, 512])
        uu4 = sbp("uu4", [128, 512])
        x1f = sbp("x1f", [128, 8 * 512])
        mark4 = sbp("mark4", [1, 1])
        attnd = sbp("attnd", [128, 2 * 8 * 512], BF16)
        pTx = [psp("pTx0", [128, 512]), psp("pTx1", [128, 512])]
        pp = [psp("pp0", [128, 512]), psp("pp1", [128, 512])]
        pls = [psp("pls0", [128, 512]), psp("pls1", [128, 512])]
        pzb = [psp("pzb0", [128, 512]), psp("pzb1", [128, 512])]

        with (
            nc.semaphore("d4a") as d4a,
            nc.semaphore("d4g") as d4g,
            nc.semaphore("t4t") as t4t,
            nc.semaphore("e4") as e4,
            nc.semaphore("p4m") as p4m,
            nc.semaphore("r4") as r4,
            nc.semaphore("c4") as c4,
            nc.semaphore("s4m") as s4m,
            nc.semaphore("ch4") as ch4,
            nc.semaphore("sl4") as sl4,
            nc.semaphore("ap4") as ap4,
            nc.semaphore("st4") as st4,
            nc.semaphore("vs4") as vs4s,
            nc.semaphore("ss4") as ss4s,
            nc.semaphore("dv4") as dv4,
            nc.semaphore("zb4") as zb4,
            nc.semaphore("c4v") as c4v,
            nc.semaphore("sx1") as sx1,
            nc.Block() as blk,
        ):
            attn_v = attn_scr[:, :].rearrange("(f p) n -> p f n", p=128)
            x1_v = x1_scr[:, :].rearrange("(f p) n -> p f n", p=128)

            @blk.sync
            def _(y):
                for qc in range(4):
                    if qc >= 2:
                        y.wait_ge(dv4, 8 * (qc - 1))
                    y.dma_start(attnb[:, (qc % 2) * 4096:(qc % 2) * 4096 + 4096]
                                .rearrange("p (f n) -> p f n", f=8),
                                attn_v[:, :, 512 * qc:512 * (qc + 1)]).then_inc(d4a, 16)
                    y.wait_ge(ap4, 8 * (qc + 1))
                    y.dma_start(x1_v[:, :, 512 * qc:512 * (qc + 1)],
                                x1f[:, :].rearrange("p (f n) -> p f n", f=8)).then_inc(st4, 16)
                y.wait_ge(st4, 64)

            @blk.gpsimd
            def _(g):
                g.dma_start(wpb[:, :].rearrange("p (c n) -> p c n", c=8),
                            w_proj[:, :, :]).then_inc(d4g, 16)
                for ch in range(16):
                    if ch >= 2:
                        g.wait_ge(t4t, 8 * (ch - 1))
                    g.dma_start(xb2[:, (ch % 2) * D:(ch % 2 + 1) * D],
                                x_seq[128 * ch:128 * (ch + 1), :]).then_inc(d4g, 16)

            @blk.tensor
            def _(t_):
                for qc in range(4):
                    for ch in range(4 * qc, 4 * qc + 4):
                        t_.wait_ge(d4g, max(48, 16 * (ch + 2)))
                        for half in range(2):
                            bi = 2 * ch + half
                            if bi >= 2:
                                t_.wait_ge(e4, 4 * (bi - 1))
                            pt = pTx[bi % 2]
                            ins = None
                            for j in range(4 * half, 4 * half + 4):
                                ins = t_.transpose(pt[:, 128 * (j % 4):128 * (j % 4) + 128],
                                                   xb2[:, (ch % 2) * D + 128 * j:(ch % 2) * D + 128 * j + 128],
                                                   c_idf[:, :])
                            ins.then_inc(t4t, 4)
                    t_.wait_ge(d4a, 16 * (qc + 1))
                    for hp in range(8):
                        zi = qc * 8 + hp
                        if zi >= 2:
                            t_.wait_ge(dv4, zi - 1)
                        t_.matmul(pzb[zi % 2][:, :], c_zsel[:, 128 * hp:128 * hp + 128],
                                  zrball[:, 512 * qc:512 * (qc + 1)],
                                  start=True, stop=True).then_inc(zb4, 1)
                    t_.wait_ge(e4, 4 * (8 * qc + 8))
                    t_.wait_ge(dv4, 8 * (qc + 1))

                    def stat_mm(d_):
                        gi_ = qc * 8 + d_
                        sl2 = slice((gi_ % 2) * 512, (gi_ % 2) * 512 + 512)
                        if d_ == 0 and qc >= 1:
                            t_.wait_ge(ch4, qc)
                        t_.wait_ge(c4, gi_ + 1)
                        t_.wait_ge(c4v, gi_ + 1)
                        t_.matmul(pls[0][:, :], c_aones[:, :], x1rb[:, sl2],
                                  start=(d_ == 0), stop=(d_ == 7))
                        t_.matmul(pls[1][:, :], c_aones[:, :], sq4[:, sl2],
                                  start=(d_ == 0), stop=(d_ == 7)).then_inc(s4m, 1)

                    for dout in range(8):
                        gi = qc * 8 + dout
                        if gi >= 2:
                            t_.wait_ge(r4, gi - 1)
                        ins = None
                        for c in range(8):
                            ins = t_.matmul(pp[gi % 2][:, :],
                                            wpb[:, c * 1024 + 128 * dout:c * 1024 + 128 * dout + 128],
                                            attnd[:, (qc % 2) * 4096 + 512 * c:(qc % 2) * 4096 + 512 * c + 512],
                                            start=(c == 0), stop=(c == 7))
                        ins.then_inc(p4m, 1)
                        if dout >= 2:
                            stat_mm(dout - 2)
                    stat_mm(6)
                    stat_mm(7)

            @blk.vector
            def _(v):
                vs = SelfSem(v, vs4s)
                v.memset(mark4[:, :], 0.0)
                for qc in range(4):
                    for ch in range(4 * qc, 4 * qc + 4):
                        for half in range(2):
                            bi = 2 * ch + half
                            v.wait_ge(t4t, 4 * (bi + 1))
                            pt = pTx[bi % 2]
                            ins = None
                            for j in range(4 * half, 4 * half + 4):
                                ins = v.tensor_copy(
                                    xoT[:, j * 512 + 128 * (ch % 4):j * 512 + 128 * (ch % 4) + 128],
                                    pt[:, 128 * (j % 4):128 * (j % 4) + 128])
                            ins.then_inc(e4, 4)
                    if qc >= 2:
                        v.wait_ge(p4m, 8 * (qc - 1))
                    for hp in range(8):
                        zi = qc * 8 + hp
                        v.wait_ge(zb4, zi + 1)
                        v.tensor_tensor(attnd[:, (qc % 2) * 4096 + 512 * hp:(qc % 2) * 4096 + 512 * hp + 512],
                                        attnb[:, (qc % 2) * 4096 + 512 * hp:(qc % 2) * 4096 + 512 * hp + 512],
                                        pzb[zi % 2][:, :], ALU.mult).then_inc(dv4, 1)
                    for dout in range(8):
                        gi = qc * 8 + dout
                        v.wait_ge(p4m, gi + 1)
                        if gi >= 2:
                            v.wait_ge(s4m, gi - 1)
                        sl = slice(512 * dout, 512 * (dout + 1))
                        sl2 = slice((gi % 2) * 512, (gi % 2) * 512 + 512)
                        vs.step(v.tensor_tensor(x1raw[:, sl], pp[gi % 2][:, :],
                                                xoT[:, sl], ALU.add))
                        vs.sync()
                        v.tensor_copy(mark4[:, :], mark4[:, :]).then_inc(r4, 1)
                        v.tensor_tensor(sq4[:, sl2], x1raw[:, sl], x1raw[:, sl],
                                        ALU.mult).then_inc(c4v, 1)
                    v.wait_ge(s4m, 8 * (qc + 1))
                    vs.step(v.tensor_scalar(m1k[:, :], pls[0][:, :], 1.0 / D, None, ALU.mult))
                    vs.sync()
                    vs.step(v.tensor_tensor(msq4[:, :], m1k[:, :], m1k[:, :], ALU.mult))
                    vs.sync()
                    vs.step(v.scalar_tensor_tensor(var4[:, :], pls[1][:, :], 1.0 / D,
                                                   msq4[:, :], ALU.mult, ALU.subtract))
                    vs.sync()
                    v.tensor_copy(mark4[:, :], mark4[:, :]).then_inc(ch4, 1)
                    v.wait_ge(sl4, qc + 1)
                    if qc >= 1:
                        v.wait_ge(st4, 16 * qc)
                        v.wait_ge(sx1, 8 * qc)
                    for dout in range(8):
                        sl = slice(512 * dout, 512 * (dout + 1))
                        vs.step(v.tensor_tensor(uu4[:, :], x1raw[:, sl], m1k[:, :], ALU.subtract))
                        vs.sync()
                        vs.step(v.tensor_tensor(x1f[:, sl], uu4[:, :], rstd4[:, :], ALU.mult))
                        vs.sync()
                        v.tensor_copy(mark4[:, :], mark4[:, :]).then_inc(ap4, 1)

            @blk.scalar
            def _(s):
                ss = SelfSem(s, ss4s)
                for qc in range(4):
                    for dout in range(8):
                        gi = qc * 8 + dout
                        s.wait_ge(r4, gi + 1)
                        if gi >= 2:
                            s.wait_ge(s4m, gi - 1)
                        sl = slice(512 * dout, 512 * (dout + 1))
                        sl2 = slice((gi % 2) * 512, (gi % 2) * 512 + 512)
                        s.copy(x1rb[:, sl2], x1raw[:, sl]).then_inc(c4, 1)
                    s.wait_ge(ch4, qc + 1)
                    ss.step(s.activation(lv4[:, :], var4[:, :], AF.Ln,
                                         bias=c_eps[:, 0:1], scale=1.0))
                    ss.sync()
                    s.activation(rstd4[:, :], lv4[:, :], AF.Exp, bias=0.0,
                                 scale=-0.5).then_inc(sl4, 1)
                    for dout in range(8):
                        s.wait_ge(ap4, qc * 8 + dout + 1)
                        sl = slice(512 * dout, 512 * (dout + 1))
                        s.copy(x1b[:, dout * NQ + 512 * qc:dout * NQ + 512 * qc + 512],
                               x1f[:, sl]).then_inc(sx1, 1)

    nc.all_engine_barrier()

    if LASTPH == "p4":
        if DEBUG:
            with nc.semaphore("dx1") as dx1, nc.Block() as blk:
                @blk.sync
                def _(y):
                    y.dma_start(dbg["x1"][:, :], x1_scr[:, :]).then_inc(dx1, 16)
                    y.wait_ge(dx1, 16)
        es2.close(); es0.close()
        return nc

    with ExitStack() as esp:
        sbp = lambda n, s, d=F32: esp.enter_context(nc.sbuf_tensor(n, s, d))
        psp = lambda n, s, d=F32: esp.enter_context(nc.psum_tensor(n, s, d))
        c_rw = sbp("c_rw", [128, 64])
        c_b1 = sbp("c_b1", [128, 32])
        b2b = sbp("b2b", [8, D], BF16)
        x1fq = sbp("x1fq", [128, 8 * 512])
        lgT = sbp("lgT", [8, 512])
        ltok = sbp("ltok", [128, 32])
        mwork = sbp("mwork", [128, 8 * 10 * 4])
        maskt = sbp("maskt", [128, 32])
        maskTb = sbp("maskTb", [8, 512], BF16)
        w1b = sbp("w1b", [128, 2 * 8 * 512], BF16)
        w2b = sbp("w2b", [128, 2 * 4096], BF16)
        h1m = sbp("h1m", [128, 32 * 512], BF16)
        x2raw = sbp("x2raw", [128, 8 * 512])
        x2rb = sbp("x2rb", [128, 512], BF16)
        sq5 = sbp("sq5", [128, 512], BF16)
        m1k5 = sbp("m1k5", [128, 512])
        msq5 = sbp("msq5", [128, 512])
        var5 = sbp("var5", [128, 512])
        lv5 = sbp("lv5", [128, 512])
        rstd5 = sbp("rstd5", [128, 512])
        uu5 = sbp("uu5", [128, 512])
        outf = sbp("outf", [128, 8 * 512])
        outtok = sbp("outtok", [128, 4 * 1024])
        mark5 = sbp("mark5", [1, 1])
        pstat = [psp("pstat0", [128, 512]), psp("pstat1", [128, 512])]
        pmb = [psp("pmb0", [128, 512]), psp("pmb1", [128, 512])]
        pf1 = [psp("pf10", [128, 512]), psp("pf11", [128, 512])]
        pout = [psp("pout0", [128, 512]), psp("pout1", [128, 512])]

        sems = ExitStack()
        _sm = lambda n: sems.enter_context(nc.semaphore(n))
        d5c = _sm("d5c"); d5x = _sm("d5x"); d5w = _sm("d5w")
        rt5 = _sm("rt5"); rv5 = _sm("rv5"); rs5 = _sm("rs5")
        f1m = _sm("f1m"); g5 = _sm("g5"); h5 = _sm("h5")
        o5m = _sm("o5m"); r5 = _sm("r5"); c5 = _sm("c5")
        s5m = _sm("s5m"); ch5 = _sm("ch5"); sl5 = _sm("sl5")
        ap5 = _sm("ap5"); tr5 = _sm("tr5"); ev5 = _sm("ev5"); mb5m = _sm("mb5m")
        st5 = _sm("st5"); vs5s = _sm("vs5"); ss5s = _sm("ss5")
        with nc.Block() as blk:
            x1_v = x1_scr[:, :].rearrange("(f p) n -> p f n", p=128)

            @blk.sync
            def _(y):
                y.dma_start(c_rw[:, :].rearrange("p (c e) -> p c e", c=8),
                            rw[:, :, :]).then_inc(d5c, 16)
                y.dma_start(c_b1[:, :], b1c[:, :]).then_inc(d5c, 16)
                for qc in range(4):
                    if qc >= 1:
                        y.wait_ge(r5, 8 * qc)
                    y.dma_start(x1fq[:, :].rearrange("p (f n) -> p f n", f=8),
                                x1_v[:, :, 512 * qc:512 * (qc + 1)]).then_inc(d5x, 16)
                    for tc in range(4):
                        y.wait_ge(ev5, 32 * qc + 8 * (tc + 1))
                        y.dma_start(out[512 * qc + 128 * tc:512 * qc + 128 * tc + 128, :],
                                    outtok[:, 1024 * tc:1024 * (tc + 1)]).then_inc(st5, 16)
                y.wait_ge(st5, 16 * 16)

            @blk.gpsimd
            def _(g):
                g.dma_start(b2b[:, :], b2[:, :]).then_inc(d5w, 16)
                for qc in range(4):
                    for e in range(NE):
                        gi = qc * 8 + e
                        if gi >= 2:
                            g.wait_ge(f1m, 4 * (gi - 1))
                        g.dma_start(w1b[:, (gi % 2) * 4096:(gi % 2) * 4096 + 4096],
                                    w1s[e, :, :]).then_inc(d5w, 16)
                    for dt in range(8):
                        gj = qc * 8 + dt
                        if gj >= 2:
                            g.wait_ge(o5m, gj - 1)
                        g.dma_start(w2b[:, (gj % 2) * 4096:(gj % 2) * 4096 + 4096],
                                    w2s[dt, :, :]).then_inc(d5w, 16)

            @blk.tensor
            def _(t_):
                t_.wait_ge(d5c, 32)
                for qc in range(4):
                    t_.wait_ge(d5x, 16 * (qc + 1))
                    if qc >= 1:
                        t_.wait_ge(s5m, 8 * qc)
                        t_.wait_ge(ch5, qc)
                    ins = None
                    for c in range(8):
                        ins = t_.matmul(pstat[0][0:8, :], c_rw[:, 8 * c:8 * c + 8],
                                        x1fq[:, 512 * c:512 * (c + 1)],
                                        start=(c == 0), stop=(c == 7))
                    ins.then_inc(rt5, 1)
                    t_.wait_ge(rv5, 5 * qc + 1)
                    for tc in range(4):
                        t_.transpose(pstat[1][:, 32 * tc:32 * tc + 8],
                                     lgT[0:8, 128 * tc:128 * tc + 128],
                                     c_idf[0:8, 0:8]).then_inc(rt5, 1)
                    if qc >= 1:
                        t_.wait_ge(ev5, 32 * qc)
                    for e in range(NE):
                        gi = qc * 8 + e
                        t_.wait_ge(d5w, 16 * (2 + 16 * qc + e))
                        for j in range(4):
                            ji = gi * 4 + j
                            if ji >= 2:
                                t_.wait_ge(g5, ji - 1)
                            if e == 4 and j == 0:
                                t_.wait_ge(rv5, 5 * qc + 4)
                                t_.wait_ge(h5, 32 * qc)
                                for tc in range(4):
                                    t_.transpose(pmb[0][0:8, 128 * tc:128 * tc + 128],
                                                 maskt[:, 8 * tc:8 * tc + 8],
                                                 c_idf[:, :]).then_inc(rt5, 1)
                            ins = None
                            for c in range(8):
                                ins = t_.matmul(
                                    pf1[ji % 2][:, :],
                                    w1b[:, (gi % 2) * 4096 + 512 * c + 128 * j:
                                        (gi % 2) * 4096 + 512 * c + 128 * j + 128],
                                    x1b[:, c * NQ + 512 * qc:c * NQ + 512 * qc + 512],
                                    start=(c == 0), stop=(c == 7))
                            ins.then_inc(f1m, 1)
                    t_.wait_ge(rv5, 5 * qc + 5)
                    for e in range(NE):
                        gi = qc * 8 + e
                        t_.wait_ge(h5, 4 * max(0, gi - 1))
                        t_.matmul(pmb[gi % 2][:, :],
                                  c_emats[:, 128 * e:128 * e + 128],
                                  maskTb[:, :], start=True, stop=True).then_inc(mb5m, 1)
                    for dt in range(8):
                        gj = qc * 8 + dt
                        t_.wait_ge(d5w, 16 * (10 + 16 * qc + dt))
                        if gj >= 2:
                            t_.wait_ge(r5, gj - 1)
                        for e in range(NE):
                            t_.wait_ge(h5, qc * 32 + 4 * (e + 1))
                            for c in range(4):
                                t_.matmul(
                                    pout[gj % 2][:, :],
                                    w2b[:, (gj % 2) * 4096 + (e * 4 + c) * 128:
                                        (gj % 2) * 4096 + (e * 4 + c) * 128 + 128],
                                    h1m[:, (e * 4 + c) * 512:(e * 4 + c) * 512 + 512],
                                    start=(e == 0 and c == 0), stop=False)
                        t_.matmul(pout[gj % 2][:, :], b2b[:, 128 * dt:128 * dt + 128],
                                  maskTb[:, :], start=False, stop=True).then_inc(o5m, 1)
                    if qc >= 1:
                        t_.wait_ge(ch5, qc)
                    for dt in range(8):
                        t_.wait_ge(c5, qc * 8 + dt + 1)
                        t_.matmul(pstat[0][:, :], c_aones[:, :], x2rb[:, :],
                                  start=(dt == 0), stop=(dt == 7))
                        t_.matmul(pstat[1][:, :], c_aones[:, :], sq5[:, :],
                                  start=(dt == 0), stop=(dt == 7)).then_inc(s5m, 1)
                    t_.wait_ge(ap5, 8 * (qc + 1))
                    for tc in range(4):
                        ti = qc * 4 + tc
                        if ti >= 2:
                            t_.wait_ge(ev5, 8 * (ti - 1))
                        ins = None
                        for dtj in range(8):
                            ins = t_.transpose(pf1[tc % 2][:, 128 * (dtj % 4):128 * (dtj % 4) + 128]
                                               , outf[:, dtj * 512 + 128 * tc:dtj * 512 + 128 * tc + 128],
                                               c_idf[:, :])
                            if dtj == 3:
                                ins.then_inc(tr5, 4)
                                t_.wait_ge(ev5, 8 * ti + 4)
                        ins.then_inc(tr5, 4)

            @blk.vector
            def _(v):
                vs = SelfSem(v, vs5s)
                v.memset(mark5[:, :], 0.0)
                for qc in range(4):
                    v.wait_ge(rt5, 9 * qc + 1)
                    vs.step(v.tensor_copy(lgT[0:8, :], pstat[0][0:8, :]))
                    vs.sync()
                    v.tensor_copy(mark5[:, :], mark5[:, :]).then_inc(rv5, 1)
                    v.wait_ge(rt5, 9 * qc + 5)
                    for tc in range(4):
                        vs.step(v.tensor_tensor(ltok[:, 8 * tc:8 * tc + 8],
                                                pstat[1][:, 32 * tc:32 * tc + 8],
                                                c_rbias[:, :], ALU.add))
                    vs.sync()
                    v.tensor_copy(mark5[:, :], mark5[:, :]).then_inc(rv5, 1)
                    for tc in range(4):
                        lsl = ltok[:, 8 * tc:8 * tc + 8]
                        w0 = mwork[:, 80 * tc:80 * tc + 80]
                        m1 = w0[:, 0:1]; ge1 = w0[:, 8:16]; lm = w0[:, 16:24]
                        m2 = w0[:, 1:2]; ind = w0[:, 24:32]; nm1 = w0[:, 4:5]
                        vs.step(v.reduce_max(m1, lsl, axis=AX.X)); vs.sync()
                        vs.step(v.tensor_scalar(ge1, lsl, m1, None, ALU.is_ge)); vs.sync()
                        vs.step(v.scalar_tensor_tensor(lm, ge1, -1e30, lsl, ALU.mult, ALU.add)); vs.sync()
                        vs.step(v.reduce_max(m2, lm, axis=AX.X)); vs.sync()
                        vs.step(v.tensor_scalar(ind, lsl, m2, None, ALU.is_ge)); vs.sync()
                        vs.step(v.tensor_scalar(nm1, m1, -1.0, None, ALU.mult)); vs.sync()
                    v.tensor_copy(mark5[:, :], mark5[:, :]).then_inc(rv5, 1)
                    v.wait_ge(rs5, 4 * (qc + 1))
                    for tc in range(4):
                        w0 = mwork[:, 80 * tc:80 * tc + 80]
                        ind = w0[:, 24:32]; ex = w0[:, 32:40]
                        wv = w0[:, 40:48]; ssm = w0[:, 2:3]; rs = w0[:, 3:4]
                        vs.step(v.tensor_tensor(wv, ex, ind, ALU.mult)); vs.sync()
                        vs.step(v.reduce_sum(ssm, wv, axis=AX.X)); vs.sync()
                        vs.step(v.reciprocal(rs, ssm)); vs.sync()
                        vs.step(v.tensor_scalar(maskt[:, 8 * tc:8 * tc + 8], wv, rs, None,
                                                ALU.mult)); vs.sync()
                    v.tensor_copy(mark5[:, :], mark5[:, :]).then_inc(rv5, 1)
                    v.wait_ge(rt5, 9 * qc + 9)
                    for tc in range(4):
                        vs.step(v.tensor_copy(maskTb[0:8, 128 * tc:128 * tc + 128],
                                              pmb[0][0:8, 128 * tc:128 * tc + 128]))
                    vs.sync()
                    v.tensor_copy(mark5[:, :], mark5[:, :]).then_inc(rv5, 1)
                    for e in range(NE):
                        gi = qc * 8 + e
                        v.wait_ge(mb5m, qc * 8 + e + 1)
                        for j in range(4):
                            ji = gi * 4 + j
                            v.wait_ge(g5, ji + 1)
                            v.tensor_tensor(h1m[:, (e * 4 + j) * 512:(e * 4 + j) * 512 + 512],
                                            h1m[:, (e * 4 + j) * 512:(e * 4 + j) * 512 + 512],
                                            pmb[gi % 2][:, :], ALU.mult).then_inc(h5, 1)
                    for dt in range(8):
                        gj = qc * 8 + dt
                        v.wait_ge(o5m, gj + 1)
                        sl = slice(512 * dt, 512 * (dt + 1))
                        vs.step(v.tensor_tensor(x2raw[:, sl], pout[gj % 2][:, :],
                                                x1fq[:, sl], ALU.add))
                        vs.sync()
                        v.tensor_copy(mark5[:, :], mark5[:, :]).then_inc(r5, 1)
                    for dt in range(8):
                        gj = qc * 8 + dt
                        sl = slice(512 * dt, 512 * (dt + 1))
                        if gj >= 1:
                            v.wait_ge(s5m, gj)
                        vs.step(v.tensor_copy(x2rb[:, :], x2raw[:, sl]))
                        vs.sync()
                        vs.step(v.tensor_tensor(sq5[:, :], x2rb[:, :], x2rb[:, :], ALU.mult))
                        vs.sync()
                        v.tensor_copy(mark5[:, :], mark5[:, :]).then_inc(c5, 1)
                    v.wait_ge(s5m, 8 * (qc + 1))
                    vs.step(v.tensor_scalar(m1k5[:, :], pstat[0][:, :], 1.0 / D, None, ALU.mult))
                    vs.sync()
                    vs.step(v.tensor_tensor(msq5[:, :], m1k5[:, :], m1k5[:, :], ALU.mult))
                    vs.sync()
                    vs.step(v.scalar_tensor_tensor(var5[:, :], pstat[1][:, :], 1.0 / D,
                                                   msq5[:, :], ALU.mult, ALU.subtract))
                    vs.sync()
                    v.tensor_copy(mark5[:, :], mark5[:, :]).then_inc(ch5, 1)
                    v.wait_ge(sl5, qc + 1)
                    if qc >= 1:
                        v.wait_ge(tr5, 32 * qc)
                    for dt in range(8):
                        sl = slice(512 * dt, 512 * (dt + 1))
                        vs.step(v.tensor_tensor(uu5[:, :], x2raw[:, sl], m1k5[:, :], ALU.subtract))
                        vs.sync()
                        vs.step(v.tensor_tensor(outf[:, sl], uu5[:, :], rstd5[:, :], ALU.mult))
                        vs.sync()
                        v.tensor_copy(mark5[:, :], mark5[:, :]).then_inc(ap5, 1)
                    for tc in range(4):
                        ti = qc * 4 + tc
                        v.wait_ge(tr5, 32 * qc + 8 * tc + 4)
                        if ti >= 1:
                            v.wait_ge(st5, 16 * ti - 16)
                        ins = None
                        for dtj in range(4):
                            ins = v.tensor_copy(outtok[:, 1024 * tc + 128 * dtj:1024 * tc + 128 * dtj + 128],
                                                pf1[tc % 2][:, 128 * dtj:128 * dtj + 128])
                        ins.then_inc(ev5, 4)
                        v.wait_ge(tr5, 32 * qc + 8 * tc + 8)
                        ins = None
                        for dtj in range(4, 8):
                            ins = v.tensor_copy(outtok[:, 1024 * tc + 128 * dtj:1024 * tc + 128 * dtj + 128],
                                                pf1[tc % 2][:, 128 * (dtj % 4):128 * (dtj % 4) + 128])
                        ins.then_inc(ev5, 4)

            @blk.scalar
            def _(s):
                ss = SelfSem(s, ss5s)
                for qc in range(4):
                    s.wait_ge(rv5, 5 * qc + 3)
                    for tc in range(4):
                        w0 = mwork[:, 80 * tc:80 * tc + 80]
                        nm1 = w0[:, 4:5]; ex = w0[:, 32:40]
                        s.activation(ex, ltok[:, 8 * tc:8 * tc + 8], AF.Exp,
                                     bias=nm1, scale=1.0).then_inc(rs5, 1)
                    for e in range(NE):
                        gi = qc * 8 + e
                        for j in range(4):
                            ji = gi * 4 + j
                            s.wait_ge(f1m, ji + 1)
                            s.activation(h1m[:, (e * 4 + j) * 512:(e * 4 + j) * 512 + 512],
                                         pf1[ji % 2][:, :], AF.Gelu,
                                         bias=c_b1[:, 4 * e + j:4 * e + j + 1],
                                         scale=1.0).then_inc(g5, 1)
                    s.wait_ge(ch5, qc + 1)
                    ss.step(s.activation(lv5[:, :], var5[:, :], AF.Ln,
                                         bias=c_eps[:, 0:1], scale=1.0))
                    ss.sync()
                    s.activation(rstd5[:, :], lv5[:, :], AF.Exp, bias=0.0,
                                 scale=-0.5).then_inc(sl5, 1)

        sems.close()

    nc.all_engine_barrier()
    es2.close()
    es0.close()
    return nc



def _prep_shared(inputs):
    w_qkv = np.ascontiguousarray(
        inputs["L0_attn_w_qkv"].reshape(8, 128, 3 * D).transpose(1, 0, 2))
    w_proj = np.ascontiguousarray(
        inputs["L0_attn_w_proj"].reshape(8, 128, D).transpose(1, 0, 2))
    w1r = np.ascontiguousarray(
        inputs["L0_ffn_W1"].reshape(NE, 8, 128, DF).transpose(0, 2, 1, 3))
    w2r = np.ascontiguousarray(
        inputs["L0_ffn_W2"].reshape(NE, 4, 128, 8, 128).transpose(3, 2, 0, 1, 4))
    b1c = np.ascontiguousarray(
        inputs["L0_ffn_b1"].reshape(NE, 4, 128).transpose(2, 0, 1).reshape(128, NE * 4))
    rw = np.ascontiguousarray(
        inputs["L0_ffn_router_w"].reshape(8, 128, NE).transpose(1, 0, 2))
    rbias = np.ascontiguousarray(
        np.broadcast_to(inputs["L0_ffn_router_b"][None, :], (128, NE)))
    qg2 = np.ascontiguousarray(np.tile(inputs["L0_attn_q_gamma"], 2))[:, None]
    qb2 = np.ascontiguousarray(np.tile(inputs["L0_attn_q_beta"], 2))[:, None]
    kg2 = np.ascontiguousarray(np.tile(inputs["L0_attn_k_gamma"], 2))[:, None]
    kb2 = np.ascontiguousarray(np.tile(inputs["L0_attn_k_beta"], 2))[:, None]
    emats = np.zeros((8, NE * 128), np.float32)
    for e in range(NE):
        emats[e, e * 128:(e + 1) * 128] = 1.0
    zsel = np.zeros((16, 8 * 128), np.float32)
    for hp in range(8):
        zsel[2 * hp, 128 * hp:128 * hp + 64] = 1.0
        zsel[2 * hp + 1, 128 * hp + 64:128 * hp + 128] = 1.0
    shared = {
        "w_qkv": w_qkv, "w_proj": w_proj, "w1r": w1r, "w2r": w2r,
        "b1c": b1c, "b2": np.ascontiguousarray(inputs["L0_ffn_b2"]),
        "rw": rw, "rbias": rbias,
        "qg2": qg2, "qb2": qb2, "kg2": kg2, "kb2": kb2,
        "idf": np.eye(128, dtype=np.float32),
        "idb": np.eye(128, dtype=np.float32).astype(BF),
        "hones": np.kron(np.eye(2, dtype=np.float32), np.ones((64, 64), np.float32)).astype(BF),
        "aones": np.ones((128, 128), np.float32).astype(BF),
        "onesrow": np.ones((1, 128), np.float32).astype(BF),
        "onescolf": np.ones((1, 128), np.float32),
        "emats": emats.astype(BF),
        "zsel": zsel.astype(BF),
        "epscol": np.full((128, 1), EPS, np.float32),
    }
    return shared


def make_in_maps(inputs):
    shared = _prep_shared(inputs)
    x = inputs["L0_attn_x"]
    in_maps = []
    for core in range(8):
        b, hf = core // 2, core % 2
        own = x[b, hf * NQ:(hf + 1) * NQ]
        other = x[b, (1 - hf) * NQ:(2 - hf) * NQ]
        x_roll = np.ascontiguousarray(np.concatenate([own, other], axis=0))
        m = dict(shared)
        m["x_seq"] = x_roll
        in_maps.append(m)
    return in_maps


def kernel(**inputs):
    in_maps = make_in_maps(inputs)
    nc = build_nc()
    res = run_bass_kernel_spmd(nc, in_maps, core_ids=list(range(8)))
    outs = [np.asarray(r["out"]) for r in res.results]
    full = np.empty((4, NS, D), np.float32)
    for core in range(8):
        b, hf = core // 2, core % 2
        full[b, hf * NQ:(hf + 1) * NQ] = outs[core]
    return full


# version 16
# speedup vs baseline: 1.0932x; 1.0164x over previous
import os
import numpy as np
import ml_dtypes
from contextlib import ExitStack

import concourse.bass as bass
import concourse.mybir as mybir
from concourse.bass_utils import run_bass_kernel_spmd

F32 = mybir.dt.float32
BF16 = mybir.dt.bfloat16
AF = mybir.ActivationFunctionType
ALU = mybir.AluOpType
AX = mybir.AxisListType
BF = ml_dtypes.bfloat16

D = 1024
H = 16
DH = 64
NQ = 2048
NS = 4096
NE = 8
DF = 512
EPS = 1e-5

DEBUG = int(os.environ.get("KMK_DEBUG", "0"))
LASTPH = os.environ.get("KMK_LAST", "p5")


class SelfSem:
    def __init__(self, eng, sem):
        self.eng, self.sem, self.n = eng, sem, 0

    def step(self, ins):
        ins.then_inc(self.sem, 1)
        self.n += 1
        return ins

    def sync(self):
        self.eng.wait_ge(self.sem, self.n)


def build_nc():
    nc = bass.Bass("TRN2", target_bir_lowering=False,
                   detect_race_conditions=bool(int(os.environ.get("KMK_RACE", "0"))))

    x_seq = nc.dram_tensor("x_seq", [NS, D], F32, kind="ExternalInput")
    w_qkv = nc.dram_tensor("w_qkv", [128, 8, 3 * D], F32, kind="ExternalInput")
    w_proj = nc.dram_tensor("w_proj", [128, 8, D], F32, kind="ExternalInput")
    w1r = nc.dram_tensor("w1r", [NE, 128, 8, DF], F32, kind="ExternalInput")
    w2r = nc.dram_tensor("w2r", [8, 128, NE, 4, 128], F32, kind="ExternalInput")
    b1c = nc.dram_tensor("b1c", [128, NE * 4], F32, kind="ExternalInput")
    b2 = nc.dram_tensor("b2", [NE, D], F32, kind="ExternalInput")
    rw = nc.dram_tensor("rw", [128, 8, NE], F32, kind="ExternalInput")
    rbias = nc.dram_tensor("rbias", [128, NE], F32, kind="ExternalInput")
    qg2 = nc.dram_tensor("qg2", [128, 1], F32, kind="ExternalInput")
    qb2 = nc.dram_tensor("qb2", [128, 1], F32, kind="ExternalInput")
    kg2 = nc.dram_tensor("kg2", [128, 1], F32, kind="ExternalInput")
    kb2 = nc.dram_tensor("kb2", [128, 1], F32, kind="ExternalInput")
    idf = nc.dram_tensor("idf", [128, 128], F32, kind="ExternalInput")
    idb = nc.dram_tensor("idb", [128, 128], BF16, kind="ExternalInput")
    hones = nc.dram_tensor("hones", [128, 128], BF16, kind="ExternalInput")
    aones = nc.dram_tensor("aones", [128, 128], BF16, kind="ExternalInput")
    onesrow = nc.dram_tensor("onesrow", [1, 128], BF16, kind="ExternalInput")
    onescolf = nc.dram_tensor("onescolf", [1, 128], F32, kind="ExternalInput")
    emats = nc.dram_tensor("emats", [8, NE * 128], BF16, kind="ExternalInput")
    zsel = nc.dram_tensor("zsel", [16, 8 * 128], BF16, kind="ExternalInput")
    epscol = nc.dram_tensor("epscol", [128, 1], F32, kind="ExternalInput")

    out = nc.dram_tensor("out", [NQ, D], F32, kind="ExternalOutput")
    attn_scr = nc.dram_tensor("attn_scr", [D, NQ], BF16, kind="Internal")
    x1_scr = nc.dram_tensor("x1_scr", [D, NQ], F32, kind="Internal")
    w1s = nc.dram_tensor("w1s", [NE, 128, 8 * DF], BF16, kind="Internal")
    w2s = nc.dram_tensor("w2s", [8, 128, 32 * 128], BF16, kind="Internal")

    dbg = {}
    def dbg_t(name, shape, dt):
        if DEBUG:
            dbg[name] = nc.dram_tensor("dbg_" + name, shape, dt, kind="ExternalOutput")
    dbg_t("qTb", [128, 8 * NQ], BF16)
    dbg_t("kTb", [128, 8 * NS], BF16)
    dbg_t("vt", [128, 32 * H * 65], BF16)
    dbg_t("cb", [128, H], F32)
    dbg_t("attn", [D, NQ], BF16)
    dbg_t("x1", [D, NQ], F32)
    dbg_t("mask", [NQ, NE], F32)

    es0 = ExitStack()
    sb0 = lambda n, s, d=F32: es0.enter_context(nc.sbuf_tensor(n, s, d))
    wcl = es0.enter_context(nc.semaphore("wcl"))
    wcs = es0.enter_context(nc.semaphore("wcs"))
    wcb = sb0("wcb", [128, 2048], BF16)

    c_idf = sb0("c_idf", [128, 128])
    c_idb = sb0("c_idb", [128, 128], BF16)
    c_hones = sb0("c_hones", [128, 128], BF16)
    c_aones = sb0("c_aones", [128, 128], BF16)
    c_onesrow = sb0("c_onesrow", [1, 128], BF16)
    c_onescolf = sb0("c_onescolf", [1, 128])
    c_emats = sb0("c_emats", [8, NE * 128], BF16)
    c_qg2 = sb0("c_qg2", [128, 1]); c_qb2 = sb0("c_qb2", [128, 1])
    c_kg2 = sb0("c_kg2", [128, 1]); c_kb2 = sb0("c_kb2", [128, 1])
    c_eps = sb0("c_eps", [128, 1])
    c_rbias = sb0("c_rbias", [128, NE])
    c_cb = sb0("c_cb", [128, H])
    c_zsel = sb0("c_zsel", [16, 8 * 128], BF16)
    zrball = sb0("zrball", [16, NQ], BF16)

    with nc.semaphore("p0s") as p0s, nc.Block() as blk:
        @blk.sync
        def _(y):
            n = 0
            for dst, src in [
                (c_idf, idf), (c_idb, idb), (c_hones, hones), (c_aones, aones),
                (c_onesrow, onesrow), (c_onescolf, onescolf), (c_emats, emats),
                (c_qg2, qg2), (c_qb2, qb2), (c_kg2, kg2), (c_kb2, kb2),
                (c_eps, epscol), (c_rbias, rbias), (c_zsel, zsel),
            ]:
                y.dma_start(dst[:, :], src[:, :]).then_inc(p0s, 16)
                n += 16
            y.wait_ge(p0s, n)
    nc.all_engine_barrier()

    es1 = ExitStack()
    sb1 = lambda n, s, d=F32: es1.enter_context(nc.sbuf_tensor(n, s, d))
    qTb = sb1("qTb", [128, 8 * NQ], BF16)
    kTb = sb1("kTb", [128, 8 * NS], BF16)

    with ExitStack() as esp:
        sbp = lambda n, s, d=F32: esp.enter_context(nc.sbuf_tensor(n, s, d))
        psp = lambda n, s, d=F32: esp.enter_context(nc.psum_tensor(n, s, d))
        wqk = sbp("wqk", [128, 8 * 2048], BF16)
        xb = sbp("xb", [128, 2 * D], BF16)
        xT = sbp("xT", [128, 2 * 8 * 512], BF16)
        stageb = sbp("stageb", [128, 4 * 512], BF16)
        sqb = sbp("sqb", [128, 4 * 512], BF16)
        m64 = sbp("m64", [128, 4 * 512])
        msq = sbp("msq", [128, 2 * 512])
        varr = sbp("varr", [128, 2 * 512])
        lvr = sbp("lvr", [128, 2 * 512])
        rstd = sbp("rstd", [128, 4 * 512])
        uu = sbp("uu", [128, 2 * 512])
        ww = sbp("ww", [128, 2 * 512])
        mark1 = sbp("mark1", [1, 1])
        pT = psp("pT", [128, 1024], BF16)
        pT2 = psp("pT2", [128, 1024], BF16)
        pmm = [psp("pmm0", [128, 512]), psp("pmm1", [128, 512]),
               psp("pmm2", [128, 512]), psp("pmm3", [128, 512])]
        pst = [psp("pst0", [128, 512]), psp("pst1", [128, 512])]

        NCH, NG = 32, 8
        units = []
        for g in range(NG):
            for t in range(8):
                units.append((1, t, g))
            if g < 4:
                for t in range(8):
                    units.append((0, t, g))
        NU = len(units)
        ucum = [0] * (NG + 1)
        for g in range(NG):
            ucum[g + 1] = ucum[g] + (16 if g < 4 else 8)

        def unit_dst(kind, t, g):
            base = (qTb if kind == 0 else kTb)
            return base[:, t * (NQ if kind == 0 else NS) + 512 * g:
                        t * (NQ if kind == 0 else NS) + 512 * (g + 1)]

        with (
            nc.semaphore("dxg") as dxg,
            nc.semaphore("tT") as tT,
            nc.semaphore("eT") as eT,
            nc.semaphore("mmq") as mmq,
            nc.semaphore("evq") as evq,
            nc.semaphore("smq") as smq,
            nc.semaphore("dch") as dch,
            nc.semaphore("stl") as stl,
            nc.semaphore("apq") as apq,
            nc.semaphore("aplast") as aplast,
            nc.semaphore("evs") as evs,
            nc.semaphore("vself") as vself,
            nc.semaphore("sself") as sself,
            nc.semaphore("dbgs") as dbgs,
            nc.Block() as blk,
        ):
            @blk.gpsimd
            def _(g):
                g.dma_start(wqk[:, :].rearrange("p (c n) -> p c n", c=8),
                            w_qkv[:, :, 0:2048]).then_inc(dxg, 16)
                for ch in range(NCH):
                    if ch >= 2:
                        g.wait_ge(tT, 8 * (ch - 1))
                    g.dma_start(xb[:, (ch % 2) * D:(ch % 2 + 1) * D],
                                x_seq[128 * ch:128 * (ch + 1), :]).then_inc(dxg, 16)

            @blk.tensor
            def _(t):
                emitted = [0]

                def qkv_unit(u):
                    kind, tl, g = units[u]
                    if u >= 4:
                        t.wait_ge(aplast, u - 3)
                    ins = None
                    for c in range(8):
                        ins = t.matmul(
                            pmm[u % 4][:, :],
                            wqk[:, c * 2048 + kind * 1024 + 128 * tl:
                                c * 2048 + kind * 1024 + 128 * tl + 128],
                            xT[:, ((g % 2) * 8 + c) * 512:((g % 2) * 8 + c) * 512 + 512],
                            start=(c == 0), stop=(c == 7))
                    ins.then_inc(mmq, 1)

                def stat_unit(u):
                    t.wait_ge(evq, u + 1)
                    if u >= 1:
                        t.wait_ge(dch, u)
                    par = u % 4
                    t.matmul(pst[0][:, :], c_hones[:, :], stageb[:, 512 * par:512 * (par + 1)],
                             start=True, stop=True)
                    t.matmul(pst[1][:, :], c_hones[:, :], sqb[:, 512 * par:512 * (par + 1)],
                             start=True, stop=True).then_inc(smq, 1)

                u = 0
                for g in range(NG):
                    for ch in range(4 * g, 4 * g + 4):
                        t.wait_ge(dxg, max(48, 16 * (ch + 2)))
                        if ch >= 2:
                            t.wait_ge(eT, 8 * (ch - 1))
                        pt = pT if ch % 2 == 0 else pT2
                        ins = None
                        for j in range(8):
                            ins = t.transpose(pt[:, 128 * j:128 * (j + 1)],
                                              xb[:, (ch % 2) * D + 128 * j:(ch % 2) * D + 128 * (j + 1)],
                                              c_idb[:, :])
                        ins.then_inc(tT, 8)
                    t.wait_ge(eT, 8 * (4 * g + 4))
                    for _i in range(ucum[g + 1] - ucum[g]):
                        qkv_unit(u)
                        if u > ucum[g]:
                            stat_unit(u - 1)
                        u += 1
                    stat_unit(u - 1)

            @blk.scalar
            def _(s):
                ss = SelfSem(s, sself)
                for g in range(NG):
                    for ch in range(4 * g, 4 * g + 4):
                        s.wait_ge(tT, 8 * (ch + 1))
                        if g >= 2:
                            s.wait_ge(mmq, ucum[g - 1])
                        pt = pT if ch % 2 == 0 else pT2
                        ins = None
                        for j in range(8):
                            ins = s.copy(
                                xT[:, ((g % 2) * 8 + j) * 512 + 128 * (ch % 4):
                                   ((g % 2) * 8 + j) * 512 + 128 * (ch % 4) + 128],
                                pt[:, 128 * j:128 * (j + 1)])
                        ins.then_inc(eT, 8)
                    def sevac(u):
                        p4e = u % 4
                        s.wait_ge(mmq, u + 1)
                        if u >= 4:
                            s.wait_ge(smq, u - 3)
                        s.copy(stageb[:, 512 * p4e:512 * (p4e + 1)],
                               pmm[p4e][:, :]).then_inc(evs, 1)

                    def schain(u):
                        p4_ = u % 4
                        p2_ = u % 2
                        s.wait_ge(dch, u + 1)
                        ins = s.activation(lvr[:, 512 * p2_:512 * (p2_ + 1)],
                                           varr[:, 512 * p2_:512 * (p2_ + 1)],
                                           AF.Ln, bias=c_eps[:, 0:1], scale=1.0)
                        ss.step(ins)
                        ss.sync()
                        s.activation(rstd[:, 512 * p4_:512 * (p4_ + 1)],
                                     lvr[:, 512 * p2_:512 * (p2_ + 1)],
                                     AF.Exp, bias=0.0, scale=-0.5).then_inc(stl, 1)

                    st_, en_ = ucum[g], ucum[g + 1]
                    for u in range(st_, en_):
                        sevac(u)
                        if u >= st_ + 2:
                            schain(u - 2)
                    schain(en_ - 2)
                    schain(en_ - 1)

            @blk.vector
            def _(v):
                vs = SelfSem(v, vself)
                v.memset(mark1[:, :], 0.0)

                def ph1(u):
                    p4_ = u % 4
                    s4 = slice(512 * p4_, 512 * (p4_ + 1))
                    v.wait_ge(evs, u + 1)
                    v.tensor_tensor(sqb[:, s4], stageb[:, s4], stageb[:, s4],
                                    ALU.mult).then_inc(evq, 1)

                def ph2(u):
                    p4_ = u % 4
                    p2_ = u % 2
                    s4 = slice(512 * p4_, 512 * (p4_ + 1))
                    s2 = slice(512 * p2_, 512 * (p2_ + 1))
                    v.wait_ge(smq, u + 1)
                    vs.step(v.tensor_scalar(m64[:, s4], pst[0][:, :], 1.0 / DH, None, ALU.mult))
                    vs.sync()
                    vs.step(v.tensor_tensor(msq[:, s2], m64[:, s4], m64[:, s4], ALU.mult))
                    vs.sync()
                    v.scalar_tensor_tensor(varr[:, s2], pst[1][:, :], 1.0 / DH, msq[:, s2],
                                           ALU.mult, ALU.subtract).then_inc(dch, 1)
                    vs.step(v.tensor_tensor(uu[:, s2], pmm[p4_][:, :], m64[:, s4], ALU.subtract))
                    vs.sync()
                    v.tensor_copy(mark1[:, :], mark1[:, :]).then_inc(aplast, 1)

                def ph3(u):
                    kind, tl, g = units[u]
                    p4_ = u % 4
                    p2_ = u % 2
                    s4 = slice(512 * p4_, 512 * (p4_ + 1))
                    s2 = slice(512 * p2_, 512 * (p2_ + 1))
                    v.wait_ge(stl, u + 1)
                    vs.step(v.tensor_tensor(ww[:, s2], uu[:, s2], rstd[:, s4], ALU.mult))
                    vs.sync()
                    gcol = c_qg2 if kind == 0 else c_kg2
                    bcol = c_qb2 if kind == 0 else c_kb2
                    v.tensor_scalar(unit_dst(kind, tl, g), ww[:, s2],
                                    gcol[:, 0:1], bcol[:, 0:1],
                                    ALU.mult, ALU.add).then_inc(apq, 1)

                for g in range(NG):
                    st_, en_ = ucum[g], ucum[g + 1]
                    for u in range(st_, en_):
                        ph1(u)
                        if u >= st_ + 1:
                            ph2(u - 1)
                        if u >= st_ + 2:
                            ph3(u - 2)
                    ph2(en_ - 1)
                    ph3(en_ - 2)
                    ph3(en_ - 1)

            @blk.sync
            def _(y):
                y.wait_ge(apq, NU)
                if DEBUG:
                    y.dma_start(dbg["qTb"][:, :], qTb[:, :]).then_inc(dbgs, 16)
                    y.dma_start(dbg["kTb"][:, :], kTb[:, :]).then_inc(dbgs, 16)
                    y.wait_ge(dbgs, 32)

    nc.all_engine_barrier()

    es1b = ExitStack()
    vt = es1b.enter_context(nc.sbuf_tensor("vt", [128, 32 * H * 65], BF16))

    with ExitStack() as esp:
        sbp = lambda n, s, d=F32: esp.enter_context(nc.sbuf_tensor(n, s, d))
        psp = lambda n, s, d=F32: esp.enter_context(nc.psum_tensor(n, s, d))
        wv = sbp("wv", [128, 8 * 1024], BF16)
        xb = sbp("xbv", [128, 2 * D], BF16)
        xTc = sbp("xTc", [128, 2 * 8 * 128], BF16)
        pT = psp("pTv", [128, 1024], BF16)
        pT2 = psp("pTv2", [128, 1024], BF16)
        pv = [psp("pv0", [128, 512]), psp("pv1", [128, 512]),
              psp("pv2", [128, 512]), psp("pv3", [128, 512])]

        with (
            nc.semaphore("dxg2") as dxg,
            nc.semaphore("tT2") as tT,
            nc.semaphore("eT2") as eT,
            nc.semaphore("mmv") as mmv,
            nc.semaphore("evv") as evv,
            nc.semaphore("ones") as onesem,
            nc.semaphore("dbgs2") as dbgs2,
            nc.Block() as blk,
        ):
            @blk.vector
            def _(v):
                v.memset(vt[:, :].rearrange("p (kc h d) -> p kc h d", kc=32, h=H)[:, :, :, 64:65],
                         1.0).then_inc(onesem, 1)

            @blk.gpsimd
            def _(g):
                g.dma_start(wv[:, :].rearrange("p (c n) -> p c n", c=8),
                            w_qkv[:, :, 2048:3072]).then_inc(dxg, 16)
                for ch in range(32):
                    if ch >= 2:
                        g.wait_ge(tT, 8 * (ch - 1))
                    g.dma_start(xb[:, (ch % 2) * D:(ch % 2 + 1) * D],
                                x_seq[128 * ch:128 * (ch + 1), :]).then_inc(dxg, 16)

            @blk.tensor
            def _(t):
                for ch in range(32):
                    par = ch % 2
                    t.wait_ge(dxg, max(48, 16 * (ch + 2)))
                    if ch >= 2:
                        t.wait_ge(eT, 8 * (ch - 1))
                    pt = pT if par == 0 else pT2
                    ins = None
                    for j in range(8):
                        ins = t.transpose(pt[:, 128 * j:128 * (j + 1)],
                                          xb[:, par * D + 128 * j:par * D + 128 * (j + 1)],
                                          c_idb[:, :])
                    ins.then_inc(tT, 8)
                    t.wait_ge(eT, 8 * (ch + 1))
                    if ch >= 2:
                        t.wait_ge(evv, 2 * (ch - 1))
                    for half in range(2):
                        ins = None
                        for c in range(8):
                            ins = t.matmul(
                                pv[2 * par + half][:, :],
                                xTc[:, (par * 8 + c) * 128:(par * 8 + c) * 128 + 128],
                                wv[:, c * 1024 + 512 * half:c * 1024 + 512 * half + 512],
                                start=(c == 0), stop=(c == 7))
                        ins.then_inc(mmv, 1)

            @blk.scalar
            def _(s):
                for ch in range(32):
                    par = ch % 2
                    s.wait_ge(tT, 8 * (ch + 1))
                    if ch >= 2:
                        s.wait_ge(mmv, 2 * (ch - 1))
                    pt = pT if par == 0 else pT2
                    ins = None
                    for j in range(8):
                        ins = s.copy(xTc[:, (par * 8 + j) * 128:(par * 8 + j) * 128 + 128],
                                     pt[:, 128 * j:128 * (j + 1)])
                    ins.then_inc(eT, 8)

            @blk.vector
            def _(v):
                v.wait_ge(onesem, 1)
                for ch in range(32):
                    par = ch % 2
                    v.wait_ge(mmv, 2 * (ch + 1))
                    dst = vt[:, :].rearrange("p (kc h d) -> p kc h d", kc=32, h=H)[:, ch, :, 0:64]
                    ins = v.tensor_copy(
                        dst[:, 0:8, :],
                        pv[2 * par][:, :].rearrange("p (h d) -> p h d", h=8))
                    ins = v.tensor_copy(
                        dst[:, 8:16, :],
                        pv[2 * par + 1][:, :].rearrange("p (h d) -> p h d", h=8))
                    ins.then_inc(evv, 2)

            @blk.sync
            def _(y):
                y.wait_ge(evv, 64)
                if DEBUG:
                    y.dma_start(dbg["vt"][:, :], vt[:, :]).then_inc(dbgs2, 16)
                    y.wait_ge(dbgs2, 16)

    nc.all_engine_barrier()

    if LASTPH == "p1":
        es1b.close(); es1.close(); es0.close()
        return nc

    with ExitStack() as esp:
        sbp = lambda n, s, d=F32: esp.enter_context(nc.sbuf_tensor(n, s, d))
        psp = lambda n, s, d=F32: esp.enter_context(nc.psum_tensor(n, s, d))
        sqt = sbp("sqt", [128, NS], BF16)
        nmx = sbp("nmx", [128, 8])
        qnq = sbp("qnq", [128, 8])
        qnk = sbp("qnk", [128, 8])
        prod = sbp("prod", [128, 8])
        hsel = sbp("hsel", [128, 256])
        mark2 = sbp("mark2", [1, 1])
        pns = [psp("pns0", [128, 512]), psp("pns1", [128, 512])]
        pcb = psp("pcb", [128, 2])

        with (
            nc.semaphore("vsq2") as vsq,
            nc.semaphore("tsm2") as tsm,
            nc.semaphore("vrd2") as vrdc,
            nc.semaphore("sch2") as sch,
            nc.semaphore("tbc2") as tbc,
            nc.semaphore("vcb2") as vcb,
            nc.semaphore("vself2") as vself2,
            nc.semaphore("sself2") as sself2,
            nc.Block() as blk,
        ):
            jobs = []
            for t in range(8):
                jobs.append((0, t, 4))
            for t in range(8):
                jobs.append((1, t, 8))

            @blk.vector
            def _(v):
                vs = SelfSem(v, vself2)
                v.memset(mark2[:, :], 0.0)
                v.memset(hsel[:, :], 0.0)
                vs.step(v.memset(hsel[0:64, 0:128], 1.0 / 64))
                vs.step(v.memset(hsel[64:128, 128:256], 1.0 / 64))
                vs.sync()
                v.tensor_copy(mark2[:, :], mark2[:, :]).then_inc(vcb, 1)
                nred = 0
                for j, (kind, t, nch) in enumerate(jobs):
                    srcb = qTb if kind == 0 else kTb
                    width = NQ if kind == 0 else NS
                    if j >= 1:
                        v.wait_ge(tsm, nred)
                    vs.step(v.tensor_tensor(sqt[:, 0:width],
                                            srcb[:, t * width:(t + 1) * width],
                                            srcb[:, t * width:(t + 1) * width],
                                            ALU.mult))
                    vs.sync()
                    v.tensor_copy(mark2[:, :], mark2[:, :]).then_inc(vsq, 1)
                    for ch in range(nch):
                        v.wait_ge(tsm, nred + ch + 1)
                        v.reduce_max(nmx[:, ch:ch + 1], pns[ch % 2][:, :],
                                     axis=AX.X).then_inc(vrdc, 1)
                    nred += nch
                    v.wait_ge(vrdc, nred)
                    dstcol = (qnq if kind == 0 else qnk)[:, t:t + 1]
                    vs.step(v.reduce_max(dstcol, nmx[:, 0:nch], axis=AX.X))
                    vs.sync()
                    if kind == 1:
                        vs.step(v.tensor_tensor(prod[:, t:t + 1], qnq[:, t:t + 1],
                                                qnk[:, t:t + 1], ALU.mult))
                        vs.sync()
                        v.tensor_copy(mark2[:, :], mark2[:, :]).then_inc(sch, 1)
                        v.wait_ge(tbc, t + 1)
                        vs.step(v.tensor_scalar(c_cb[:, 2 * t:2 * t + 1], pcb[:, 0:1],
                                                -1.0, None, ALU.mult))
                        vs.step(v.tensor_scalar(c_cb[:, 2 * t + 1:2 * t + 2], pcb[:, 1:2],
                                                -1.0, None, ALU.mult))
                        vs.sync()
                        v.tensor_copy(mark2[:, :], mark2[:, :]).then_inc(vcb, 1)

            @blk.tensor
            def _(t_):
                t_.wait_ge(vcb, 1)
                nred = 0
                for j, (kind, t, nch) in enumerate(jobs):
                    t_.wait_ge(vsq, j + 1)
                    for ch in range(nch):
                        if nred + ch >= 2:
                            t_.wait_ge(vrdc, nred + ch - 1)
                        t_.matmul(pns[ch % 2][:, :], c_hones[:, :],
                                  sqt[:, 512 * ch:512 * (ch + 1)],
                                  start=True, stop=True).then_inc(tsm, 1)
                    nred += nch
                    if kind == 1:
                        t_.wait_ge(sch, t + 1)
                        if t >= 1:
                            t_.wait_ge(vcb, 1 + t)
                        t_.matmul(pcb[:, 0:1], hsel[:, 0:128], prod[:, t:t + 1],
                                  start=True, stop=True)
                        t_.matmul(pcb[:, 1:2], hsel[:, 128:256], prod[:, t:t + 1],
                                  start=True, stop=True).then_inc(tbc, 1)

    nc.all_engine_barrier()
    with ExitStack() as esp:
        sbp = lambda n, s, d=F32: esp.enter_context(nc.sbuf_tensor(n, s, d))
        tmpa = sbp("tmpa", [128, H])
        with (
            nc.semaphore("fx1") as fx1,
            nc.semaphore("fx2") as fx2,
            nc.Block() as blk,
        ):
            @blk.vector
            def _(v):
                v.tensor_scalar(tmpa[:, :], c_cb[:, :], -1.0, None,
                                ALU.mult).then_inc(fx1, 1)
                v.wait_ge(fx2, 2)
                v.tensor_scalar(c_cb[:, :], tmpa[:, :], -0.125, None,
                                ALU.mult).then_inc(fx1, 1)

            @blk.scalar
            def _(s):
                s.wait_ge(fx1, 1)
                s.activation(tmpa[:, :], tmpa[:, :], AF.Ln, bias=0.0,
                             scale=1.0).then_inc(fx2, 1)
                s.wait_ge(fx2, 1)
                s.activation(tmpa[:, :], tmpa[:, :], AF.Exp, bias=0.0,
                             scale=0.5).then_inc(fx2, 1)

            @blk.sync
            def _(y):
                if DEBUG:
                    y.wait_ge(fx1, 2)
                    with nc.semaphore("dcb") as dcb:
                        y.dma_start(dbg["cb"][:, :], c_cb[:, :]).then_inc(dcb, 16)
                        y.wait_ge(dcb, 16)

    nc.all_engine_barrier()

    if LASTPH == "p2":
        es1b.close(); es1.close(); es0.close()
        return nc

    with ExitStack() as esp:
        sbp = lambda n, s, d=F32: esp.enter_context(nc.sbuf_tensor(n, s, d))
        psp = lambda n, s, d=F32: esp.enter_context(nc.psum_tensor(n, s, d))
        probs0 = sbp("probs0", [128, 2048], BF16)
        probs1 = sbp("probs1", [128, 2048], BF16)
        att0 = sbp("att0", [128, 1024], BF16)
        att1 = sbp("att1", [128, 1024], BF16)
        zpk = sbp("zpk", [128, 16], BF16)
        zpkf = sbp("zpkf", [128, 16])
        zpki = sbp("zpki", [128, 16])
        psc = [psp("psc0", [128, 1024]), psp("psc1", [128, 1024])]
        pE0 = psp("pE0", [128, 1024])
        pE1 = psp("pE1", [128, 1024])

        with (
            nc.semaphore("scm") as scm,
            nc.semaphore("exq") as exq,
            nc.semaphore("avm") as avm,
            nc.semaphore("rev") as rev,
            nc.semaphore("zpd") as zpd,
            nc.semaphore("zrc") as zrc,
            nc.semaphore("zud") as zud,
            nc.semaphore("std") as std,
            nc.semaphore("vself3") as vself3,
            nc.Block() as blk,
        ):
            NIT = 16
            NC_ = 32

            def it_hp(it):
                return it // 2, it % 2


            @blk.tensor
            def _(t_):
                def emit_attv(k):
                    it, c = k // NC_, k % NC_
                    hp, qc = it_hp(it)
                    for i in range(2):
                        t_.wait_ge(exq, 2 * k + i + 1)
                        if c == 0:
                            t_.wait_ge(rev, 2 * it)
                        pe = pE0 if i == 0 else pE1
                        pr = probs0 if i == 0 else probs1
                        ins = None
                        for n in range(2):
                            ins = t_.matmul(
                                pe[0:65, 512 * n:512 * (n + 1)],
                                vt[:, (c * H + 2 * hp + i) * 65:
                                   (c * H + 2 * hp + i) * 65 + 65],
                                pr[:, (c % 2) * 1024 + 512 * n:(c % 2) * 1024 + 512 * n + 512],
                                start=(c == 0), stop=(c == NC_ - 1))
                        ins.then_inc(avm, 1)

                for k in range(NIT * NC_):
                    it, c = k // NC_, k % NC_
                    hp, qc = it_hp(it)
                    for i in range(2):
                        if 2 * k + i >= 2:
                            t_.wait_ge(exq, 2 * k + i - 1)
                        ins = None
                        for n in range(2):
                            ins = t_.matmul(
                                psc[i][:, 512 * n:512 * (n + 1)],
                                kTb[64 * i:64 * i + 64,
                                    hp * NS + 128 * c:hp * NS + 128 * c + 128],
                                qTb[64 * i:64 * i + 64,
                                    hp * NQ + 1024 * qc + 512 * n:
                                    hp * NQ + 1024 * qc + 512 * n + 512],
                                start=True, stop=True,
                                tile_position=(64 * i, 0))
                        ins.then_inc(scm, 1)
                    if k >= 1:
                        emit_attv(k - 1)
                emit_attv(NIT * NC_ - 1)

            @blk.scalar
            def _(s):
                for it in range(NIT):
                    hp, qc = it_hp(it)
                    for c in range(NC_):
                        k = it * NC_ + c
                        for i in range(2):
                            h = 2 * hp + i
                            pr = probs0 if i == 0 else probs1
                            s.wait_ge(scm, 2 * k + i + 1)
                            if k >= 2:
                                s.wait_ge(avm, 2 * (k - 2) + i + 1)
                            s.activation(pr[:, (c % 2) * 1024:(c % 2) * 1024 + 1024],
                                         psc[i][:, :], AF.Exp,
                                         bias=c_cb[:, h:h + 1],
                                         scale=0.125).then_inc(exq, 1)

            @blk.vector
            def _(v):
                vs = SelfSem(v, vself3)
                for it in range(NIT):
                    hp, qc = it_hp(it)
                    v.wait_ge(avm, 64 * (it + 1))
                    if it >= 1:
                        v.wait_ge(std, 32 * it)
                        v.wait_ge(zpd, 32 * it)
                    v.tensor_copy(att0[0:65, :], pE0[0:65, :])
                    v.tensor_copy(att1[0:65, :], pE1[0:65, :]).then_inc(rev, 2)
                    v.wait_ge(zpd, 32 * it + 32)
                    vs.step(v.tensor_copy(zpkf[:, :], zpk[:, :]))
                    vs.sync()
                    v.reciprocal(zpki[:, :], zpkf[:, :]).then_inc(zrc, 1)

            @blk.gpsimd
            def _(g):
                for it in range(NIT):
                    g.wait_ge(rev, 2 * it + 2)
                    g.dma_start(zpk[:, 0:8], att0[64:65, :]).then_inc(zpd, 16)
                    g.dma_start(zpk[:, 8:16], att1[64:65, :]).then_inc(zpd, 16)
                    hp, qc = it_hp(it)
                    g.wait_ge(zrc, it + 1)
                    g.dma_start(zrball[2 * hp:2 * hp + 1, 1024 * qc:1024 * (qc + 1)],
                                zpki[:, 0:8]).then_inc(zud, 16)
                    g.dma_start(zrball[2 * hp + 1:2 * hp + 2, 1024 * qc:1024 * (qc + 1)],
                                zpki[:, 8:16]).then_inc(zud, 16)
                    for k in (2 * it, 2 * it + 1):
                        if k >= 1:
                            g.wait_ge(wcs, 16 * k)
                        if k < 16:
                            src_ = w1r[k // 2, :, :, :].rearrange("p c n -> p (c n)")
                        else:
                            src_ = w2r[(k - 16) // 2, :, :, :, :].rearrange("p e c n -> p (e c n)")
                        g.dma_start(wcb[:, :],
                                    src_[:, 2048 * (k & 1):2048 * (k & 1) + 2048]).then_inc(wcl, 16)

            @blk.sync
            def _(y):
                for it in range(NIT):
                    hp, qc = it_hp(it)
                    y.wait_ge(rev, 2 * it + 2)
                    y.dma_start(attn_scr[128 * hp:128 * hp + 64, 1024 * qc:1024 * (qc + 1)],
                                att0[0:64, :]).then_inc(std, 16)
                    y.dma_start(attn_scr[128 * hp + 64:128 * hp + 128, 1024 * qc:1024 * (qc + 1)],
                                att1[0:64, :]).then_inc(std, 16)
                    for k in (2 * it, 2 * it + 1):
                        y.wait_ge(wcl, 16 * (k + 1))
                        if k < 16:
                            dst_ = w1s[k // 2, :, 2048 * (k & 1):2048 * (k & 1) + 2048]
                        else:
                            dst_ = w2s[(k - 16) // 2, :, 2048 * (k & 1):2048 * (k & 1) + 2048]
                        y.dma_start(dst_, wcb[:, :]).then_inc(wcs, 16)
                y.wait_ge(std, 32 * NIT)
                y.wait_ge(zud, 32 * NIT)
                y.wait_ge(wcs, 512)
                if DEBUG:
                    with nc.semaphore("datn") as datn:
                        y.dma_start(dbg["attn"][:, :], attn_scr[:, :]).then_inc(datn, 16)
                        y.wait_ge(datn, 16)

    nc.all_engine_barrier()

    if LASTPH == "p3":
        es1b.close(); es1.close(); es0.close()
        return nc

    es1b.close()
    es1.close()

    es2 = ExitStack()
    x1b = es2.enter_context(nc.sbuf_tensor("x1b", [128, 8 * NQ], BF16))

    with ExitStack() as esp:
        sbp = lambda n, s, d=F32: esp.enter_context(nc.sbuf_tensor(n, s, d))
        psp = lambda n, s, d=F32: esp.enter_context(nc.psum_tensor(n, s, d))
        wpb = sbp("wpb", [128, 8 * 1024], BF16)
        attnb = sbp("attnb", [128, 2 * 8 * 512], BF16)
        xb2 = sbp("xb2", [128, 2 * D])
        xoT = sbp("xoT", [128, 8 * 512])
        x1raw = sbp("x1raw", [128, 8 * 512])
        x1rb = sbp("x1rb", [128, 2 * 512], BF16)
        sq4 = sbp("sq4", [128, 2 * 512], BF16)
        m1k = sbp("m1k", [128, 512])
        msq4 = sbp("msq4", [128, 512])
        var4 = sbp("var4", [128, 512])
        lv4 = sbp("lv4", [128, 512])
        rstd4 = sbp("rstd4", [128, 512])
        uu4 = sbp("uu4", [128, 512])
        x1f = sbp("x1f", [128, 8 * 512])
        mark4 = sbp("mark4", [1, 1])
        attnd = sbp("attnd", [128, 2 * 8 * 512], BF16)
        pTx = [psp("pTx0", [128, 512]), psp("pTx1", [128, 512])]
        pp = [psp("pp0", [128, 512]), psp("pp1", [128, 512])]
        pls = [psp("pls0", [128, 512]), psp("pls1", [128, 512])]
        pzb = [psp("pzb0", [128, 512]), psp("pzb1", [128, 512])]

        with (
            nc.semaphore("d4a") as d4a,
            nc.semaphore("d4g") as d4g,
            nc.semaphore("t4t") as t4t,
            nc.semaphore("e4") as e4,
            nc.semaphore("p4m") as p4m,
            nc.semaphore("r4") as r4,
            nc.semaphore("c4") as c4,
            nc.semaphore("s4m") as s4m,
            nc.semaphore("ch4") as ch4,
            nc.semaphore("sl4") as sl4,
            nc.semaphore("ap4") as ap4,
            nc.semaphore("st4") as st4,
            nc.semaphore("vs4") as vs4s,
            nc.semaphore("ss4") as ss4s,
            nc.semaphore("dv4") as dv4,
            nc.semaphore("zb4") as zb4,
            nc.semaphore("c4v") as c4v,
            nc.semaphore("sx1") as sx1,
            nc.Block() as blk,
        ):
            attn_v = attn_scr[:, :].rearrange("(f p) n -> p f n", p=128)
            x1_v = x1_scr[:, :].rearrange("(f p) n -> p f n", p=128)

            @blk.sync
            def _(y):
                for qc in range(4):
                    if qc >= 2:
                        y.wait_ge(dv4, 8 * (qc - 1))
                    y.dma_start(attnb[:, (qc % 2) * 4096:(qc % 2) * 4096 + 4096]
                                .rearrange("p (f n) -> p f n", f=8),
                                attn_v[:, :, 512 * qc:512 * (qc + 1)]).then_inc(d4a, 16)
                    y.wait_ge(ap4, 8 * (qc + 1))
                    y.dma_start(x1_v[:, :, 512 * qc:512 * (qc + 1)],
                                x1f[:, :].rearrange("p (f n) -> p f n", f=8)).then_inc(st4, 16)
                y.wait_ge(st4, 64)

            @blk.gpsimd
            def _(g):
                g.dma_start(wpb[:, :].rearrange("p (c n) -> p c n", c=8),
                            w_proj[:, :, :]).then_inc(d4g, 16)
                for ch in range(16):
                    if ch >= 2:
                        g.wait_ge(t4t, 8 * (ch - 1))
                    g.dma_start(xb2[:, (ch % 2) * D:(ch % 2 + 1) * D],
                                x_seq[128 * ch:128 * (ch + 1), :]).then_inc(d4g, 16)

            @blk.tensor
            def _(t_):
                for qc in range(4):
                    for ch in range(4 * qc, 4 * qc + 4):
                        t_.wait_ge(d4g, max(48, 16 * (ch + 2)))
                        for half in range(2):
                            bi = 2 * ch + half
                            if bi >= 2:
                                t_.wait_ge(e4, 4 * (bi - 1))
                            pt = pTx[bi % 2]
                            ins = None
                            for j in range(4 * half, 4 * half + 4):
                                ins = t_.transpose(pt[:, 128 * (j % 4):128 * (j % 4) + 128],
                                                   xb2[:, (ch % 2) * D + 128 * j:(ch % 2) * D + 128 * j + 128],
                                                   c_idf[:, :])
                            ins.then_inc(t4t, 4)
                    t_.wait_ge(d4a, 16 * (qc + 1))
                    for hp in range(8):
                        zi = qc * 8 + hp
                        if zi >= 2:
                            t_.wait_ge(dv4, zi - 1)
                        t_.matmul(pzb[zi % 2][:, :], c_zsel[:, 128 * hp:128 * hp + 128],
                                  zrball[:, 512 * qc:512 * (qc + 1)],
                                  start=True, stop=True).then_inc(zb4, 1)
                    t_.wait_ge(e4, 4 * (8 * qc + 8))
                    t_.wait_ge(dv4, 8 * (qc + 1))

                    def stat_mm(d_):
                        gi_ = qc * 8 + d_
                        sl2 = slice((gi_ % 2) * 512, (gi_ % 2) * 512 + 512)
                        if d_ == 0 and qc >= 1:
                            t_.wait_ge(ch4, qc)
                        t_.wait_ge(c4, gi_ + 1)
                        t_.wait_ge(c4v, gi_ + 1)
                        t_.matmul(pls[0][:, :], c_aones[:, :], x1rb[:, sl2],
                                  start=(d_ == 0), stop=(d_ == 7))
                        t_.matmul(pls[1][:, :], c_aones[:, :], sq4[:, sl2],
                                  start=(d_ == 0), stop=(d_ == 7)).then_inc(s4m, 1)

                    for dout in range(8):
                        gi = qc * 8 + dout
                        if gi >= 2:
                            t_.wait_ge(r4, gi - 1)
                        ins = None
                        for c in range(8):
                            ins = t_.matmul(pp[gi % 2][:, :],
                                            wpb[:, c * 1024 + 128 * dout:c * 1024 + 128 * dout + 128],
                                            attnd[:, (qc % 2) * 4096 + 512 * c:(qc % 2) * 4096 + 512 * c + 512],
                                            start=(c == 0), stop=(c == 7))
                        ins.then_inc(p4m, 1)
                        if dout >= 2:
                            stat_mm(dout - 2)
                    stat_mm(6)
                    stat_mm(7)

            @blk.vector
            def _(v):
                vs = SelfSem(v, vs4s)
                v.memset(mark4[:, :], 0.0)
                for qc in range(4):
                    for ch in range(4 * qc, 4 * qc + 4):
                        for half in range(2):
                            bi = 2 * ch + half
                            v.wait_ge(t4t, 4 * (bi + 1))
                            pt = pTx[bi % 2]
                            ins = None
                            for j in range(4 * half, 4 * half + 4):
                                ins = v.tensor_copy(
                                    xoT[:, j * 512 + 128 * (ch % 4):j * 512 + 128 * (ch % 4) + 128],
                                    pt[:, 128 * (j % 4):128 * (j % 4) + 128])
                            ins.then_inc(e4, 4)
                    if qc >= 2:
                        v.wait_ge(p4m, 8 * (qc - 1))
                    for hp in range(8):
                        zi = qc * 8 + hp
                        v.wait_ge(zb4, zi + 1)
                        v.tensor_tensor(attnd[:, (qc % 2) * 4096 + 512 * hp:(qc % 2) * 4096 + 512 * hp + 512],
                                        attnb[:, (qc % 2) * 4096 + 512 * hp:(qc % 2) * 4096 + 512 * hp + 512],
                                        pzb[zi % 2][:, :], ALU.mult).then_inc(dv4, 1)
                    for dout in range(8):
                        gi = qc * 8 + dout
                        v.wait_ge(p4m, gi + 1)
                        if gi >= 2:
                            v.wait_ge(s4m, gi - 1)
                        sl = slice(512 * dout, 512 * (dout + 1))
                        sl2 = slice((gi % 2) * 512, (gi % 2) * 512 + 512)
                        vs.step(v.tensor_tensor(x1raw[:, sl], pp[gi % 2][:, :],
                                                xoT[:, sl], ALU.add))
                        vs.sync()
                        v.tensor_copy(mark4[:, :], mark4[:, :]).then_inc(r4, 1)
                        v.tensor_tensor(sq4[:, sl2], x1raw[:, sl], x1raw[:, sl],
                                        ALU.mult).then_inc(c4v, 1)
                    v.wait_ge(s4m, 8 * (qc + 1))
                    vs.step(v.tensor_scalar(m1k[:, :], pls[0][:, :], 1.0 / D, None, ALU.mult))
                    vs.sync()
                    vs.step(v.tensor_tensor(msq4[:, :], m1k[:, :], m1k[:, :], ALU.mult))
                    vs.sync()
                    vs.step(v.scalar_tensor_tensor(var4[:, :], pls[1][:, :], 1.0 / D,
                                                   msq4[:, :], ALU.mult, ALU.subtract))
                    vs.sync()
                    v.tensor_copy(mark4[:, :], mark4[:, :]).then_inc(ch4, 1)
                    v.wait_ge(sl4, qc + 1)
                    if qc >= 1:
                        v.wait_ge(st4, 16 * qc)
                        v.wait_ge(sx1, 8 * qc)
                    for dout in range(8):
                        sl = slice(512 * dout, 512 * (dout + 1))
                        vs.step(v.tensor_tensor(uu4[:, :], x1raw[:, sl], m1k[:, :], ALU.subtract))
                        vs.sync()
                        vs.step(v.tensor_tensor(x1f[:, sl], uu4[:, :], rstd4[:, :], ALU.mult))
                        vs.sync()
                        v.tensor_copy(mark4[:, :], mark4[:, :]).then_inc(ap4, 1)

            @blk.scalar
            def _(s):
                ss = SelfSem(s, ss4s)
                for qc in range(4):
                    for dout in range(8):
                        gi = qc * 8 + dout
                        s.wait_ge(r4, gi + 1)
                        if gi >= 2:
                            s.wait_ge(s4m, gi - 1)
                        sl = slice(512 * dout, 512 * (dout + 1))
                        sl2 = slice((gi % 2) * 512, (gi % 2) * 512 + 512)
                        s.copy(x1rb[:, sl2], x1raw[:, sl]).then_inc(c4, 1)
                    s.wait_ge(ch4, qc + 1)
                    ss.step(s.activation(lv4[:, :], var4[:, :], AF.Ln,
                                         bias=c_eps[:, 0:1], scale=1.0))
                    ss.sync()
                    s.activation(rstd4[:, :], lv4[:, :], AF.Exp, bias=0.0,
                                 scale=-0.5).then_inc(sl4, 1)
                    for dout in range(8):
                        s.wait_ge(ap4, qc * 8 + dout + 1)
                        sl = slice(512 * dout, 512 * (dout + 1))
                        s.copy(x1b[:, dout * NQ + 512 * qc:dout * NQ + 512 * qc + 512],
                               x1f[:, sl]).then_inc(sx1, 1)

    nc.all_engine_barrier()

    if LASTPH == "p4":
        if DEBUG:
            with nc.semaphore("dx1") as dx1, nc.Block() as blk:
                @blk.sync
                def _(y):
                    y.dma_start(dbg["x1"][:, :], x1_scr[:, :]).then_inc(dx1, 16)
                    y.wait_ge(dx1, 16)
        es2.close(); es0.close()
        return nc

    with ExitStack() as esp:
        sbp = lambda n, s, d=F32: esp.enter_context(nc.sbuf_tensor(n, s, d))
        psp = lambda n, s, d=F32: esp.enter_context(nc.psum_tensor(n, s, d))
        c_rw = sbp("c_rw", [128, 64])
        c_b1 = sbp("c_b1", [128, 32])
        b2b = sbp("b2b", [8, D], BF16)
        x1fq = sbp("x1fq", [128, 8 * 512])
        lgT = sbp("lgT", [8, 512])
        ltok = sbp("ltok", [128, 32])
        mwork = sbp("mwork", [128, 8 * 10 * 4])
        maskt = sbp("maskt", [128, 32])
        maskTb = sbp("maskTb", [8, 512], BF16)
        w1b = sbp("w1b", [128, 2 * 8 * 512], BF16)
        w2b = sbp("w2b", [128, 2 * 4096], BF16)
        h1m = sbp("h1m", [128, 32 * 512], BF16)
        x2raw = sbp("x2raw", [128, 8 * 512])
        x2rb = sbp("x2rb", [128, 2 * 512], BF16)
        sq5 = sbp("sq5", [128, 2 * 512], BF16)
        m1k5 = sbp("m1k5", [128, 512])
        msq5 = sbp("msq5", [128, 512])
        var5 = sbp("var5", [128, 512])
        lv5 = sbp("lv5", [128, 512])
        rstd5 = sbp("rstd5", [128, 512])
        uu5 = sbp("uu5", [128, 512])
        outf = sbp("outf", [128, 8 * 512])
        outtok = sbp("outtok", [128, 4 * 1024])
        mark5 = sbp("mark5", [1, 1])
        pstat = [psp("pstat0", [128, 512]), psp("pstat1", [128, 512])]
        pmb = [psp("pmb0", [128, 512]), psp("pmb1", [128, 512])]
        pf1 = [psp("pf10", [128, 512]), psp("pf11", [128, 512])]
        pout = [psp("pout0", [128, 512]), psp("pout1", [128, 512])]

        sems = ExitStack()
        _sm = lambda n: sems.enter_context(nc.semaphore(n))
        d5c = _sm("d5c"); d5x = _sm("d5x"); d5w = _sm("d5w")
        rt5 = _sm("rt5"); rv5 = _sm("rv5"); rs5 = _sm("rs5")
        f1m = _sm("f1m"); g5 = _sm("g5"); h5 = _sm("h5")
        o5m = _sm("o5m"); r5 = _sm("r5"); c5 = _sm("c5")
        s5m = _sm("s5m"); ch5 = _sm("ch5"); sl5 = _sm("sl5")
        ap5 = _sm("ap5"); tr5 = _sm("tr5"); ev5 = _sm("ev5"); mb5m = _sm("mb5m")
        st5 = _sm("st5"); vs5s = _sm("vs5"); ss5s = _sm("ss5")
        c5v = _sm("c5v")
        with nc.Block() as blk:
            x1_v = x1_scr[:, :].rearrange("(f p) n -> p f n", p=128)

            @blk.sync
            def _(y):
                y.dma_start(c_rw[:, :].rearrange("p (c e) -> p c e", c=8),
                            rw[:, :, :]).then_inc(d5c, 16)
                y.dma_start(c_b1[:, :], b1c[:, :]).then_inc(d5c, 16)
                for qc in range(4):
                    if qc >= 1:
                        y.wait_ge(r5, 8 * qc)
                    y.dma_start(x1fq[:, :].rearrange("p (f n) -> p f n", f=8),
                                x1_v[:, :, 512 * qc:512 * (qc + 1)]).then_inc(d5x, 16)
                    for tc in range(4):
                        y.wait_ge(ev5, 32 * qc + 8 * (tc + 1))
                        y.dma_start(out[512 * qc + 128 * tc:512 * qc + 128 * tc + 128, :],
                                    outtok[:, 1024 * tc:1024 * (tc + 1)]).then_inc(st5, 16)
                y.wait_ge(st5, 16 * 16)

            @blk.gpsimd
            def _(g):
                g.dma_start(b2b[:, :], b2[:, :]).then_inc(d5w, 16)
                for qc in range(4):
                    for e in range(NE):
                        gi = qc * 8 + e
                        if gi >= 2:
                            g.wait_ge(f1m, 4 * (gi - 1))
                        g.dma_start(w1b[:, (gi % 2) * 4096:(gi % 2) * 4096 + 4096],
                                    w1s[e, :, :]).then_inc(d5w, 16)
                    for dt in range(8):
                        gj = qc * 8 + dt
                        if gj >= 2:
                            g.wait_ge(o5m, gj - 1)
                        g.dma_start(w2b[:, (gj % 2) * 4096:(gj % 2) * 4096 + 4096],
                                    w2s[dt, :, :]).then_inc(d5w, 16)

            @blk.tensor
            def _(t_):
                t_.wait_ge(d5c, 32)
                for qc in range(4):
                    t_.wait_ge(d5x, 16 * (qc + 1))
                    if qc >= 1:
                        t_.wait_ge(s5m, 8 * qc)
                        t_.wait_ge(ch5, qc)
                    ins = None
                    for c in range(8):
                        ins = t_.matmul(pstat[0][0:8, :], c_rw[:, 8 * c:8 * c + 8],
                                        x1fq[:, 512 * c:512 * (c + 1)],
                                        start=(c == 0), stop=(c == 7))
                    ins.then_inc(rt5, 1)
                    t_.wait_ge(rv5, 5 * qc + 1)
                    for tc in range(4):
                        t_.transpose(pstat[1][:, 32 * tc:32 * tc + 8],
                                     lgT[0:8, 128 * tc:128 * tc + 128],
                                     c_idf[0:8, 0:8]).then_inc(rt5, 1)
                    if qc >= 1:
                        t_.wait_ge(ev5, 32 * qc)
                    for e in range(NE):
                        gi = qc * 8 + e
                        t_.wait_ge(d5w, 16 * (2 + 16 * qc + e))
                        for j in range(4):
                            ji = gi * 4 + j
                            if ji >= 2:
                                t_.wait_ge(g5, ji - 1)
                            if e == 4 and j == 0:
                                t_.wait_ge(rv5, 5 * qc + 4)
                                t_.wait_ge(h5, 32 * qc)
                                for tc in range(4):
                                    t_.transpose(pmb[0][0:8, 128 * tc:128 * tc + 128],
                                                 maskt[:, 8 * tc:8 * tc + 8],
                                                 c_idf[:, :]).then_inc(rt5, 1)
                            ins = None
                            for c in range(8):
                                ins = t_.matmul(
                                    pf1[ji % 2][:, :],
                                    w1b[:, (gi % 2) * 4096 + 512 * c + 128 * j:
                                        (gi % 2) * 4096 + 512 * c + 128 * j + 128],
                                    x1b[:, c * NQ + 512 * qc:c * NQ + 512 * qc + 512],
                                    start=(c == 0), stop=(c == 7))
                            ins.then_inc(f1m, 1)
                    t_.wait_ge(rv5, 5 * qc + 5)
                    for e in range(NE):
                        gi = qc * 8 + e
                        t_.wait_ge(h5, 4 * max(0, gi - 1))
                        t_.matmul(pmb[gi % 2][:, :],
                                  c_emats[:, 128 * e:128 * e + 128],
                                  maskTb[:, :], start=True, stop=True).then_inc(mb5m, 1)
                    def stat_mm5(d_):
                        gi_ = qc * 8 + d_
                        sl2 = slice((gi_ % 2) * 512, (gi_ % 2) * 512 + 512)
                        if d_ == 0 and qc >= 1:
                            t_.wait_ge(ch5, qc)
                        t_.wait_ge(c5, gi_ + 1)
                        t_.wait_ge(c5v, gi_ + 1)
                        t_.matmul(pstat[0][:, :], c_aones[:, :], x2rb[:, sl2],
                                  start=(d_ == 0), stop=(d_ == 7))
                        t_.matmul(pstat[1][:, :], c_aones[:, :], sq5[:, sl2],
                                  start=(d_ == 0), stop=(d_ == 7)).then_inc(s5m, 1)

                    for dt in range(8):
                        gj = qc * 8 + dt
                        t_.wait_ge(d5w, 16 * (10 + 16 * qc + dt))
                        if gj >= 2:
                            t_.wait_ge(r5, gj - 1)
                        for e in range(NE):
                            t_.wait_ge(h5, qc * 32 + 4 * (e + 1))
                            for c in range(4):
                                t_.matmul(
                                    pout[gj % 2][:, :],
                                    w2b[:, (gj % 2) * 4096 + (e * 4 + c) * 128:
                                        (gj % 2) * 4096 + (e * 4 + c) * 128 + 128],
                                    h1m[:, (e * 4 + c) * 512:(e * 4 + c) * 512 + 512],
                                    start=(e == 0 and c == 0), stop=False)
                        t_.matmul(pout[gj % 2][:, :], b2b[:, 128 * dt:128 * dt + 128],
                                  maskTb[:, :], start=False, stop=True).then_inc(o5m, 1)
                        if dt >= 2:
                            stat_mm5(dt - 2)
                    stat_mm5(6)
                    stat_mm5(7)
                    t_.wait_ge(ap5, 8 * (qc + 1))
                    for tc in range(4):
                        ti = qc * 4 + tc
                        if ti >= 2:
                            t_.wait_ge(ev5, 8 * (ti - 1))
                        ins = None
                        for dtj in range(8):
                            ins = t_.transpose(pf1[tc % 2][:, 128 * (dtj % 4):128 * (dtj % 4) + 128]
                                               , outf[:, dtj * 512 + 128 * tc:dtj * 512 + 128 * tc + 128],
                                               c_idf[:, :])
                            if dtj == 3:
                                ins.then_inc(tr5, 4)
                                t_.wait_ge(ev5, 8 * ti + 4)
                        ins.then_inc(tr5, 4)

            @blk.vector
            def _(v):
                vs = SelfSem(v, vs5s)
                v.memset(mark5[:, :], 0.0)
                for qc in range(4):
                    v.wait_ge(rt5, 9 * qc + 1)
                    vs.step(v.tensor_copy(lgT[0:8, :], pstat[0][0:8, :]))
                    vs.sync()
                    v.tensor_copy(mark5[:, :], mark5[:, :]).then_inc(rv5, 1)
                    v.wait_ge(rt5, 9 * qc + 5)
                    for tc in range(4):
                        vs.step(v.tensor_tensor(ltok[:, 8 * tc:8 * tc + 8],
                                                pstat[1][:, 32 * tc:32 * tc + 8],
                                                c_rbias[:, :], ALU.add))
                    vs.sync()
                    v.tensor_copy(mark5[:, :], mark5[:, :]).then_inc(rv5, 1)
                    for tc in range(4):
                        lsl = ltok[:, 8 * tc:8 * tc + 8]
                        w0 = mwork[:, 80 * tc:80 * tc + 80]
                        m1 = w0[:, 0:1]; ge1 = w0[:, 8:16]; lm = w0[:, 16:24]
                        m2 = w0[:, 1:2]; ind = w0[:, 24:32]; nm1 = w0[:, 4:5]
                        vs.step(v.reduce_max(m1, lsl, axis=AX.X)); vs.sync()
                        vs.step(v.tensor_scalar(ge1, lsl, m1, None, ALU.is_ge)); vs.sync()
                        vs.step(v.scalar_tensor_tensor(lm, ge1, -1e30, lsl, ALU.mult, ALU.add)); vs.sync()
                        vs.step(v.reduce_max(m2, lm, axis=AX.X)); vs.sync()
                        vs.step(v.tensor_scalar(ind, lsl, m2, None, ALU.is_ge)); vs.sync()
                        vs.step(v.tensor_scalar(nm1, m1, -1.0, None, ALU.mult)); vs.sync()
                    v.tensor_copy(mark5[:, :], mark5[:, :]).then_inc(rv5, 1)
                    v.wait_ge(rs5, 4 * (qc + 1))
                    for tc in range(4):
                        w0 = mwork[:, 80 * tc:80 * tc + 80]
                        ind = w0[:, 24:32]; ex = w0[:, 32:40]
                        wv = w0[:, 40:48]; ssm = w0[:, 2:3]; rs = w0[:, 3:4]
                        vs.step(v.tensor_tensor(wv, ex, ind, ALU.mult)); vs.sync()
                        vs.step(v.reduce_sum(ssm, wv, axis=AX.X)); vs.sync()
                        vs.step(v.reciprocal(rs, ssm)); vs.sync()
                        vs.step(v.tensor_scalar(maskt[:, 8 * tc:8 * tc + 8], wv, rs, None,
                                                ALU.mult)); vs.sync()
                    v.tensor_copy(mark5[:, :], mark5[:, :]).then_inc(rv5, 1)
                    v.wait_ge(rt5, 9 * qc + 9)
                    for tc in range(4):
                        vs.step(v.tensor_copy(maskTb[0:8, 128 * tc:128 * tc + 128],
                                              pmb[0][0:8, 128 * tc:128 * tc + 128]))
                    vs.sync()
                    v.tensor_copy(mark5[:, :], mark5[:, :]).then_inc(rv5, 1)
                    for e in range(NE):
                        gi = qc * 8 + e
                        v.wait_ge(mb5m, qc * 8 + e + 1)
                        for j in range(4):
                            ji = gi * 4 + j
                            v.wait_ge(g5, ji + 1)
                            v.tensor_tensor(h1m[:, (e * 4 + j) * 512:(e * 4 + j) * 512 + 512],
                                            h1m[:, (e * 4 + j) * 512:(e * 4 + j) * 512 + 512],
                                            pmb[gi % 2][:, :], ALU.mult).then_inc(h5, 1)
                    for dt in range(8):
                        gj = qc * 8 + dt
                        v.wait_ge(o5m, gj + 1)
                        if gj >= 2:
                            v.wait_ge(s5m, gj - 1)
                        sl = slice(512 * dt, 512 * (dt + 1))
                        sl2 = slice((gj % 2) * 512, (gj % 2) * 512 + 512)
                        vs.step(v.tensor_tensor(x2raw[:, sl], pout[gj % 2][:, :],
                                                x1fq[:, sl], ALU.add))
                        vs.sync()
                        v.tensor_copy(mark5[:, :], mark5[:, :]).then_inc(r5, 1)
                        v.tensor_tensor(sq5[:, sl2], x2raw[:, sl], x2raw[:, sl],
                                        ALU.mult).then_inc(c5v, 1)
                    v.wait_ge(s5m, 8 * (qc + 1))
                    vs.step(v.tensor_scalar(m1k5[:, :], pstat[0][:, :], 1.0 / D, None, ALU.mult))
                    vs.sync()
                    vs.step(v.tensor_tensor(msq5[:, :], m1k5[:, :], m1k5[:, :], ALU.mult))
                    vs.sync()
                    vs.step(v.scalar_tensor_tensor(var5[:, :], pstat[1][:, :], 1.0 / D,
                                                   msq5[:, :], ALU.mult, ALU.subtract))
                    vs.sync()
                    v.tensor_copy(mark5[:, :], mark5[:, :]).then_inc(ch5, 1)
                    v.wait_ge(sl5, qc + 1)
                    if qc >= 1:
                        v.wait_ge(tr5, 32 * qc)
                    for dt in range(8):
                        sl = slice(512 * dt, 512 * (dt + 1))
                        vs.step(v.tensor_tensor(uu5[:, :], x2raw[:, sl], m1k5[:, :], ALU.subtract))
                        vs.sync()
                        vs.step(v.tensor_tensor(outf[:, sl], uu5[:, :], rstd5[:, :], ALU.mult))
                        vs.sync()
                        v.tensor_copy(mark5[:, :], mark5[:, :]).then_inc(ap5, 1)
                    for tc in range(4):
                        ti = qc * 4 + tc
                        v.wait_ge(tr5, 32 * qc + 8 * tc + 4)
                        if ti >= 1:
                            v.wait_ge(st5, 16 * ti - 16)
                        ins = None
                        for dtj in range(4):
                            ins = v.tensor_copy(outtok[:, 1024 * tc + 128 * dtj:1024 * tc + 128 * dtj + 128],
                                                pf1[tc % 2][:, 128 * dtj:128 * dtj + 128])
                        ins.then_inc(ev5, 4)
                        v.wait_ge(tr5, 32 * qc + 8 * tc + 8)
                        ins = None
                        for dtj in range(4, 8):
                            ins = v.tensor_copy(outtok[:, 1024 * tc + 128 * dtj:1024 * tc + 128 * dtj + 128],
                                                pf1[tc % 2][:, 128 * (dtj % 4):128 * (dtj % 4) + 128])
                        ins.then_inc(ev5, 4)

            @blk.scalar
            def _(s):
                ss = SelfSem(s, ss5s)
                for qc in range(4):
                    s.wait_ge(rv5, 5 * qc + 3)
                    for tc in range(4):
                        w0 = mwork[:, 80 * tc:80 * tc + 80]
                        nm1 = w0[:, 4:5]; ex = w0[:, 32:40]
                        s.activation(ex, ltok[:, 8 * tc:8 * tc + 8], AF.Exp,
                                     bias=nm1, scale=1.0).then_inc(rs5, 1)
                    for e in range(NE):
                        gi = qc * 8 + e
                        for j in range(4):
                            ji = gi * 4 + j
                            s.wait_ge(f1m, ji + 1)
                            s.activation(h1m[:, (e * 4 + j) * 512:(e * 4 + j) * 512 + 512],
                                         pf1[ji % 2][:, :], AF.Gelu,
                                         bias=c_b1[:, 4 * e + j:4 * e + j + 1],
                                         scale=1.0).then_inc(g5, 1)
                    for dt in range(8):
                        gj = qc * 8 + dt
                        s.wait_ge(r5, gj + 1)
                        if gj >= 2:
                            s.wait_ge(s5m, gj - 1)
                        sl = slice(512 * dt, 512 * (dt + 1))
                        sl2 = slice((gj % 2) * 512, (gj % 2) * 512 + 512)
                        s.copy(x2rb[:, sl2], x2raw[:, sl]).then_inc(c5, 1)
                    s.wait_ge(ch5, qc + 1)
                    ss.step(s.activation(lv5[:, :], var5[:, :], AF.Ln,
                                         bias=c_eps[:, 0:1], scale=1.0))
                    ss.sync()
                    s.activation(rstd5[:, :], lv5[:, :], AF.Exp, bias=0.0,
                                 scale=-0.5).then_inc(sl5, 1)

        sems.close()

    nc.all_engine_barrier()
    es2.close()
    es0.close()
    return nc



def _prep_shared(inputs):
    w_qkv = np.ascontiguousarray(
        inputs["L0_attn_w_qkv"].reshape(8, 128, 3 * D).transpose(1, 0, 2))
    w_proj = np.ascontiguousarray(
        inputs["L0_attn_w_proj"].reshape(8, 128, D).transpose(1, 0, 2))
    w1r = np.ascontiguousarray(
        inputs["L0_ffn_W1"].reshape(NE, 8, 128, DF).transpose(0, 2, 1, 3))
    w2r = np.ascontiguousarray(
        inputs["L0_ffn_W2"].reshape(NE, 4, 128, 8, 128).transpose(3, 2, 0, 1, 4))
    b1c = np.ascontiguousarray(
        inputs["L0_ffn_b1"].reshape(NE, 4, 128).transpose(2, 0, 1).reshape(128, NE * 4))
    rw = np.ascontiguousarray(
        inputs["L0_ffn_router_w"].reshape(8, 128, NE).transpose(1, 0, 2))
    rbias = np.ascontiguousarray(
        np.broadcast_to(inputs["L0_ffn_router_b"][None, :], (128, NE)))
    qg2 = np.ascontiguousarray(np.tile(inputs["L0_attn_q_gamma"], 2))[:, None]
    qb2 = np.ascontiguousarray(np.tile(inputs["L0_attn_q_beta"], 2))[:, None]
    kg2 = np.ascontiguousarray(np.tile(inputs["L0_attn_k_gamma"], 2))[:, None]
    kb2 = np.ascontiguousarray(np.tile(inputs["L0_attn_k_beta"], 2))[:, None]
    emats = np.zeros((8, NE * 128), np.float32)
    for e in range(NE):
        emats[e, e * 128:(e + 1) * 128] = 1.0
    zsel = np.zeros((16, 8 * 128), np.float32)
    for hp in range(8):
        zsel[2 * hp, 128 * hp:128 * hp + 64] = 1.0
        zsel[2 * hp + 1, 128 * hp + 64:128 * hp + 128] = 1.0
    shared = {
        "w_qkv": w_qkv, "w_proj": w_proj, "w1r": w1r, "w2r": w2r,
        "b1c": b1c, "b2": np.ascontiguousarray(inputs["L0_ffn_b2"]),
        "rw": rw, "rbias": rbias,
        "qg2": qg2, "qb2": qb2, "kg2": kg2, "kb2": kb2,
        "idf": np.eye(128, dtype=np.float32),
        "idb": np.eye(128, dtype=np.float32).astype(BF),
        "hones": np.kron(np.eye(2, dtype=np.float32), np.ones((64, 64), np.float32)).astype(BF),
        "aones": np.ones((128, 128), np.float32).astype(BF),
        "onesrow": np.ones((1, 128), np.float32).astype(BF),
        "onescolf": np.ones((1, 128), np.float32),
        "emats": emats.astype(BF),
        "zsel": zsel.astype(BF),
        "epscol": np.full((128, 1), EPS, np.float32),
    }
    return shared


def make_in_maps(inputs):
    shared = _prep_shared(inputs)
    x = inputs["L0_attn_x"]
    in_maps = []
    for core in range(8):
        b, hf = core // 2, core % 2
        own = x[b, hf * NQ:(hf + 1) * NQ]
        other = x[b, (1 - hf) * NQ:(2 - hf) * NQ]
        x_roll = np.ascontiguousarray(np.concatenate([own, other], axis=0))
        m = dict(shared)
        m["x_seq"] = x_roll
        in_maps.append(m)
    return in_maps


def kernel(**inputs):
    in_maps = make_in_maps(inputs)
    nc = build_nc()
    res = run_bass_kernel_spmd(nc, in_maps, core_ids=list(range(8)))
    outs = [np.asarray(r["out"]) for r in res.results]
    full = np.empty((4, NS, D), np.float32)
    for core in range(8):
        b, hf = core // 2, core % 2
        full[b, hf * NQ:(hf + 1) * NQ] = outs[core]
    return full


# version 18
# speedup vs baseline: 1.1088x; 1.0143x over previous
import os
import numpy as np
import ml_dtypes
from contextlib import ExitStack

import concourse.bass as bass
import concourse.mybir as mybir
from concourse.bass_utils import run_bass_kernel_spmd

F32 = mybir.dt.float32
BF16 = mybir.dt.bfloat16
AF = mybir.ActivationFunctionType
ALU = mybir.AluOpType
AX = mybir.AxisListType
BF = ml_dtypes.bfloat16

D = 1024
H = 16
DH = 64
NQ = 2048
NS = 4096
NE = 8
DF = 512
EPS = 1e-5

DEBUG = int(os.environ.get("KMK_DEBUG", "0"))
LASTPH = os.environ.get("KMK_LAST", "p5")


class SelfSem:
    def __init__(self, eng, sem):
        self.eng, self.sem, self.n = eng, sem, 0

    def step(self, ins):
        ins.then_inc(self.sem, 1)
        self.n += 1
        return ins

    def sync(self):
        self.eng.wait_ge(self.sem, self.n)


def build_nc():
    nc = bass.Bass("TRN2", target_bir_lowering=False,
                   detect_race_conditions=bool(int(os.environ.get("KMK_RACE", "0"))))

    x_seq = nc.dram_tensor("x_seq", [NS, D], F32, kind="ExternalInput")
    w_qkv = nc.dram_tensor("w_qkv", [128, 8, 3 * D], F32, kind="ExternalInput")
    w_proj = nc.dram_tensor("w_proj", [128, 8, D], F32, kind="ExternalInput")
    w1r = nc.dram_tensor("w1r", [NE, 128, 8, DF], F32, kind="ExternalInput")
    w2r = nc.dram_tensor("w2r", [8, 128, NE, 4, 128], F32, kind="ExternalInput")
    b1c = nc.dram_tensor("b1c", [128, NE * 4], F32, kind="ExternalInput")
    b2 = nc.dram_tensor("b2", [NE, D], F32, kind="ExternalInput")
    rw = nc.dram_tensor("rw", [128, 8, NE], F32, kind="ExternalInput")
    rbias = nc.dram_tensor("rbias", [128, NE], F32, kind="ExternalInput")
    qg2 = nc.dram_tensor("qg2", [128, 1], F32, kind="ExternalInput")
    qb2 = nc.dram_tensor("qb2", [128, 1], F32, kind="ExternalInput")
    kg2 = nc.dram_tensor("kg2", [128, 1], F32, kind="ExternalInput")
    kb2 = nc.dram_tensor("kb2", [128, 1], F32, kind="ExternalInput")
    idf = nc.dram_tensor("idf", [128, 128], F32, kind="ExternalInput")
    idb = nc.dram_tensor("idb", [128, 128], BF16, kind="ExternalInput")
    hones = nc.dram_tensor("hones", [128, 128], BF16, kind="ExternalInput")
    aones = nc.dram_tensor("aones", [128, 128], BF16, kind="ExternalInput")
    onesrow = nc.dram_tensor("onesrow", [1, 128], BF16, kind="ExternalInput")
    onescolf = nc.dram_tensor("onescolf", [1, 128], F32, kind="ExternalInput")
    emats = nc.dram_tensor("emats", [8, NE * 128], BF16, kind="ExternalInput")
    zsel = nc.dram_tensor("zsel", [16, 8 * 128], BF16, kind="ExternalInput")
    epscol = nc.dram_tensor("epscol", [128, 1], F32, kind="ExternalInput")

    out = nc.dram_tensor("out", [NQ, D], F32, kind="ExternalOutput")
    attn_scr = nc.dram_tensor("attn_scr", [D, NQ], BF16, kind="Internal")
    x1_scr = nc.dram_tensor("x1_scr", [D, NQ], F32, kind="Internal")
    w1s = nc.dram_tensor("w1s", [NE, 128, 8 * DF], BF16, kind="Internal")
    w2s = nc.dram_tensor("w2s", [8, 128, 32 * 128], BF16, kind="Internal")

    dbg = {}
    def dbg_t(name, shape, dt):
        if DEBUG:
            dbg[name] = nc.dram_tensor("dbg_" + name, shape, dt, kind="ExternalOutput")
    dbg_t("qTb", [128, 8 * NQ], BF16)
    dbg_t("kTb", [128, 8 * NS], BF16)
    dbg_t("vt", [128, 32 * H * 65], BF16)
    dbg_t("cb", [128, H], F32)
    dbg_t("attn", [D, NQ], BF16)
    dbg_t("x1", [D, NQ], F32)
    dbg_t("mask", [NQ, NE], F32)

    es0 = ExitStack()
    sb0 = lambda n, s, d=F32: es0.enter_context(nc.sbuf_tensor(n, s, d))
    wcl = es0.enter_context(nc.semaphore("wcl"))
    wcs = es0.enter_context(nc.semaphore("wcs"))
    wcb = sb0("wcb", [128, 2048], BF16)

    c_idf = sb0("c_idf", [128, 128])
    c_idb = sb0("c_idb", [128, 128], BF16)
    c_hones = sb0("c_hones", [128, 128], BF16)
    c_aones = sb0("c_aones", [128, 128], BF16)
    c_onesrow = sb0("c_onesrow", [1, 128], BF16)
    c_onescolf = sb0("c_onescolf", [1, 128])
    c_emats = sb0("c_emats", [8, NE * 128], BF16)
    c_qg2 = sb0("c_qg2", [128, 1]); c_qb2 = sb0("c_qb2", [128, 1])
    c_kg2 = sb0("c_kg2", [128, 1]); c_kb2 = sb0("c_kb2", [128, 1])
    c_eps = sb0("c_eps", [128, 1])
    c_rbias = sb0("c_rbias", [128, NE])
    c_cb = sb0("c_cb", [128, H])
    c_zsel = sb0("c_zsel", [16, 8 * 128], BF16)
    zrball = sb0("zrball", [16, NQ], BF16)

    with nc.semaphore("p0s") as p0s, nc.Block() as blk:
        @blk.sync
        def _(y):
            n = 0
            for dst, src in [
                (c_idf, idf), (c_idb, idb), (c_hones, hones), (c_aones, aones),
                (c_onesrow, onesrow), (c_onescolf, onescolf), (c_emats, emats),
                (c_qg2, qg2), (c_qb2, qb2), (c_kg2, kg2), (c_kb2, kb2),
                (c_eps, epscol), (c_rbias, rbias), (c_zsel, zsel),
            ]:
                y.dma_start(dst[:, :], src[:, :]).then_inc(p0s, 16)
                n += 16
            y.wait_ge(p0s, n)
    nc.all_engine_barrier()

    es1 = ExitStack()
    sb1 = lambda n, s, d=F32: es1.enter_context(nc.sbuf_tensor(n, s, d))
    qTb = sb1("qTb", [128, 8 * NQ], BF16)
    kTb = sb1("kTb", [128, 8 * NS], BF16)

    with ExitStack() as esp:
        sbp = lambda n, s, d=F32: esp.enter_context(nc.sbuf_tensor(n, s, d))
        psp = lambda n, s, d=F32: esp.enter_context(nc.psum_tensor(n, s, d))
        wqk = sbp("wqk", [128, 8 * 2048], BF16)
        xb = sbp("xb", [128, 2 * D], BF16)
        xT = sbp("xT", [128, 2 * 8 * 512], BF16)
        stageb = sbp("stageb", [128, 4 * 512], BF16)
        sqb = sbp("sqb", [128, 4 * 512], BF16)
        m64 = sbp("m64", [128, 4 * 512])
        msq = sbp("msq", [128, 2 * 512])
        varr = sbp("varr", [128, 2 * 512])
        lvr = sbp("lvr", [128, 2 * 512])
        rstd = sbp("rstd", [128, 4 * 512])
        uu = sbp("uu", [128, 2 * 512])
        ww = sbp("ww", [128, 2 * 512])
        mark1 = sbp("mark1", [1, 1])
        pT = psp("pT", [128, 1024], BF16)
        pT2 = psp("pT2", [128, 1024], BF16)
        pmm = [psp("pmm0", [128, 512]), psp("pmm1", [128, 512]),
               psp("pmm2", [128, 512]), psp("pmm3", [128, 512])]
        pst = [psp("pst0", [128, 512]), psp("pst1", [128, 512])]

        NCH, NG = 32, 8
        units = []
        for g in range(NG):
            for t in range(8):
                units.append((1, t, g))
            if g < 4:
                for t in range(8):
                    units.append((0, t, g))
        NU = len(units)
        ucum = [0] * (NG + 1)
        for g in range(NG):
            ucum[g + 1] = ucum[g] + (16 if g < 4 else 8)

        def unit_dst(kind, t, g):
            base = (qTb if kind == 0 else kTb)
            return base[:, t * (NQ if kind == 0 else NS) + 512 * g:
                        t * (NQ if kind == 0 else NS) + 512 * (g + 1)]

        with (
            nc.semaphore("dxg") as dxg,
            nc.semaphore("tT") as tT,
            nc.semaphore("eT") as eT,
            nc.semaphore("mmq") as mmq,
            nc.semaphore("evq") as evq,
            nc.semaphore("smq") as smq,
            nc.semaphore("dch") as dch,
            nc.semaphore("stl") as stl,
            nc.semaphore("apq") as apq,
            nc.semaphore("aplast") as aplast,
            nc.semaphore("evs") as evs,
            nc.semaphore("vself") as vself,
            nc.semaphore("sself") as sself,
            nc.semaphore("dbgs") as dbgs,
            nc.Block() as blk,
        ):
            @blk.gpsimd
            def _(g):
                g.dma_start(wqk[:, :].rearrange("p (c n) -> p c n", c=8),
                            w_qkv[:, :, 0:2048]).then_inc(dxg, 16)
                for ch in range(NCH):
                    if ch >= 2:
                        g.wait_ge(tT, 8 * (ch - 1))
                    g.dma_start(xb[:, (ch % 2) * D:(ch % 2 + 1) * D],
                                x_seq[128 * ch:128 * (ch + 1), :]).then_inc(dxg, 16)

            @blk.tensor
            def _(t):
                emitted = [0]

                def qkv_unit(u):
                    kind, tl, g = units[u]
                    if u >= 4:
                        t.wait_ge(aplast, u - 3)
                    ins = None
                    for c in range(8):
                        ins = t.matmul(
                            pmm[u % 4][:, :],
                            wqk[:, c * 2048 + kind * 1024 + 128 * tl:
                                c * 2048 + kind * 1024 + 128 * tl + 128],
                            xT[:, ((g % 2) * 8 + c) * 512:((g % 2) * 8 + c) * 512 + 512],
                            start=(c == 0), stop=(c == 7))
                    ins.then_inc(mmq, 1)

                def stat_unit(u):
                    t.wait_ge(evq, u + 1)
                    if u >= 1:
                        t.wait_ge(dch, u)
                    par = u % 4
                    t.matmul(pst[0][:, :], c_hones[:, :], stageb[:, 512 * par:512 * (par + 1)],
                             start=True, stop=True)
                    t.matmul(pst[1][:, :], c_hones[:, :], sqb[:, 512 * par:512 * (par + 1)],
                             start=True, stop=True).then_inc(smq, 1)

                u = 0
                for g in range(NG):
                    for ch in range(4 * g, 4 * g + 4):
                        t.wait_ge(dxg, max(48, 16 * (ch + 2)))
                        if ch >= 2:
                            t.wait_ge(eT, 8 * (ch - 1))
                        pt = pT if ch % 2 == 0 else pT2
                        ins = None
                        for j in range(8):
                            ins = t.transpose(pt[:, 128 * j:128 * (j + 1)],
                                              xb[:, (ch % 2) * D + 128 * j:(ch % 2) * D + 128 * (j + 1)],
                                              c_idb[:, :])
                        ins.then_inc(tT, 8)
                    t.wait_ge(eT, 8 * (4 * g + 4))
                    for _i in range(ucum[g + 1] - ucum[g]):
                        qkv_unit(u)
                        if u > ucum[g]:
                            stat_unit(u - 1)
                        u += 1
                    stat_unit(u - 1)

            @blk.scalar
            def _(s):
                ss = SelfSem(s, sself)
                for g in range(NG):
                    for ch in range(4 * g, 4 * g + 4):
                        s.wait_ge(tT, 8 * (ch + 1))
                        if g >= 2:
                            s.wait_ge(mmq, ucum[g - 1])
                        pt = pT if ch % 2 == 0 else pT2
                        ins = None
                        for j in range(8):
                            ins = s.copy(
                                xT[:, ((g % 2) * 8 + j) * 512 + 128 * (ch % 4):
                                   ((g % 2) * 8 + j) * 512 + 128 * (ch % 4) + 128],
                                pt[:, 128 * j:128 * (j + 1)])
                        ins.then_inc(eT, 8)
                    def sevac(u):
                        p4e = u % 4
                        s.wait_ge(mmq, u + 1)
                        if u >= 4:
                            s.wait_ge(smq, u - 3)
                        s.copy(stageb[:, 512 * p4e:512 * (p4e + 1)],
                               pmm[p4e][:, :]).then_inc(evs, 1)

                    def schain(u):
                        p4_ = u % 4
                        p2_ = u % 2
                        s.wait_ge(dch, u + 1)
                        ins = s.activation(lvr[:, 512 * p2_:512 * (p2_ + 1)],
                                           varr[:, 512 * p2_:512 * (p2_ + 1)],
                                           AF.Ln, bias=c_eps[:, 0:1], scale=1.0)
                        ss.step(ins)
                        ss.sync()
                        s.activation(rstd[:, 512 * p4_:512 * (p4_ + 1)],
                                     lvr[:, 512 * p2_:512 * (p2_ + 1)],
                                     AF.Exp, bias=0.0, scale=-0.5).then_inc(stl, 1)

                    st_, en_ = ucum[g], ucum[g + 1]
                    for u in range(st_, en_):
                        sevac(u)
                        if u >= st_ + 2:
                            schain(u - 2)
                    schain(en_ - 2)
                    schain(en_ - 1)

            @blk.vector
            def _(v):
                vs = SelfSem(v, vself)
                v.memset(mark1[:, :], 0.0)

                def ph1(u):
                    p4_ = u % 4
                    s4 = slice(512 * p4_, 512 * (p4_ + 1))
                    v.wait_ge(evs, u + 1)
                    v.tensor_tensor(sqb[:, s4], stageb[:, s4], stageb[:, s4],
                                    ALU.mult).then_inc(evq, 1)

                def ph2(u):
                    p4_ = u % 4
                    p2_ = u % 2
                    s4 = slice(512 * p4_, 512 * (p4_ + 1))
                    s2 = slice(512 * p2_, 512 * (p2_ + 1))
                    v.wait_ge(smq, u + 1)
                    vs.step(v.tensor_scalar(m64[:, s4], pst[0][:, :], 1.0 / DH, None, ALU.mult))
                    vs.sync()
                    vs.step(v.tensor_tensor(msq[:, s2], m64[:, s4], m64[:, s4], ALU.mult))
                    vs.sync()
                    v.scalar_tensor_tensor(varr[:, s2], pst[1][:, :], 1.0 / DH, msq[:, s2],
                                           ALU.mult, ALU.subtract).then_inc(dch, 1)
                    vs.step(v.tensor_tensor(uu[:, s2], pmm[p4_][:, :], m64[:, s4], ALU.subtract))
                    vs.sync()
                    v.tensor_copy(mark1[:, :], mark1[:, :]).then_inc(aplast, 1)

                def ph3(u):
                    kind, tl, g = units[u]
                    p4_ = u % 4
                    p2_ = u % 2
                    s4 = slice(512 * p4_, 512 * (p4_ + 1))
                    s2 = slice(512 * p2_, 512 * (p2_ + 1))
                    v.wait_ge(stl, u + 1)
                    vs.step(v.tensor_tensor(ww[:, s2], uu[:, s2], rstd[:, s4], ALU.mult))
                    vs.sync()
                    gcol = c_qg2 if kind == 0 else c_kg2
                    bcol = c_qb2 if kind == 0 else c_kb2
                    v.tensor_scalar(unit_dst(kind, tl, g), ww[:, s2],
                                    gcol[:, 0:1], bcol[:, 0:1],
                                    ALU.mult, ALU.add).then_inc(apq, 1)

                for g in range(NG):
                    st_, en_ = ucum[g], ucum[g + 1]
                    for u in range(st_, en_):
                        ph1(u)
                        if u >= st_ + 1:
                            ph2(u - 1)
                        if u >= st_ + 2:
                            ph3(u - 2)
                    ph2(en_ - 1)
                    ph3(en_ - 2)
                    ph3(en_ - 1)

            @blk.sync
            def _(y):
                y.wait_ge(apq, NU)
                if DEBUG:
                    y.dma_start(dbg["qTb"][:, :], qTb[:, :]).then_inc(dbgs, 16)
                    y.dma_start(dbg["kTb"][:, :], kTb[:, :]).then_inc(dbgs, 16)
                    y.wait_ge(dbgs, 32)

    nc.all_engine_barrier()

    es1b = ExitStack()
    vt = es1b.enter_context(nc.sbuf_tensor("vt", [128, 32 * H * 65], BF16))

    with ExitStack() as esp:
        sbp = lambda n, s, d=F32: esp.enter_context(nc.sbuf_tensor(n, s, d))
        psp = lambda n, s, d=F32: esp.enter_context(nc.psum_tensor(n, s, d))
        wv = sbp("wv", [128, 8 * 1024], BF16)
        xb = sbp("xbv", [128, 2 * D], BF16)
        xTc = sbp("xTc", [128, 2 * 8 * 128], BF16)
        pT = psp("pTv", [128, 1024], BF16)
        pT2 = psp("pTv2", [128, 1024], BF16)
        pv = [psp("pv0", [128, 512]), psp("pv1", [128, 512]),
              psp("pv2", [128, 512]), psp("pv3", [128, 512])]

        with (
            nc.semaphore("dxg2") as dxg,
            nc.semaphore("tT2") as tT,
            nc.semaphore("eT2") as eT,
            nc.semaphore("mmv") as mmv,
            nc.semaphore("evv") as evv,
            nc.semaphore("ones") as onesem,
            nc.semaphore("dbgs2") as dbgs2,
            nc.Block() as blk,
        ):
            @blk.vector
            def _(v):
                v.memset(vt[:, :].rearrange("p (kc h d) -> p kc h d", kc=32, h=H)[:, :, :, 64:65],
                         1.0).then_inc(onesem, 1)

            @blk.gpsimd
            def _(g):
                g.dma_start(wv[:, :].rearrange("p (c n) -> p c n", c=8),
                            w_qkv[:, :, 2048:3072]).then_inc(dxg, 16)
                for ch in range(32):
                    if ch >= 2:
                        g.wait_ge(tT, 8 * (ch - 1))
                    g.dma_start(xb[:, (ch % 2) * D:(ch % 2 + 1) * D],
                                x_seq[128 * ch:128 * (ch + 1), :]).then_inc(dxg, 16)

            @blk.tensor
            def _(t):
                for ch in range(32):
                    par = ch % 2
                    t.wait_ge(dxg, max(48, 16 * (ch + 2)))
                    if ch >= 2:
                        t.wait_ge(eT, 8 * (ch - 1))
                    pt = pT if par == 0 else pT2
                    ins = None
                    for j in range(8):
                        ins = t.transpose(pt[:, 128 * j:128 * (j + 1)],
                                          xb[:, par * D + 128 * j:par * D + 128 * (j + 1)],
                                          c_idb[:, :])
                    ins.then_inc(tT, 8)
                    t.wait_ge(eT, 8 * (ch + 1))
                    if ch >= 2:
                        t.wait_ge(evv, 2 * (ch - 1))
                    for half in range(2):
                        ins = None
                        for c in range(8):
                            ins = t.matmul(
                                pv[2 * par + half][:, :],
                                xTc[:, (par * 8 + c) * 128:(par * 8 + c) * 128 + 128],
                                wv[:, c * 1024 + 512 * half:c * 1024 + 512 * half + 512],
                                start=(c == 0), stop=(c == 7))
                        ins.then_inc(mmv, 1)

            @blk.scalar
            def _(s):
                for ch in range(32):
                    par = ch % 2
                    s.wait_ge(tT, 8 * (ch + 1))
                    if ch >= 2:
                        s.wait_ge(mmv, 2 * (ch - 1))
                    pt = pT if par == 0 else pT2
                    s.copy(xTc[:, par * 1024:par * 1024 + 1024],
                           pt[:, :]).then_inc(eT, 8)

            @blk.vector
            def _(v):
                v.wait_ge(onesem, 1)
                for ch in range(32):
                    par = ch % 2
                    v.wait_ge(mmv, 2 * (ch + 1))
                    dst = vt[:, :].rearrange("p (kc h d) -> p kc h d", kc=32, h=H)[:, ch, :, 0:64]
                    ins = v.tensor_copy(
                        dst[:, 0:8, :],
                        pv[2 * par][:, :].rearrange("p (h d) -> p h d", h=8))
                    ins = v.tensor_copy(
                        dst[:, 8:16, :],
                        pv[2 * par + 1][:, :].rearrange("p (h d) -> p h d", h=8))
                    ins.then_inc(evv, 2)

            @blk.sync
            def _(y):
                y.wait_ge(evv, 64)
                if DEBUG:
                    y.dma_start(dbg["vt"][:, :], vt[:, :]).then_inc(dbgs2, 16)
                    y.wait_ge(dbgs2, 16)

    nc.all_engine_barrier()

    if LASTPH == "p1":
        es1b.close(); es1.close(); es0.close()
        return nc

    with ExitStack() as esp:
        sbp = lambda n, s, d=F32: esp.enter_context(nc.sbuf_tensor(n, s, d))
        psp = lambda n, s, d=F32: esp.enter_context(nc.psum_tensor(n, s, d))
        sqt = sbp("sqt", [128, NS], BF16)
        nmx = sbp("nmx", [128, 8])
        qnq = sbp("qnq", [128, 8])
        qnk = sbp("qnk", [128, 8])
        prod = sbp("prod", [128, 8])
        hsel = sbp("hsel", [128, 256])
        mark2 = sbp("mark2", [1, 1])
        pns = [psp("pns0", [128, 512]), psp("pns1", [128, 512])]
        pcb = psp("pcb", [128, 2])

        with (
            nc.semaphore("vsq2") as vsq,
            nc.semaphore("tsm2") as tsm,
            nc.semaphore("vrd2") as vrdc,
            nc.semaphore("sch2") as sch,
            nc.semaphore("tbc2") as tbc,
            nc.semaphore("vcb2") as vcb,
            nc.semaphore("vself2") as vself2,
            nc.semaphore("sself2") as sself2,
            nc.Block() as blk,
        ):
            jobs = []
            for t in range(8):
                jobs.append((0, t, 4))
            for t in range(8):
                jobs.append((1, t, 8))

            @blk.vector
            def _(v):
                vs = SelfSem(v, vself2)
                v.memset(mark2[:, :], 0.0)
                v.memset(hsel[:, :], 0.0)
                vs.step(v.memset(hsel[0:64, 0:128], 1.0 / 64))
                vs.step(v.memset(hsel[64:128, 128:256], 1.0 / 64))
                vs.sync()
                v.tensor_copy(mark2[:, :], mark2[:, :]).then_inc(vcb, 1)
                nred = 0
                for j, (kind, t, nch) in enumerate(jobs):
                    srcb = qTb if kind == 0 else kTb
                    width = NQ if kind == 0 else NS
                    if j >= 1:
                        v.wait_ge(tsm, nred)
                    vs.step(v.tensor_tensor(sqt[:, 0:width],
                                            srcb[:, t * width:(t + 1) * width],
                                            srcb[:, t * width:(t + 1) * width],
                                            ALU.mult))
                    vs.sync()
                    v.tensor_copy(mark2[:, :], mark2[:, :]).then_inc(vsq, 1)
                    for ch in range(nch):
                        v.wait_ge(tsm, nred + ch + 1)
                        v.reduce_max(nmx[:, ch:ch + 1], pns[ch % 2][:, :],
                                     axis=AX.X).then_inc(vrdc, 1)
                    nred += nch
                    v.wait_ge(vrdc, nred)
                    dstcol = (qnq if kind == 0 else qnk)[:, t:t + 1]
                    vs.step(v.reduce_max(dstcol, nmx[:, 0:nch], axis=AX.X))
                    vs.sync()
                    if kind == 1:
                        vs.step(v.tensor_tensor(prod[:, t:t + 1], qnq[:, t:t + 1],
                                                qnk[:, t:t + 1], ALU.mult))
                        vs.sync()
                        v.tensor_copy(mark2[:, :], mark2[:, :]).then_inc(sch, 1)
                        v.wait_ge(tbc, t + 1)
                        vs.step(v.tensor_scalar(c_cb[:, 2 * t:2 * t + 1], pcb[:, 0:1],
                                                -1.0, None, ALU.mult))
                        vs.step(v.tensor_scalar(c_cb[:, 2 * t + 1:2 * t + 2], pcb[:, 1:2],
                                                -1.0, None, ALU.mult))
                        vs.sync()
                        v.tensor_copy(mark2[:, :], mark2[:, :]).then_inc(vcb, 1)

            @blk.tensor
            def _(t_):
                t_.wait_ge(vcb, 1)
                nred = 0
                for j, (kind, t, nch) in enumerate(jobs):
                    t_.wait_ge(vsq, j + 1)
                    for ch in range(nch):
                        if nred + ch >= 2:
                            t_.wait_ge(vrdc, nred + ch - 1)
                        t_.matmul(pns[ch % 2][:, :], c_hones[:, :],
                                  sqt[:, 512 * ch:512 * (ch + 1)],
                                  start=True, stop=True).then_inc(tsm, 1)
                    nred += nch
                    if kind == 1:
                        t_.wait_ge(sch, t + 1)
                        if t >= 1:
                            t_.wait_ge(vcb, 1 + t)
                        t_.matmul(pcb[:, 0:1], hsel[:, 0:128], prod[:, t:t + 1],
                                  start=True, stop=True)
                        t_.matmul(pcb[:, 1:2], hsel[:, 128:256], prod[:, t:t + 1],
                                  start=True, stop=True).then_inc(tbc, 1)

    nc.all_engine_barrier()
    with ExitStack() as esp:
        sbp = lambda n, s, d=F32: esp.enter_context(nc.sbuf_tensor(n, s, d))
        tmpa = sbp("tmpa", [128, H])
        with (
            nc.semaphore("fx1") as fx1,
            nc.semaphore("fx2") as fx2,
            nc.Block() as blk,
        ):
            @blk.vector
            def _(v):
                v.tensor_scalar(tmpa[:, :], c_cb[:, :], -1.0, None,
                                ALU.mult).then_inc(fx1, 1)
                v.wait_ge(fx2, 2)
                v.tensor_scalar(c_cb[:, :], tmpa[:, :], -0.125, None,
                                ALU.mult).then_inc(fx1, 1)

            @blk.scalar
            def _(s):
                s.wait_ge(fx1, 1)
                s.activation(tmpa[:, :], tmpa[:, :], AF.Ln, bias=0.0,
                             scale=1.0).then_inc(fx2, 1)
                s.wait_ge(fx2, 1)
                s.activation(tmpa[:, :], tmpa[:, :], AF.Exp, bias=0.0,
                             scale=0.5).then_inc(fx2, 1)

            @blk.sync
            def _(y):
                if DEBUG:
                    y.wait_ge(fx1, 2)
                    with nc.semaphore("dcb") as dcb:
                        y.dma_start(dbg["cb"][:, :], c_cb[:, :]).then_inc(dcb, 16)
                        y.wait_ge(dcb, 16)

    nc.all_engine_barrier()

    if LASTPH == "p2":
        es1b.close(); es1.close(); es0.close()
        return nc

    with ExitStack() as esp:
        sbp = lambda n, s, d=F32: esp.enter_context(nc.sbuf_tensor(n, s, d))
        psp = lambda n, s, d=F32: esp.enter_context(nc.psum_tensor(n, s, d))
        probs0 = sbp("probs0", [128, 2048], BF16)
        probs1 = sbp("probs1", [128, 2048], BF16)
        att0 = sbp("att0", [128, 1024], BF16)
        att1 = sbp("att1", [128, 1024], BF16)
        zpk = sbp("zpk", [128, 16], BF16)
        zpkf = sbp("zpkf", [128, 16])
        zpki = sbp("zpki", [128, 16])
        psc = [psp("psc0", [128, 1024]), psp("psc1", [128, 1024])]
        pE0 = psp("pE0", [128, 1024])
        pE1 = psp("pE1", [128, 1024])

        with (
            nc.semaphore("scm") as scm,
            nc.semaphore("exq") as exq,
            nc.semaphore("avm") as avm,
            nc.semaphore("rev") as rev,
            nc.semaphore("zpd") as zpd,
            nc.semaphore("zrc") as zrc,
            nc.semaphore("zud") as zud,
            nc.semaphore("std") as std,
            nc.semaphore("vself3") as vself3,
            nc.Block() as blk,
        ):
            NIT = 16
            NC_ = 32

            def it_hp(it):
                return it // 2, it % 2


            @blk.tensor
            def _(t_):
                def emit_attv(k):
                    it, c = k // NC_, k % NC_
                    hp, qc = it_hp(it)
                    for i in range(2):
                        t_.wait_ge(exq, 2 * k + i + 1)
                        if c == 0:
                            t_.wait_ge(rev, 2 * it)
                        pe = pE0 if i == 0 else pE1
                        pr = probs0 if i == 0 else probs1
                        ins = None
                        for n in range(2):
                            ins = t_.matmul(
                                pe[0:65, 512 * n:512 * (n + 1)],
                                vt[:, (c * H + 2 * hp + i) * 65:
                                   (c * H + 2 * hp + i) * 65 + 65],
                                pr[:, (c % 2) * 1024 + 512 * n:(c % 2) * 1024 + 512 * n + 512],
                                start=(c == 0), stop=(c == NC_ - 1))
                        ins.then_inc(avm, 1)

                for k in range(NIT * NC_):
                    it, c = k // NC_, k % NC_
                    hp, qc = it_hp(it)
                    for i in range(2):
                        if 2 * k + i >= 2:
                            t_.wait_ge(exq, 2 * k + i - 1)
                        ins = None
                        for n in range(2):
                            ins = t_.matmul(
                                psc[i][:, 512 * n:512 * (n + 1)],
                                kTb[64 * i:64 * i + 64,
                                    hp * NS + 128 * c:hp * NS + 128 * c + 128],
                                qTb[64 * i:64 * i + 64,
                                    hp * NQ + 1024 * qc + 512 * n:
                                    hp * NQ + 1024 * qc + 512 * n + 512],
                                start=True, stop=True,
                                tile_position=(64 * i, 0))
                        ins.then_inc(scm, 1)
                    if k >= 1:
                        emit_attv(k - 1)
                emit_attv(NIT * NC_ - 1)

            @blk.scalar
            def _(s):
                for it in range(NIT):
                    hp, qc = it_hp(it)
                    for c in range(NC_):
                        k = it * NC_ + c
                        for i in range(2):
                            h = 2 * hp + i
                            pr = probs0 if i == 0 else probs1
                            s.wait_ge(scm, 2 * k + i + 1)
                            if k >= 2:
                                s.wait_ge(avm, 2 * (k - 2) + i + 1)
                            s.activation(pr[:, (c % 2) * 1024:(c % 2) * 1024 + 1024],
                                         psc[i][:, :], AF.Exp,
                                         bias=c_cb[:, h:h + 1],
                                         scale=0.125).then_inc(exq, 1)

            @blk.vector
            def _(v):
                vs = SelfSem(v, vself3)
                for it in range(NIT):
                    hp, qc = it_hp(it)
                    v.wait_ge(avm, 64 * (it + 1))
                    if it >= 1:
                        v.wait_ge(std, 32 * it)
                        v.wait_ge(zpd, 32 * it)
                    v.tensor_copy(att0[0:65, :], pE0[0:65, :])
                    v.tensor_copy(att1[0:65, :], pE1[0:65, :]).then_inc(rev, 2)
                    v.wait_ge(zpd, 32 * it + 32)
                    vs.step(v.tensor_copy(zpkf[:, :], zpk[:, :]))
                    vs.sync()
                    v.reciprocal(zpki[:, :], zpkf[:, :]).then_inc(zrc, 1)

            @blk.gpsimd
            def _(g):
                for it in range(NIT):
                    g.wait_ge(rev, 2 * it + 2)
                    g.dma_start(zpk[:, 0:8], att0[64:65, :]).then_inc(zpd, 16)
                    g.dma_start(zpk[:, 8:16], att1[64:65, :]).then_inc(zpd, 16)
                    hp, qc = it_hp(it)
                    g.wait_ge(zrc, it + 1)
                    g.dma_start(zrball[2 * hp:2 * hp + 1, 1024 * qc:1024 * (qc + 1)],
                                zpki[:, 0:8]).then_inc(zud, 16)
                    g.dma_start(zrball[2 * hp + 1:2 * hp + 2, 1024 * qc:1024 * (qc + 1)],
                                zpki[:, 8:16]).then_inc(zud, 16)
                    for k in (2 * it, 2 * it + 1):
                        if k >= 1:
                            g.wait_ge(wcs, 16 * k)
                        if k < 16:
                            src_ = w1r[k // 2, :, :, :].rearrange("p c n -> p (c n)")
                        else:
                            src_ = w2r[(k - 16) // 2, :, :, :, :].rearrange("p e c n -> p (e c n)")
                        g.dma_start(wcb[:, :],
                                    src_[:, 2048 * (k & 1):2048 * (k & 1) + 2048]).then_inc(wcl, 16)

            @blk.sync
            def _(y):
                for it in range(NIT):
                    hp, qc = it_hp(it)
                    y.wait_ge(rev, 2 * it + 2)
                    y.dma_start(attn_scr[128 * hp:128 * hp + 64, 1024 * qc:1024 * (qc + 1)],
                                att0[0:64, :]).then_inc(std, 16)
                    y.dma_start(attn_scr[128 * hp + 64:128 * hp + 128, 1024 * qc:1024 * (qc + 1)],
                                att1[0:64, :]).then_inc(std, 16)
                    for k in (2 * it, 2 * it + 1):
                        y.wait_ge(wcl, 16 * (k + 1))
                        if k < 16:
                            dst_ = w1s[k // 2, :, 2048 * (k & 1):2048 * (k & 1) + 2048]
                        else:
                            dst_ = w2s[(k - 16) // 2, :, 2048 * (k & 1):2048 * (k & 1) + 2048]
                        y.dma_start(dst_, wcb[:, :]).then_inc(wcs, 16)
                y.wait_ge(std, 32 * NIT)
                y.wait_ge(zud, 32 * NIT)
                y.wait_ge(wcs, 512)
                if DEBUG:
                    with nc.semaphore("datn") as datn:
                        y.dma_start(dbg["attn"][:, :], attn_scr[:, :]).then_inc(datn, 16)
                        y.wait_ge(datn, 16)

    nc.all_engine_barrier()

    if LASTPH == "p3":
        es1b.close(); es1.close(); es0.close()
        return nc

    es1b.close()
    es1.close()

    es2 = ExitStack()
    x1b = es2.enter_context(nc.sbuf_tensor("x1b", [128, 8 * NQ], BF16))

    with ExitStack() as esp:
        sbp = lambda n, s, d=F32: esp.enter_context(nc.sbuf_tensor(n, s, d))
        psp = lambda n, s, d=F32: esp.enter_context(nc.psum_tensor(n, s, d))
        wpb = sbp("wpb", [128, 8 * 1024], BF16)
        attnb = sbp("attnb", [128, 2 * 8 * 512], BF16)
        xb2 = sbp("xb2", [128, 2 * D])
        xoT = sbp("xoT", [128, 8 * 512])
        x1raw = sbp("x1raw", [128, 8 * 512])
        x1rb = sbp("x1rb", [128, 2 * 512], BF16)
        sq4 = sbp("sq4", [128, 2 * 512], BF16)
        m1k = sbp("m1k", [128, 512])
        msq4 = sbp("msq4", [128, 512])
        var4 = sbp("var4", [128, 512])
        lv4 = sbp("lv4", [128, 512])
        rstd4 = sbp("rstd4", [128, 512])
        uu4 = sbp("uu4", [128, 512])
        x1f = sbp("x1f", [128, 8 * 512])
        mark4 = sbp("mark4", [1, 1])
        attnd = sbp("attnd", [128, 2 * 8 * 512], BF16)
        pTx = [psp("pTx0", [128, 512]), psp("pTx1", [128, 512])]
        pp = [psp("pp0", [128, 512]), psp("pp1", [128, 512])]
        pls = [psp("pls0", [128, 512]), psp("pls1", [128, 512])]
        pzb = [psp("pzb0", [128, 512]), psp("pzb1", [128, 512])]

        with (
            nc.semaphore("d4a") as d4a,
            nc.semaphore("d4g") as d4g,
            nc.semaphore("t4t") as t4t,
            nc.semaphore("e4") as e4,
            nc.semaphore("p4m") as p4m,
            nc.semaphore("r4") as r4,
            nc.semaphore("c4") as c4,
            nc.semaphore("s4m") as s4m,
            nc.semaphore("ch4") as ch4,
            nc.semaphore("sl4") as sl4,
            nc.semaphore("ap4") as ap4,
            nc.semaphore("st4") as st4,
            nc.semaphore("vs4") as vs4s,
            nc.semaphore("ss4") as ss4s,
            nc.semaphore("dv4") as dv4,
            nc.semaphore("zb4") as zb4,
            nc.semaphore("c4v") as c4v,
            nc.semaphore("sx1") as sx1,
            nc.Block() as blk,
        ):
            attn_v = attn_scr[:, :].rearrange("(f p) n -> p f n", p=128)
            x1_v = x1_scr[:, :].rearrange("(f p) n -> p f n", p=128)

            @blk.sync
            def _(y):
                for qc in range(4):
                    if qc >= 2:
                        y.wait_ge(dv4, 8 * (qc - 1))
                    y.dma_start(attnb[:, (qc % 2) * 4096:(qc % 2) * 4096 + 4096]
                                .rearrange("p (f n) -> p f n", f=8),
                                attn_v[:, :, 512 * qc:512 * (qc + 1)]).then_inc(d4a, 16)
                    y.wait_ge(ap4, 8 * (qc + 1))
                    y.dma_start(x1_v[:, :, 512 * qc:512 * (qc + 1)],
                                x1f[:, :].rearrange("p (f n) -> p f n", f=8)).then_inc(st4, 16)
                y.wait_ge(st4, 64)

            @blk.gpsimd
            def _(g):
                g.dma_start(wpb[:, :].rearrange("p (c n) -> p c n", c=8),
                            w_proj[:, :, :]).then_inc(d4g, 16)
                for ch in range(16):
                    if ch >= 2:
                        g.wait_ge(t4t, 8 * (ch - 1))
                    g.dma_start(xb2[:, (ch % 2) * D:(ch % 2 + 1) * D],
                                x_seq[128 * ch:128 * (ch + 1), :]).then_inc(d4g, 16)

            @blk.tensor
            def _(t_):
                for qc in range(4):
                    for ch in range(4 * qc, 4 * qc + 4):
                        t_.wait_ge(d4g, max(48, 16 * (ch + 2)))
                        for half in range(2):
                            bi = 2 * ch + half
                            if bi >= 2:
                                t_.wait_ge(e4, 4 * (bi - 1))
                            pt = pTx[bi % 2]
                            ins = None
                            for j in range(4 * half, 4 * half + 4):
                                ins = t_.transpose(pt[:, 128 * (j % 4):128 * (j % 4) + 128],
                                                   xb2[:, (ch % 2) * D + 128 * j:(ch % 2) * D + 128 * j + 128],
                                                   c_idf[:, :])
                            ins.then_inc(t4t, 4)
                    t_.wait_ge(d4a, 16 * (qc + 1))
                    for hp in range(8):
                        zi = qc * 8 + hp
                        if zi >= 2:
                            t_.wait_ge(dv4, zi - 1)
                        t_.matmul(pzb[zi % 2][:, :], c_zsel[:, 128 * hp:128 * hp + 128],
                                  zrball[:, 512 * qc:512 * (qc + 1)],
                                  start=True, stop=True).then_inc(zb4, 1)
                    t_.wait_ge(e4, 4 * (8 * qc + 8))
                    t_.wait_ge(dv4, 8 * (qc + 1))

                    def stat_mm(d_):
                        gi_ = qc * 8 + d_
                        sl2 = slice((gi_ % 2) * 512, (gi_ % 2) * 512 + 512)
                        if d_ == 0 and qc >= 1:
                            t_.wait_ge(ch4, qc)
                        t_.wait_ge(c4, gi_ + 1)
                        t_.wait_ge(c4v, gi_ + 1)
                        t_.matmul(pls[0][:, :], c_aones[:, :], x1rb[:, sl2],
                                  start=(d_ == 0), stop=(d_ == 7))
                        t_.matmul(pls[1][:, :], c_aones[:, :], sq4[:, sl2],
                                  start=(d_ == 0), stop=(d_ == 7)).then_inc(s4m, 1)

                    for dout in range(8):
                        gi = qc * 8 + dout
                        if gi >= 2:
                            t_.wait_ge(r4, gi - 1)
                        ins = None
                        for c in range(8):
                            ins = t_.matmul(pp[gi % 2][:, :],
                                            wpb[:, c * 1024 + 128 * dout:c * 1024 + 128 * dout + 128],
                                            attnd[:, (qc % 2) * 4096 + 512 * c:(qc % 2) * 4096 + 512 * c + 512],
                                            start=(c == 0), stop=(c == 7))
                        ins.then_inc(p4m, 1)
                        if dout >= 2:
                            stat_mm(dout - 2)
                    stat_mm(6)
                    stat_mm(7)

            @blk.vector
            def _(v):
                vs = SelfSem(v, vs4s)
                v.memset(mark4[:, :], 0.0)
                for qc in range(4):
                    for ch in range(4 * qc, 4 * qc + 4):
                        for half in range(2):
                            bi = 2 * ch + half
                            v.wait_ge(t4t, 4 * (bi + 1))
                            pt = pTx[bi % 2]
                            ins = None
                            for j in range(4 * half, 4 * half + 4):
                                ins = v.tensor_copy(
                                    xoT[:, j * 512 + 128 * (ch % 4):j * 512 + 128 * (ch % 4) + 128],
                                    pt[:, 128 * (j % 4):128 * (j % 4) + 128])
                            ins.then_inc(e4, 4)
                    if qc >= 2:
                        v.wait_ge(p4m, 8 * (qc - 1))
                    for hp in range(8):
                        zi = qc * 8 + hp
                        v.wait_ge(zb4, zi + 1)
                        v.tensor_tensor(attnd[:, (qc % 2) * 4096 + 512 * hp:(qc % 2) * 4096 + 512 * hp + 512],
                                        attnb[:, (qc % 2) * 4096 + 512 * hp:(qc % 2) * 4096 + 512 * hp + 512],
                                        pzb[zi % 2][:, :], ALU.mult).then_inc(dv4, 1)
                    for dout in range(8):
                        gi = qc * 8 + dout
                        v.wait_ge(p4m, gi + 1)
                        if gi >= 2:
                            v.wait_ge(s4m, gi - 1)
                        sl = slice(512 * dout, 512 * (dout + 1))
                        sl2 = slice((gi % 2) * 512, (gi % 2) * 512 + 512)
                        vs.step(v.tensor_tensor(x1raw[:, sl], pp[gi % 2][:, :],
                                                xoT[:, sl], ALU.add))
                        vs.sync()
                        v.tensor_copy(mark4[:, :], mark4[:, :]).then_inc(r4, 1)
                        v.tensor_tensor(sq4[:, sl2], x1raw[:, sl], x1raw[:, sl],
                                        ALU.mult).then_inc(c4v, 1)
                    v.wait_ge(s4m, 8 * (qc + 1))
                    vs.step(v.tensor_scalar(m1k[:, :], pls[0][:, :], 1.0 / D, None, ALU.mult))
                    vs.sync()
                    vs.step(v.tensor_tensor(msq4[:, :], m1k[:, :], m1k[:, :], ALU.mult))
                    vs.sync()
                    vs.step(v.scalar_tensor_tensor(var4[:, :], pls[1][:, :], 1.0 / D,
                                                   msq4[:, :], ALU.mult, ALU.subtract))
                    vs.sync()
                    v.tensor_copy(mark4[:, :], mark4[:, :]).then_inc(ch4, 1)
                    v.wait_ge(sl4, qc + 1)
                    if qc >= 1:
                        v.wait_ge(st4, 16 * qc)
                        v.wait_ge(sx1, 8 * qc)
                    for dout in range(8):
                        sl = slice(512 * dout, 512 * (dout + 1))
                        vs.step(v.tensor_tensor(uu4[:, :], x1raw[:, sl], m1k[:, :], ALU.subtract))
                        vs.sync()
                        vs.step(v.tensor_tensor(x1f[:, sl], uu4[:, :], rstd4[:, :], ALU.mult))
                        vs.sync()
                        v.tensor_copy(mark4[:, :], mark4[:, :]).then_inc(ap4, 1)

            @blk.scalar
            def _(s):
                ss = SelfSem(s, ss4s)
                for qc in range(4):
                    for dout in range(8):
                        gi = qc * 8 + dout
                        s.wait_ge(r4, gi + 1)
                        if gi >= 2:
                            s.wait_ge(s4m, gi - 1)
                        sl = slice(512 * dout, 512 * (dout + 1))
                        sl2 = slice((gi % 2) * 512, (gi % 2) * 512 + 512)
                        s.copy(x1rb[:, sl2], x1raw[:, sl]).then_inc(c4, 1)
                    s.wait_ge(ch4, qc + 1)
                    ss.step(s.activation(lv4[:, :], var4[:, :], AF.Ln,
                                         bias=c_eps[:, 0:1], scale=1.0))
                    ss.sync()
                    s.activation(rstd4[:, :], lv4[:, :], AF.Exp, bias=0.0,
                                 scale=-0.5).then_inc(sl4, 1)
                    for dout in range(8):
                        s.wait_ge(ap4, qc * 8 + dout + 1)
                        sl = slice(512 * dout, 512 * (dout + 1))
                        s.copy(x1b[:, dout * NQ + 512 * qc:dout * NQ + 512 * qc + 512],
                               x1f[:, sl]).then_inc(sx1, 1)

    nc.all_engine_barrier()

    if LASTPH == "p4":
        if DEBUG:
            with nc.semaphore("dx1") as dx1, nc.Block() as blk:
                @blk.sync
                def _(y):
                    y.dma_start(dbg["x1"][:, :], x1_scr[:, :]).then_inc(dx1, 16)
                    y.wait_ge(dx1, 16)
        es2.close(); es0.close()
        return nc

    with ExitStack() as esp:
        sbp = lambda n, s, d=F32: esp.enter_context(nc.sbuf_tensor(n, s, d))
        psp = lambda n, s, d=F32: esp.enter_context(nc.psum_tensor(n, s, d))
        c_rw = sbp("c_rw", [128, 64])
        c_b1 = sbp("c_b1", [128, 32])
        b2b = sbp("b2b", [8, D], BF16)
        x1fq = sbp("x1fq", [128, 8 * 512])
        lgT = sbp("lgT", [8, 512])
        ltok = sbp("ltok", [128, 32])
        mwork = sbp("mwork", [128, 8 * 10 * 4])
        maskt = sbp("maskt", [128, 32])
        maskTb = sbp("maskTb", [8, 512], BF16)
        w1b = sbp("w1b", [128, 2 * 8 * 512], BF16)
        w2b = sbp("w2b", [128, 2 * 4096], BF16)
        h1m = sbp("h1m", [128, 32 * 512], BF16)
        x2raw = sbp("x2raw", [128, 8 * 512])
        x2rb = sbp("x2rb", [128, 2 * 512], BF16)
        sq5 = sbp("sq5", [128, 2 * 512], BF16)
        m1k5 = sbp("m1k5", [128, 512])
        msq5 = sbp("msq5", [128, 512])
        var5 = sbp("var5", [128, 512])
        lv5 = sbp("lv5", [128, 512])
        rstd5 = sbp("rstd5", [128, 512])
        uu5 = sbp("uu5", [128, 512])
        outf = sbp("outf", [128, 8 * 512])
        outtok = sbp("outtok", [128, 4 * 1024])
        mark5 = sbp("mark5", [1, 1])
        pstat = [psp("pstat0", [128, 512]), psp("pstat1", [128, 512])]
        pmb = [psp("pmb0", [128, 512]), psp("pmb1", [128, 512])]
        pf1 = [psp("pf10", [128, 512]), psp("pf11", [128, 512])]
        pout = [psp("pout0", [128, 512]), psp("pout1", [128, 512])]

        sems = ExitStack()
        _sm = lambda n: sems.enter_context(nc.semaphore(n))
        d5c = _sm("d5c"); d5x = _sm("d5x"); d5w = _sm("d5w")
        rt5 = _sm("rt5"); rv5 = _sm("rv5"); rs5 = _sm("rs5")
        f1m = _sm("f1m"); g5 = _sm("g5"); h5 = _sm("h5")
        o5m = _sm("o5m"); r5 = _sm("r5"); c5 = _sm("c5")
        s5m = _sm("s5m"); ch5 = _sm("ch5"); sl5 = _sm("sl5")
        ap5 = _sm("ap5"); tr5 = _sm("tr5"); ev5 = _sm("ev5"); mb5m = _sm("mb5m")
        st5 = _sm("st5"); vs5s = _sm("vs5"); ss5s = _sm("ss5")
        c5v = _sm("c5v")
        with nc.Block() as blk:
            x1_v = x1_scr[:, :].rearrange("(f p) n -> p f n", p=128)

            @blk.sync
            def _(y):
                y.dma_start(c_rw[:, :].rearrange("p (c e) -> p c e", c=8),
                            rw[:, :, :]).then_inc(d5c, 16)
                y.dma_start(c_b1[:, :], b1c[:, :]).then_inc(d5c, 16)
                for qc in range(4):
                    if qc >= 1:
                        y.wait_ge(r5, 8 * qc)
                    y.dma_start(x1fq[:, :].rearrange("p (f n) -> p f n", f=8),
                                x1_v[:, :, 512 * qc:512 * (qc + 1)]).then_inc(d5x, 16)
                    for tc in range(4):
                        y.wait_ge(ev5, 32 * qc + 8 * (tc + 1))
                        y.dma_start(out[512 * qc + 128 * tc:512 * qc + 128 * tc + 128, :],
                                    outtok[:, 1024 * tc:1024 * (tc + 1)]).then_inc(st5, 16)
                y.wait_ge(st5, 16 * 16)

            @blk.gpsimd
            def _(g):
                g.dma_start(b2b[:, :], b2[:, :]).then_inc(d5w, 16)
                for qc in range(4):
                    for e in range(NE):
                        gi = qc * 8 + e
                        if gi >= 2:
                            g.wait_ge(f1m, 4 * (gi - 1))
                        g.dma_start(w1b[:, (gi % 2) * 4096:(gi % 2) * 4096 + 4096],
                                    w1s[e, :, :]).then_inc(d5w, 16)
                    for dt in range(8):
                        gj = qc * 8 + dt
                        if gj >= 2:
                            g.wait_ge(o5m, gj - 1)
                        g.dma_start(w2b[:, (gj % 2) * 4096:(gj % 2) * 4096 + 4096],
                                    w2s[dt, :, :]).then_inc(d5w, 16)

            @blk.tensor
            def _(t_):
                t_.wait_ge(d5c, 32)
                for qc in range(4):
                    t_.wait_ge(d5x, 16 * (qc + 1))
                    if qc >= 1:
                        t_.wait_ge(s5m, 8 * qc)
                        t_.wait_ge(ch5, qc)
                    ins = None
                    for c in range(8):
                        ins = t_.matmul(pstat[0][0:8, :], c_rw[:, 8 * c:8 * c + 8],
                                        x1fq[:, 512 * c:512 * (c + 1)],
                                        start=(c == 0), stop=(c == 7))
                    ins.then_inc(rt5, 1)
                    t_.wait_ge(rv5, 5 * qc + 1)
                    for tc in range(4):
                        t_.transpose(pstat[1][:, 32 * tc:32 * tc + 8],
                                     lgT[0:8, 128 * tc:128 * tc + 128],
                                     c_idf[0:8, 0:8]).then_inc(rt5, 1)
                    if qc >= 1:
                        t_.wait_ge(ev5, 32 * qc)
                    for e in range(NE):
                        gi = qc * 8 + e
                        t_.wait_ge(d5w, 16 * (2 + 16 * qc + e))
                        for j in range(4):
                            ji = gi * 4 + j
                            if ji >= 2:
                                t_.wait_ge(g5, ji - 1)
                            if e == 4 and j == 0:
                                t_.wait_ge(rv5, 5 * qc + 4)
                                t_.wait_ge(h5, 32 * qc)
                                for tc in range(4):
                                    t_.transpose(pmb[0][0:8, 128 * tc:128 * tc + 128],
                                                 maskt[:, 8 * tc:8 * tc + 8],
                                                 c_idf[:, :]).then_inc(rt5, 1)
                            ins = None
                            for c in range(8):
                                ins = t_.matmul(
                                    pf1[ji % 2][:, :],
                                    w1b[:, (gi % 2) * 4096 + 512 * c + 128 * j:
                                        (gi % 2) * 4096 + 512 * c + 128 * j + 128],
                                    x1b[:, c * NQ + 512 * qc:c * NQ + 512 * qc + 512],
                                    start=(c == 0), stop=(c == 7))
                            ins.then_inc(f1m, 1)
                    t_.wait_ge(rv5, 5 * qc + 5)
                    for e in range(NE):
                        gi = qc * 8 + e
                        t_.wait_ge(h5, 4 * max(0, gi - 1))
                        t_.matmul(pmb[gi % 2][:, :],
                                  c_emats[:, 128 * e:128 * e + 128],
                                  maskTb[:, :], start=True, stop=True).then_inc(mb5m, 1)
                    def stat_mm5(d_):
                        gi_ = qc * 8 + d_
                        sl2 = slice((gi_ % 2) * 512, (gi_ % 2) * 512 + 512)
                        if d_ == 0 and qc >= 1:
                            t_.wait_ge(ch5, qc)
                        t_.wait_ge(c5, gi_ + 1)
                        t_.wait_ge(c5v, gi_ + 1)
                        t_.matmul(pstat[0][:, :], c_aones[:, :], x2rb[:, sl2],
                                  start=(d_ == 0), stop=(d_ == 7))
                        t_.matmul(pstat[1][:, :], c_aones[:, :], sq5[:, sl2],
                                  start=(d_ == 0), stop=(d_ == 7)).then_inc(s5m, 1)

                    for dt in range(8):
                        gj = qc * 8 + dt
                        t_.wait_ge(d5w, 16 * (10 + 16 * qc + dt))
                        if gj >= 2:
                            t_.wait_ge(r5, gj - 1)
                        for e in range(NE):
                            t_.wait_ge(h5, qc * 32 + 4 * (e + 1))
                            for c in range(4):
                                t_.matmul(
                                    pout[gj % 2][:, :],
                                    w2b[:, (gj % 2) * 4096 + (e * 4 + c) * 128:
                                        (gj % 2) * 4096 + (e * 4 + c) * 128 + 128],
                                    h1m[:, (e * 4 + c) * 512:(e * 4 + c) * 512 + 512],
                                    start=(e == 0 and c == 0), stop=False)
                        t_.matmul(pout[gj % 2][:, :], b2b[:, 128 * dt:128 * dt + 128],
                                  maskTb[:, :], start=False, stop=True).then_inc(o5m, 1)
                        if dt >= 2:
                            stat_mm5(dt - 2)
                    stat_mm5(6)
                    stat_mm5(7)
                    t_.wait_ge(ap5, 8 * (qc + 1))
                    for tc in range(4):
                        ti = qc * 4 + tc
                        if ti >= 2:
                            t_.wait_ge(ev5, 8 * (ti - 1))
                        ins = None
                        for dtj in range(8):
                            ins = t_.transpose(pf1[tc % 2][:, 128 * (dtj % 4):128 * (dtj % 4) + 128]
                                               , outf[:, dtj * 512 + 128 * tc:dtj * 512 + 128 * tc + 128],
                                               c_idf[:, :])
                            if dtj == 3:
                                ins.then_inc(tr5, 4)
                                t_.wait_ge(ev5, 8 * ti + 4)
                        ins.then_inc(tr5, 4)

            @blk.vector
            def _(v):
                vs = SelfSem(v, vs5s)
                v.memset(mark5[:, :], 0.0)
                for qc in range(4):
                    v.wait_ge(rt5, 9 * qc + 1)
                    vs.step(v.tensor_copy(lgT[0:8, :], pstat[0][0:8, :]))
                    vs.sync()
                    v.tensor_copy(mark5[:, :], mark5[:, :]).then_inc(rv5, 1)
                    v.wait_ge(rt5, 9 * qc + 5)
                    for tc in range(4):
                        vs.step(v.tensor_tensor(ltok[:, 8 * tc:8 * tc + 8],
                                                pstat[1][:, 32 * tc:32 * tc + 8],
                                                c_rbias[:, :], ALU.add))
                    vs.sync()
                    v.tensor_copy(mark5[:, :], mark5[:, :]).then_inc(rv5, 1)
                    for tc in range(4):
                        lsl = ltok[:, 8 * tc:8 * tc + 8]
                        w0 = mwork[:, 80 * tc:80 * tc + 80]
                        m1 = w0[:, 0:1]; ge1 = w0[:, 8:16]; lm = w0[:, 16:24]
                        m2 = w0[:, 1:2]; ind = w0[:, 24:32]; nm1 = w0[:, 4:5]
                        vs.step(v.reduce_max(m1, lsl, axis=AX.X)); vs.sync()
                        vs.step(v.tensor_scalar(ge1, lsl, m1, None, ALU.is_ge)); vs.sync()
                        vs.step(v.scalar_tensor_tensor(lm, ge1, -1e30, lsl, ALU.mult, ALU.add)); vs.sync()
                        vs.step(v.reduce_max(m2, lm, axis=AX.X)); vs.sync()
                        vs.step(v.tensor_scalar(ind, lsl, m2, None, ALU.is_ge)); vs.sync()
                        vs.step(v.tensor_scalar(nm1, m1, -1.0, None, ALU.mult)); vs.sync()
                    v.tensor_copy(mark5[:, :], mark5[:, :]).then_inc(rv5, 1)
                    v.wait_ge(rs5, 4 * (qc + 1))
                    for tc in range(4):
                        w0 = mwork[:, 80 * tc:80 * tc + 80]
                        ind = w0[:, 24:32]; ex = w0[:, 32:40]
                        wv = w0[:, 40:48]; ssm = w0[:, 2:3]; rs = w0[:, 3:4]
                        vs.step(v.tensor_tensor(wv, ex, ind, ALU.mult)); vs.sync()
                        vs.step(v.reduce_sum(ssm, wv, axis=AX.X)); vs.sync()
                        vs.step(v.reciprocal(rs, ssm)); vs.sync()
                        vs.step(v.tensor_scalar(maskt[:, 8 * tc:8 * tc + 8], wv, rs, None,
                                                ALU.mult)); vs.sync()
                    v.tensor_copy(mark5[:, :], mark5[:, :]).then_inc(rv5, 1)
                    v.wait_ge(rt5, 9 * qc + 9)
                    for tc in range(4):
                        vs.step(v.tensor_copy(maskTb[0:8, 128 * tc:128 * tc + 128],
                                              pmb[0][0:8, 128 * tc:128 * tc + 128]))
                    vs.sync()
                    v.tensor_copy(mark5[:, :], mark5[:, :]).then_inc(rv5, 1)
                    for e in range(NE):
                        gi = qc * 8 + e
                        v.wait_ge(mb5m, qc * 8 + e + 1)
                        for j in range(4):
                            ji = gi * 4 + j
                            v.wait_ge(g5, ji + 1)
                            v.tensor_tensor(h1m[:, (e * 4 + j) * 512:(e * 4 + j) * 512 + 512],
                                            h1m[:, (e * 4 + j) * 512:(e * 4 + j) * 512 + 512],
                                            pmb[gi % 2][:, :], ALU.mult).then_inc(h5, 1)
                    for dt in range(8):
                        gj = qc * 8 + dt
                        v.wait_ge(o5m, gj + 1)
                        if gj >= 2:
                            v.wait_ge(s5m, gj - 1)
                        sl = slice(512 * dt, 512 * (dt + 1))
                        sl2 = slice((gj % 2) * 512, (gj % 2) * 512 + 512)
                        vs.step(v.tensor_tensor(x2raw[:, sl], pout[gj % 2][:, :],
                                                x1fq[:, sl], ALU.add))
                        vs.sync()
                        v.tensor_copy(mark5[:, :], mark5[:, :]).then_inc(r5, 1)
                        v.tensor_tensor(sq5[:, sl2], x2raw[:, sl], x2raw[:, sl],
                                        ALU.mult).then_inc(c5v, 1)
                    v.wait_ge(s5m, 8 * (qc + 1))
                    vs.step(v.tensor_scalar(m1k5[:, :], pstat[0][:, :], 1.0 / D, None, ALU.mult))
                    vs.sync()
                    vs.step(v.tensor_tensor(msq5[:, :], m1k5[:, :], m1k5[:, :], ALU.mult))
                    vs.sync()
                    vs.step(v.scalar_tensor_tensor(var5[:, :], pstat[1][:, :], 1.0 / D,
                                                   msq5[:, :], ALU.mult, ALU.subtract))
                    vs.sync()
                    v.tensor_copy(mark5[:, :], mark5[:, :]).then_inc(ch5, 1)
                    v.wait_ge(sl5, qc + 1)
                    if qc >= 1:
                        v.wait_ge(tr5, 32 * qc)
                    for dt in range(8):
                        sl = slice(512 * dt, 512 * (dt + 1))
                        vs.step(v.tensor_tensor(uu5[:, :], x2raw[:, sl], m1k5[:, :], ALU.subtract))
                        vs.sync()
                        vs.step(v.tensor_tensor(outf[:, sl], uu5[:, :], rstd5[:, :], ALU.mult))
                        vs.sync()
                        v.tensor_copy(mark5[:, :], mark5[:, :]).then_inc(ap5, 1)
                    for tc in range(4):
                        ti = qc * 4 + tc
                        v.wait_ge(tr5, 32 * qc + 8 * tc + 4)
                        if ti >= 1:
                            v.wait_ge(st5, 16 * ti - 16)
                        ins = None
                        for dtj in range(4):
                            ins = v.tensor_copy(outtok[:, 1024 * tc + 128 * dtj:1024 * tc + 128 * dtj + 128],
                                                pf1[tc % 2][:, 128 * dtj:128 * dtj + 128])
                        ins.then_inc(ev5, 4)
                        v.wait_ge(tr5, 32 * qc + 8 * tc + 8)
                        ins = None
                        for dtj in range(4, 8):
                            ins = v.tensor_copy(outtok[:, 1024 * tc + 128 * dtj:1024 * tc + 128 * dtj + 128],
                                                pf1[tc % 2][:, 128 * (dtj % 4):128 * (dtj % 4) + 128])
                        ins.then_inc(ev5, 4)

            @blk.scalar
            def _(s):
                ss = SelfSem(s, ss5s)
                for qc in range(4):
                    s.wait_ge(rv5, 5 * qc + 3)
                    for tc in range(4):
                        w0 = mwork[:, 80 * tc:80 * tc + 80]
                        nm1 = w0[:, 4:5]; ex = w0[:, 32:40]
                        s.activation(ex, ltok[:, 8 * tc:8 * tc + 8], AF.Exp,
                                     bias=nm1, scale=1.0).then_inc(rs5, 1)
                    for e in range(NE):
                        gi = qc * 8 + e
                        for j in range(4):
                            ji = gi * 4 + j
                            s.wait_ge(f1m, ji + 1)
                            s.activation(h1m[:, (e * 4 + j) * 512:(e * 4 + j) * 512 + 512],
                                         pf1[ji % 2][:, :], AF.Gelu,
                                         bias=c_b1[:, 4 * e + j:4 * e + j + 1],
                                         scale=1.0).then_inc(g5, 1)
                    for dt in range(8):
                        gj = qc * 8 + dt
                        s.wait_ge(r5, gj + 1)
                        if gj >= 2:
                            s.wait_ge(s5m, gj - 1)
                        sl = slice(512 * dt, 512 * (dt + 1))
                        sl2 = slice((gj % 2) * 512, (gj % 2) * 512 + 512)
                        s.copy(x2rb[:, sl2], x2raw[:, sl]).then_inc(c5, 1)
                    s.wait_ge(ch5, qc + 1)
                    ss.step(s.activation(lv5[:, :], var5[:, :], AF.Ln,
                                         bias=c_eps[:, 0:1], scale=1.0))
                    ss.sync()
                    s.activation(rstd5[:, :], lv5[:, :], AF.Exp, bias=0.0,
                                 scale=-0.5).then_inc(sl5, 1)

        sems.close()

    nc.all_engine_barrier()
    es2.close()
    es0.close()
    return nc



def _prep_shared(inputs):
    w_qkv = np.ascontiguousarray(
        inputs["L0_attn_w_qkv"].reshape(8, 128, 3 * D).transpose(1, 0, 2))
    w_proj = np.ascontiguousarray(
        inputs["L0_attn_w_proj"].reshape(8, 128, D).transpose(1, 0, 2))
    w1r = np.ascontiguousarray(
        inputs["L0_ffn_W1"].reshape(NE, 8, 128, DF).transpose(0, 2, 1, 3))
    w2r = np.ascontiguousarray(
        inputs["L0_ffn_W2"].reshape(NE, 4, 128, 8, 128).transpose(3, 2, 0, 1, 4))
    b1c = np.ascontiguousarray(
        inputs["L0_ffn_b1"].reshape(NE, 4, 128).transpose(2, 0, 1).reshape(128, NE * 4))
    rw = np.ascontiguousarray(
        inputs["L0_ffn_router_w"].reshape(8, 128, NE).transpose(1, 0, 2))
    rbias = np.ascontiguousarray(
        np.broadcast_to(inputs["L0_ffn_router_b"][None, :], (128, NE)))
    qg2 = np.ascontiguousarray(np.tile(inputs["L0_attn_q_gamma"], 2))[:, None]
    qb2 = np.ascontiguousarray(np.tile(inputs["L0_attn_q_beta"], 2))[:, None]
    kg2 = np.ascontiguousarray(np.tile(inputs["L0_attn_k_gamma"], 2))[:, None]
    kb2 = np.ascontiguousarray(np.tile(inputs["L0_attn_k_beta"], 2))[:, None]
    emats = np.zeros((8, NE * 128), np.float32)
    for e in range(NE):
        emats[e, e * 128:(e + 1) * 128] = 1.0
    zsel = np.zeros((16, 8 * 128), np.float32)
    for hp in range(8):
        zsel[2 * hp, 128 * hp:128 * hp + 64] = 1.0
        zsel[2 * hp + 1, 128 * hp + 64:128 * hp + 128] = 1.0
    shared = {
        "w_qkv": w_qkv, "w_proj": w_proj, "w1r": w1r, "w2r": w2r,
        "b1c": b1c, "b2": np.ascontiguousarray(inputs["L0_ffn_b2"]),
        "rw": rw, "rbias": rbias,
        "qg2": qg2, "qb2": qb2, "kg2": kg2, "kb2": kb2,
        "idf": np.eye(128, dtype=np.float32),
        "idb": np.eye(128, dtype=np.float32).astype(BF),
        "hones": np.kron(np.eye(2, dtype=np.float32), np.ones((64, 64), np.float32)).astype(BF),
        "aones": np.ones((128, 128), np.float32).astype(BF),
        "onesrow": np.ones((1, 128), np.float32).astype(BF),
        "onescolf": np.ones((1, 128), np.float32),
        "emats": emats.astype(BF),
        "zsel": zsel.astype(BF),
        "epscol": np.full((128, 1), EPS, np.float32),
    }
    return shared


def make_in_maps(inputs):
    shared = _prep_shared(inputs)
    x = inputs["L0_attn_x"]
    in_maps = []
    for core in range(8):
        b, hf = core // 2, core % 2
        own = x[b, hf * NQ:(hf + 1) * NQ]
        other = x[b, (1 - hf) * NQ:(2 - hf) * NQ]
        x_roll = np.ascontiguousarray(np.concatenate([own, other], axis=0))
        m = dict(shared)
        m["x_seq"] = x_roll
        in_maps.append(m)
    return in_maps


def kernel(**inputs):
    in_maps = make_in_maps(inputs)
    nc = build_nc()
    res = run_bass_kernel_spmd(nc, in_maps, core_ids=list(range(8)))
    outs = [np.asarray(r["out"]) for r in res.results]
    full = np.empty((4, NS, D), np.float32)
    for core in range(8):
        b, hf = core // 2, core % 2
        full[b, hf * NQ:(hf + 1) * NQ] = outs[core]
    return full
